# version 10
# baseline (speedup 1.0000x reference)
import math
from contextlib import ExitStack
import numpy as np
import ml_dtypes
import concourse.bass as bass
import concourse.mybir as mybir
from concourse.bass_utils import run_bass_kernel_spmd
from concourse.aot_env import allow_ici_all_to_all

F32 = mybir.dt.float32
F16 = mybir.dt.float16
BF16 = mybir.dt.bfloat16
I32 = mybir.dt.int32
AF = mybir.ActivationFunctionType
ALU = mybir.AluOpType
AX = mybir.AxisListType

B, S, DM = 2, 4096, 2048
NT = B * S
TOK = NT // 8
NTILE = TOK // 128
QKVO = 4096
D = 128
E = 256
I_FF = 8192
LAMBDA_INIT = 0.8 - 0.6 * math.exp(-0.3)
SCALE = D ** -0.5
EPS = 1e-5
NEG = float(np.finfo(np.float32).min)
MAGIC = 12582912.0
TGF = 256
NGF = TOK // TGF


class Build:

    def __init__(self, nc):
        self.nc = nc
        self.sem = {}
        self.cnt = {}
        self.waited = {}
        for s in ["V", "SC", "T", "CC"]:
            self.sem[s] = nc.alloc_semaphore("sem_" + s)
            self.cnt[s] = 0
        for e in ["sync", "vector", "scalar", "tensor", "gpsimd"]:
            self.waited[e] = {}

    def dsem(self, name):
        if name not in self.sem:
            self.sem[name] = self.nc.alloc_semaphore("sem_" + name)
            self.cnt[name] = 0
        return name

    def tok(self, s):
        return (s, self.cnt[s])

    def _eng(self, name):
        return getattr(self.nc, name)

    def wait(self, engname, *tokens):
        w = self.waited[engname]
        for t in tokens:
            if t is None:
                continue
            s, v = t
            if v <= 0 or w.get(s, 0) >= v:
                continue
            self._eng(engname).wait_ge(self.sem[s], v)
            w[s] = v

    def bump(self, s, instr, k=1):
        instr.then_inc(self.sem[s], k)
        self.cnt[s] += k
        return (s, self.cnt[s])

    def dma(self, q, out, in_, sem=None, wait=()):
        eng = {"dS": "sync", "dA": "scalar", "dG": "gpsimd"}[q]
        s = self.dsem(sem if sem is not None else "d_" + q)
        self.wait(eng, *wait)
        return self.bump(s, self._eng(eng).dma_start(out=out, in_=in_), 16)

    def v(self, fn, *a, wait=(), **kw):
        self.wait("vector", *wait)
        return self.bump("V", getattr(self.nc.vector, fn)(*a, **kw))

    def s(self, fn, *a, wait=(), **kw):
        self.wait("scalar", *wait)
        return self.bump("SC", getattr(self.nc.scalar, fn)(*a, **kw))

    def t(self, fn, *a, wait=(), **kw):
        self.wait("tensor", *wait)
        return self.bump("T", getattr(self.nc.tensor, fn)(*a, **kw))

    def barrier(self, dma_sems=()):
        toks = [self.tok("V"), self.tok("SC"), self.tok("T")]
        toks += [(s, self.cnt[s]) for s in dma_sems if s in self.cnt]
        for e in ["sync", "vector", "scalar", "tensor"]:
            self.wait(e, *toks)

    def cc(self, kind, op, rg, ins, outs, wait=()):
        self.wait("gpsimd", *wait)
        return self.bump("CC", self.nc.gpsimd.collective_compute(
            kind, op, replica_groups=rg, ins=ins, outs=outs))


P0CUT = 99


def build(stage=9, dbg=(), dbg_qbmax=0):
    nc = bass.Bass("TRN2", target_bir_lowering=False, debug=False, num_devices=8)
    K = Build(nc)
    RG = [list(range(8))]

    x_in = nc.dram_tensor("x", [TOK, DM], F32, kind="ExternalInput").ap()
    wqkvT = nc.dram_tensor("wqkvT", [8, 128, 8192], F32, kind="ExternalInput").ap()
    qkv_b = nc.dram_tensor("qkv_b", [128, 32], F32, kind="ExternalInput").ap()
    owT = nc.dram_tensor("owT", [4, 128, 8192], F32, kind="ExternalInput").ap()
    out_b = nc.dram_tensor("out_b", [128, 16], F32, kind="ExternalInput").ap()
    lamv = nc.dram_tensor("lamv", [1, 512], F32, kind="ExternalInput").ap()
    sublnw = nc.dram_tensor("sublnw", [1, 256], F32, kind="ExternalInput").ap()
    wgs = nc.dram_tensor("wgs", [DM, 1024], F32, kind="ExternalInput").ap()
    wus = nc.dram_tensor("wus", [DM, 1024], F32, kind="ExternalInput").ap()
    wds = nc.dram_tensor("wds", [I_FF, 256], F32, kind="ExternalInput").ap()
    cmat = nc.dram_tensor("cmat", [386, 128], F32, kind="ExternalInput").ap()
    masks = nc.dram_tensor("masks", [512, 512], F32, kind="ExternalInput").ap()

    y_out = nc.dram_tensor("y", [TOK, DM], F32, kind="ExternalOutput").ap()
    dbg_out = {}
    for name, shape in dbg:
        dbg_out[name] = nc.dram_tensor(name, list(shape), F32, kind="ExternalOutput").ap()

    cin_mean = nc.dram_tensor("cin_mean", [1, 8], F32).ap()
    cout_mean = nc.dram_tensor("cout_mean", [1, 8], F32).ap()
    cin_wq = nc.dram_tensor("cin_wq", [3, DM * 1024], BF16).ap()
    cout_wq = nc.dram_tensor("cout_wq", [24, DM * 1024], BF16).ap()
    SH16 = 2 * 2 * 128 * TOK
    cin_q16 = nc.dram_tensor("cin_q16", [8, SH16], F16).ap()
    cout_q16 = nc.dram_tensor("cout_q16", [8, SH16], F16).ap()
    cin_k16 = nc.dram_tensor("cin_k16", [8, SH16], F16).ap()
    cout_k16 = nc.dram_tensor("cout_k16", [8, SH16], F16).ap()
    SH32 = 2 * 128 * TOK
    cin_v32 = nc.dram_tensor("cin_v32", [8, SH32], F32).ap()
    cout_v32 = nc.dram_tensor("cout_v32", [8, SH32], F32).ap()
    SHO = 2 * 128 * TOK
    cin_o = nc.dram_tensor("cin_o", [8, SHO], F32).ap()
    cout_o = nc.dram_tensor("cout_o", [8, SHO], F32).ap()
    xq_scr = nc.dram_tensor("xq_scr", [4, 128, 4096], BF16).ap()
    ffn_scr = nc.dram_tensor("ffn_scr", [16, 128, TOK], F32).ap()

    ident = nc.alloc_sbuf_tensor("ident", [128, 128], F32)
    identb = nc.alloc_sbuf_tensor("identb", [128, 128], BF16)
    h128 = nc.alloc_sbuf_tensor("h128", [128, 128], F32)
    bd64 = nc.alloc_sbuf_tensor("bd64", [128, 128], F32)
    p2c = nc.alloc_sbuf_tensor("p2c", [2, 128], F32)
    onerow = nc.alloc_sbuf_tensor("onerow", [1, 128], F32)
    qkvb_sb = nc.alloc_sbuf_tensor("qkvb_sb", [128, 32], F32)
    outb_sb = nc.alloc_sbuf_tensor("outb_sb", [128, 16], F32)
    lam_sb = nc.alloc_sbuf_tensor("lam_sb", [1, 512], F32)
    subln_sb = nc.alloc_sbuf_tensor("subln_sb", [1, 256], F32)
    subln_bc = nc.alloc_sbuf_tensor("subln_bc", [128, 256], F32)
    lam_col = nc.alloc_sbuf_tensor("lam_col", [128, 1], F32)
    sm = nc.alloc_sbuf_tensor("sm", [128, 192], F32)
    sm2 = nc.alloc_sbuf_tensor("sm2", [128, 160], F32)
    rowbuf = nc.alloc_sbuf_tensor("rowbuf", [2, 512], F32)
    dq_t = nc.alloc_sbuf_tensor("dq_t", [128, 8], F32)
    dqg_row = nc.alloc_sbuf_tensor("dqg_row", [1, TOK], F32)
    dqu_row = nc.alloc_sbuf_tensor("dqu_row", [1, TOK], F32)
    dqd_row = nc.alloc_sbuf_tensor("dqd_row", [1, TOK], F32)
    mean_sb = nc.alloc_sbuf_tensor("mean_sb", [1, 8], F32)
    swcol = nc.alloc_sbuf_tensor("swcol", [128, 4], F32)

    sc_mask = ExitStack()
    mask_sb = sc_mask.enter_context(nc.sbuf_tensor("mask_sb", [128, 2048], F32))
    dqrow = sc_mask.enter_context(nc.sbuf_tensor("dqrow", [1, TOK], F32))
    pb = [nc.alloc_psum_tensor(f"pb{i}", [128, 512], F32) for i in range(8)]

    with nc.Block() as block, allow_ici_all_to_all():
        K.dma("dS", ident[:], cmat[0:128, :], sem="cst")
        K.dma("dS", h128[:], cmat[128:256, :], sem="cst")
        K.dma("dS", bd64[:], cmat[256:384, :], sem="cst")
        K.dma("dS", p2c[:], cmat[384:386, :], sem="cst")
        for _m in range(4):
            K.dma("dS", mask_sb[:, _m * 512:(_m + 1) * 512], masks[_m * 128:(_m + 1) * 128, :], sem="cst")
        K.dma("dS", qkvb_sb[:], qkv_b, sem="cst")
        K.dma("dS", outb_sb[:], out_b, sem="cst")
        K.dma("dS", lam_sb[:], lamv, sem="cst")
        K.dma("dS", subln_sb[:], sublnw, sem="cst")
        tok_const = K.tok("cst")

        t_ones = K.v("memset", onerow[:], 1.0)
        K.v("tensor_copy", identb[:], ident[:], wait=(tok_const,))
        tok_idc = K.tok("V")
        K.wait("tensor", tok_const, t_ones)
        t_mm = K.t("matmul", pb[6][:, 0:256], onerow[:], subln_sb[:], start=True, stop=True)
        t_sbc = K.v("tensor_copy", subln_bc[:], pb[6][:, 0:256], wait=(t_mm,))

        t1 = K.v("tensor_mul", rowbuf[0:1, 0:128], lam_sb[0:1, 0:128], lam_sb[0:1, 128:256], wait=(tok_const,))
        t2 = K.v("tensor_mul", rowbuf[0:1, 128:256], lam_sb[0:1, 256:384], lam_sb[0:1, 384:512])
        t3 = K.v("reduce_sum", sm2[0:1, 140:141], rowbuf[0:1, 0:128], axis=AX.X, wait=(t1,))
        t4 = K.v("reduce_sum", sm2[0:1, 141:142], rowbuf[0:1, 128:256], axis=AX.X, wait=(t2,))
        t5 = K.s("activation", sm2[0:1, 142:143], sm2[0:1, 140:141], AF.Exp, wait=(t3,))
        t6 = K.s("activation", sm2[0:1, 143:144], sm2[0:1, 141:142], AF.Exp, wait=(t4,))
        t7 = K.v("tensor_scalar", sm2[0:1, 138:139], sm2[0:1, 143:144], -1.0, LAMBDA_INIT,
                 ALU.mult, ALU.add, wait=(t5, t6))
        t8 = K.v("tensor_add", sm2[0:1, 139:140], sm2[0:1, 138:139], sm2[0:1, 142:143], wait=(t7,))
        K.wait("tensor", t8, t_sbc)
        t9 = K.t("matmul", pb[6][:, 256:257], onerow[:], sm2[0:1, 139:140], start=True, stop=True)
        K.v("tensor_copy", lam_col[:], pb[6][:, 256:257], wait=(t9,))
        tok_lam = K.tok("V")
        tok_pb6_free = K.tok("V")

        sc_xnT = ExitStack()
        sc_p0 = ExitStack()
        xnT = [sc_xnT.enter_context(nc.sbuf_tensor(f"xnT{f}", [128, TOK], F32)) for f in range(16)]
        xqT = [sc_p0.enter_context(nc.sbuf_tensor(f"xqT{f}", [128, TOK], BF16)) for f in range(16)]
        xt = [sc_p0.enter_context(nc.sbuf_tensor(f"xt{j}", [128, DM], F32)) for j in range(2)]
        xc = [sc_p0.enter_context(nc.sbuf_tensor(f"xc{j}", [128, DM], F32)) for j in range(2)]
        xnb = [sc_p0.enter_context(nc.sbuf_tensor(f"xnb{j}", [128, DM], F32)) for j in range(2)]
        sqd = xt
        scrf = xc

        xdma = [None] * NTILE
        tile_vdone = [None] * NTILE
        tile_tdone = [None] * NTILE
        last_xq_ev = [None] * NTILE
        for i in range(2):
            xdma[i] = K.dma("dS", xt[i][:], x_in[i * 128:(i + 1) * 128, :], sem=f"x{i % 2}")
        for i in range(NTILE):
            bs_ = i % 2
            cb = 16 * i
            c = lambda j: sm[:, cb + j:cb + j + 1]
            K.v("reduce_sum", c(0), xt[bs_][:], axis=AX.X, wait=(xdma[i],))
            tm = K.v("tensor_scalar_mul", c(1), c(0), -1.0 / DM, wait=(K.tok("V"),))
            t_xc = K.s("activation", xc[bs_][:], xt[bs_][:], AF.Identity,
                       bias=c(1), scale=1.0, wait=(tm, xdma[i]))
            t_sq = K.s("activation", sqd[bs_][:], xc[bs_][:], AF.Square,
                       accum_out=c(2), wait=(t_xc,))
            tv = K.v("tensor_scalar", c(3), c(2), 1.0 / DM, EPS, ALU.mult, ALU.add, wait=(t_sq,))
            tu = K.s("activation", c(4), c(3), AF.Ln, wait=(tv,))
            tlast = K.s("activation", c(5), c(4), AF.Exp, scale=-0.5, wait=(tu,))
            rcur = 5
            for it in range(1):
                K.v("tensor_mul", c(6), c(rcur), c(rcur), wait=(tlast,))
                K.v("tensor_mul", c(6), c(6), c(3), wait=(K.tok("V"),))
                K.v("tensor_scalar", c(6), c(6), -0.5, 1.5, ALU.mult, ALU.add, wait=(K.tok("V"),))
                tlast = K.v("tensor_mul", c(7), c(rcur), c(6), wait=(K.tok("V"),))
                rcur = 7
            wprev = [tile_tdone[i - 2], tile_vdone[i - 2]] if i >= 2 else []
            t_xn = K.v("tensor_scalar_mul", xnb[bs_][:], xc[bs_][:], c(rcur),
                       wait=[tlast] + wprev)
            if P0CUT < 2:
                tile_vdone[i] = K.tok("V")
                tile_tdone[i] = K.tok("T")
                last_xq_ev[i] = None
                if i + 2 < NTILE:
                    xdma[i + 2] = K.dma("dS", xt[bs_][:], x_in[(i + 2) * 128:(i + 3) * 128, :],
                                        sem=f"x{bs_}", wait=(tile_vdone[i], K.tok("V")))
                continue
            K.v("tensor_reduce", c(8), xnb[bs_][:], axis=AX.X, op=ALU.max,
                apply_absolute_value=True, wait=(t_xn,))
            K.v("tensor_scalar_max", c(9), c(8), EPS, wait=(K.tok("V"),))
            K.v("reciprocal", c(10), c(9), wait=(K.tok("V"),))
            tsc2 = K.v("tensor_scalar_mul", c(11), c(10), 128.0, wait=(K.tok("V"),))
            K.v("tensor_scalar_mul", dq_t[:, i:i + 1], c(9), 1.0 / 128.0)
            tq1 = K.v("tensor_scalar_mul", scrf[bs_][:], xnb[bs_][:], c(11), wait=(tsc2, t_xn))
            tq2 = K.v("tensor_scalar_add", scrf[bs_][:], scrf[bs_][:], MAGIC, wait=(tq1,))
            tq4 = K.v("tensor_scalar", sqd[bs_][:], scrf[bs_][:], MAGIC, 127.0,
                      ALU.subtract, ALU.min, wait=(tq2, t_sq))
            tile_vdone[i] = K.tok("V")
            if P0CUT < 3:
                tile_tdone[i] = K.tok("T")
                last_xq_ev[i] = None
                if i + 2 < NTILE:
                    xdma[i + 2] = K.dma("dS", xt[bs_][:], x_in[(i + 2) * 128:(i + 3) * 128, :],
                                        sem=f"x{bs_}", wait=(tile_vdone[i], K.tok("V")))
                continue
            ev_xn = []
            for bg in range(4):
                bk = pb[4 + bg]
                w = [t_xn, tok_idc]
                if i >= 1:
                    w.append(last_xq_ev[i - 1])
                for sl in range(4):
                    f = bg * 4 + sl
                    K.t("transpose", bk[:, sl * 128:(sl + 1) * 128],
                        xnb[bs_][:, f * 128:(f + 1) * 128], ident[:],
                        wait=w if sl == 0 else ())
                tbk = K.tok("T")
                for sl in range(4):
                    f = bg * 4 + sl
                    ev_xn.append(K.v("tensor_copy", xnT[f][:, i * 128:(i + 1) * 128],
                                     bk[:, sl * 128:(sl + 1) * 128], wait=(tbk,)))
            ev_xq = []
            if P0CUT < 4:
                tile_tdone[i] = K.tok("T")
                last_xq_ev[i] = ev_xn[-1]
                if i + 2 < NTILE:
                    xdma[i + 2] = K.dma("dS", xt[bs_][:], x_in[(i + 2) * 128:(i + 3) * 128, :],
                                        sem=f"x{bs_}", wait=(tile_vdone[i], tile_tdone[i]))
                continue
            for bg in range(4):
                bk = pb[4 + bg]
                for sl in range(4):
                    f = bg * 4 + sl
                    K.t("transpose", bk[:, sl * 128:(sl + 1) * 128],
                        sqd[bs_][:, f * 128:(f + 1) * 128], ident[:],
                        wait=(tq4, ev_xn[bg * 4 + 3]) if sl == 0 else ())
                tbk = K.tok("T")
                for sl in range(4):
                    f = bg * 4 + sl
                    ev_xq.append(K.v("tensor_copy", xqT[f][:, i * 128:(i + 1) * 128],
                                     bk[:, sl * 128:(sl + 1) * 128], wait=(tbk,)))
            last_xq_ev[i] = ev_xq[-1]
            tile_tdone[i] = K.tok("T")
            if i + 2 < NTILE:
                xdma[i + 2] = K.dma("dS", xt[bs_][:], x_in[(i + 2) * 128:(i + 3) * 128, :],
                                    sem=f"x{bs_}", wait=(tile_vdone[i], tile_tdone[i]))
        K.wait("tensor", tok_pb6_free, K.tok("V"))
        for half in range(2 if P0CUT >= 5 else 0):
            for i4 in range(4):
                i = half * 4 + i4
                K.t("transpose", pb[6][0:1, i4 * 128:(i4 + 1) * 128], dq_t[:, i:i + 1], ident[:])
            ttr = K.tok("T")
            K.v("tensor_copy", dqrow[0:1, half * 512:(half + 1) * 512], pb[6][0:1, 0:512], wait=(ttr,))
            K.wait("tensor", K.tok("V"))
        tok_dqrow = K.tok("V")

        for f in range(16 if P0CUT >= 6 else 0):
            for g_ in range(4):
                K.dma("dA", xq_scr[:].rearrange("g p (k c) -> g p k c", k=16)[g_, :, f],
                      xqT[f][:, g_ * 256:(g_ + 1) * 256], sem="xqw", wait=(K.tok("V"),))
        tok_xq_scr = K.tok("xqw") if "xqw" in K.cnt else None

        tok_ag = None
        if stage >= 4:
            sc_pp = ExitStack()
            ptile = [sc_pp.enter_context(nc.sbuf_tensor(f"ptile{j}", [128, 1024], F32)) for j in range(2)]
            qtile = [sc_pp.enter_context(nc.sbuf_tensor(f"qtile{j}", [128, 1024], BF16)) for j in range(2)]
            wds_v = wds[:].rearrange("(a p) c -> a p c", p=128)
            wgs_v = wgs[:].rearrange("(a p) c -> a p c", p=128)
            wus_v = wus[:].rearrange("(a p) c -> a p c", p=128)
            ciwq_blk = cin_wq[:].rearrange("w (s p k c) -> w s p k c", s=4, p=128, k=16)

            pdma = [None, None]
            pcons = [None, None]
            pmm = [None, None]
            onepc = sm2[:, 33:34]
            t_one2 = K.v("memset", onepc, 1.0)
            for wi, (wv, ntl) in enumerate([(wgs_v, 16), (wus_v, 16), (wds_v, 64)]):
                for a in range(ntl):
                    j = (a + wi) % 2
                    w = [pcons[j]] if pcons[j] else []
                    srcv = wv[a] if ntl == 16 else wv[a][:, 0:256]
                    pdma[j] = K.dma("dG", ptile[j][:, 0:(1024 if ntl == 16 else 256)], srcv,
                                    sem=f"pp{j}", wait=w)
                    tr = K.v("tensor_reduce", sm2[:, 34 + j:35 + j], ptile[j][:, 0:(1024 if ntl == 16 else 256)],
                             axis=AX.X, op=ALU.add, apply_absolute_value=True,
                             wait=[pdma[j], pmm[j]])
                    pcons[j] = tr
                    K.wait("tensor", tr, t_one2)
                    pmm[j] = K.t("matmul", pb[0][0:1, wi:wi + 1], sm2[:, 34 + j:35 + j], onepc,
                                 start=(wi == 0 and a == 0), stop=(wi == 2 and a == ntl - 1),
                                 skip_group_check=True)
            t_sums = K.v("tensor_copy", mean_sb[0:1, 0:3], pb[0][0:1, 0:3], wait=(K.tok("T"),))
            tz = K.v("memset", mean_sb[0:1, 3:8], 0.0)
            tmdma = K.dma("dG", cin_mean, mean_sb[0:1, :], sem="ppm", wait=(t_sums, tz))
            cc_m = K.cc("AllReduce", ALU.add, RG, [cin_mean.opt()], [cout_mean.opt()],
                        wait=[tmdma])
            tok_ar = K.tok("CC")
            tmback = K.dma("dG", mean_sb[0:1, :], cout_mean, sem="ppm2", wait=(tok_ar,))
            NELT = float(DM * I_FF)
            t_m1 = K.v("tensor_scalar_mul", mean_sb[0:1, 4:7], mean_sb[0:1, 0:3], 1.0 / NELT, wait=(tmback,))
            t_m2 = K.v("tensor_scalar_max", mean_sb[0:1, 4:7], mean_sb[0:1, 4:7], EPS, wait=(K.tok("V"),))
            t_m3 = K.v("reciprocal", mean_sb[0:1, 0:3], mean_sb[0:1, 4:7], wait=(K.tok("V"),))
            K.wait("tensor", t_m3)
            t_bcm = K.t("matmul", pb[0][:, 0:3], onerow[:], mean_sb[0:1, 0:3], start=True, stop=True)
            t_swc = K.v("tensor_copy", swcol[:, 0:3], pb[0][:, 0:3], wait=(t_bcm,))
            t_cd = K.v("tensor_scalar_mul", mean_sb[0:1, 7:8], mean_sb[0:1, 6:7],
                       1.0 / (128.0 * math.sqrt(float(I_FF))), wait=(t_m2,))
            K.wait("tensor", t_cd, t_swc)
            t_bcd = K.t("matmul", pb[0][:, 4:5], onerow[:], mean_sb[0:1, 7:8], start=True, stop=True)
            t_cdc = K.v("tensor_copy", swcol[:, 3:4], pb[0][:, 4:5], wait=(t_bcd,))
            t_dg = K.v("tensor_scalar", dqg_row[0:1, :], dqrow[0:1, :], mean_sb[0:1, 4:5], None,
                       ALU.mult, wait=(tok_dqrow, t_m2))
            t_du = K.v("tensor_scalar", dqu_row[0:1, :], dqrow[0:1, :], mean_sb[0:1, 5:6], None,
                       ALU.mult, wait=(K.tok("V"),))
            tok_dqgu = K.tok("V")
            pdma = [None, None]
            pcons = [None, None]
            qfree = [None, None]
            for wi, (wv, ntl) in enumerate([(wgs_v, 16), (wus_v, 16), (wds_v, 64)]):
                for a in range(ntl):
                    j = (a + wi) % 2
                    ww = 1024 if ntl == 16 else 256
                    w = [pcons[j]] if pcons[j] else []
                    pdma[j] = K.dma("dG", ptile[j][:, 0:ww], wv[a] if ntl == 16 else wv[a][:, 0:256],
                                    sem=f"pq{j}", wait=w)
                    t1 = K.v("tensor_scalar", ptile[j][:, 0:ww], ptile[j][:, 0:ww],
                             swcol[:, wi:wi + 1], MAGIC, ALU.mult, ALU.add,
                             wait=(pdma[j], t_swc, qfree[j]))
                    t2 = K.v("tensor_scalar", ptile[j][:, 0:ww], ptile[j][:, 0:ww],
                             MAGIC, 1.0, ALU.subtract, ALU.min, wait=(K.tok("V"),))
                    t3 = K.v("tensor_scalar_max", qtile[j][:, 0:ww], ptile[j][:, 0:ww],
                             -1.0, wait=(K.tok("V"),))
                    pcons[j] = t3
                    if ntl == 16:
                        for sub in range(4):
                            K.dma("dA", ciwq_blk[wi, sub, :, a],
                                  qtile[j][:, sub * 256:(sub + 1) * 256], sem=f"pw{j}",
                                  wait=(t3,))
                        qfree[j] = K.tok(f"pw{j}")
                    else:
                        qfree[j] = K.dma("dA", ciwq_blk[2, a >> 4, :, a & 15],
                                         qtile[j][:, 0:256], sem=f"pw{j}", wait=(t3,))
            tok_pw = [K.tok("pw0"), K.tok("pw1")]
            cc_ag = K.cc("AllGather", ALU.bypass, RG, [cin_wq.opt()], [cout_wq.opt()],
                         wait=tok_pw)
            tok_ag = K.tok("CC")
            sc_pp.close()
            K.barrier(dma_sems=["pp0", "pp1", "pq0", "pq1", "pw0", "pw1", "ppm", "ppm2"])

        q16v = cin_q16[:].rearrange("r (s p t) -> r s p t", s=4, p=128)
        k16v = cin_k16[:].rearrange("r (s p t) -> r s p t", s=4, p=128)
        v32v = cin_v32[:].rearrange("r (s p t) -> r s p t", s=2, p=128)
        tok_p1 = None
        if stage >= 2:
            sc_p0.close()
            K.barrier(dma_sems=["xqw", "x0", "x1"])
            sc_p1 = ExitStack()
            wqb = [sc_p1.enter_context(nc.sbuf_tensor(f"wqb{j}", [128, 8192], F32))
                   for j in range(2)]
            stq = [sc_p1.enter_context(nc.sbuf_tensor(f"stq{j}", [128, 512], F32)) for j in range(2)]
            sthi = [sc_p1.enter_context(nc.sbuf_tensor(f"sthi{j}", [128, 512], F16)) for j in range(2)]
            stlo = [sc_p1.enter_context(nc.sbuf_tensor(f"stlo{j}", [128, 512], F16)) for j in range(2)]

            wq_tok = [None, None]
            wq_cons = [None, None]

            def load_wq(og):
                j = og % 2
                w = [wq_cons[j]] if wq_cons[j] else []
                K.dma("dS", wqb[j][:, 0:4096], wqkvT[og][:, 0:4096], sem=f"wq{j}", wait=w)
                K.dma("dA", wqb[j][:, 4096:8192], wqkvT[og][:, 4096:8192], sem=f"wqa{j}", wait=w)
                wq_tok[j] = [K.tok(f"wq{j}"), K.tok(f"wqa{j}")]

            load_wq(4)
            load_wq(5)
            bank_tok = [None] * 4
            slot_q_free = [None, None]
            slot_h_free = [None, None]
            slot_l_free = [None, None]
            m = 0
            tok_a2a_kv_holder = [None]
            og_order = [4, 5, 6, 7, 0, 1, 2, 3]
            for og in og_order:
                j = og % 2
                for tg in range(2):
                    for oc4 in range(4):
                        oc = og * 4 + oc4
                        bk = pb[oc4]
                        K.wait("tensor", *(wq_tok[j] + [bank_tok[oc4]]))
                        for kc in range(16):
                            K.t("matmul", bk[:, 0:512],
                                wqb[j][:, kc * 512 + oc4 * 128:kc * 512 + (oc4 + 1) * 128],
                                xnT[kc][:, tg * 512:(tg + 1) * 512],
                                start=(kc == 0), stop=(kc == 15))
                        tmm = K.tok("T")
                        sq_ = m % 2
                        t_qf = K.s("activation", stq[sq_][:], bk[:, 0:512], AF.Identity,
                                   bias=qkvb_sb[:, oc:oc + 1], scale=1.0,
                                   wait=(tmm, slot_q_free[sq_]))
                        bank_tok[oc4] = t_qf
                        if oc < 24:
                            t_hi = K.s("activation", sthi[sq_][:], stq[sq_][:], AF.Identity,
                                       bias=0.0, scale=1.0, wait=(t_qf, slot_h_free[sq_]))
                            t_lo = K.v("tensor_sub", stlo[sq_][:], stq[sq_][:], sthi[sq_][:],
                                       wait=(t_hi, slot_l_free[sq_]))
                            slot_q_free[sq_] = t_lo
                            if oc < 16:
                                dsts = [oc >> 1]
                                dstv = q16v
                                pre = "q"
                            else:
                                g = (oc - 16) >> 1
                                dsts = [2 * g, 2 * g + 1]
                                dstv = k16v
                                pre = "k"
                            sec = 0 + (oc & 1)
                            secl = 2 + (oc & 1)
                            for dj in dsts:
                                K.dma("dA", dstv[dj, sec, :, tg * 512:(tg + 1) * 512],
                                      sthi[sq_][:], sem=f"{pre}h{sq_}", wait=(t_hi,))
                                K.dma("dA", dstv[dj, secl, :, tg * 512:(tg + 1) * 512],
                                      stlo[sq_][:], sem=f"{pre}l{sq_}", wait=(t_lo,))
                            slot_h_free[sq_] = K.tok(f"{pre}h{sq_}")
                            slot_l_free[sq_] = K.tok(f"{pre}l{sq_}")
                        else:
                            g = (oc - 24) >> 1
                            half = (oc - 24) & 1
                            for dj in (2 * g, 2 * g + 1):
                                K.dma("dA", v32v[dj, half, :, tg * 512:(tg + 1) * 512],
                                      stq[sq_][:], sem=f"qv{sq_}", wait=(t_qf,))
                            slot_q_free[sq_] = K.tok(f"qv{sq_}")
                        m += 1
                oi = og_order.index(og)
                if oi == 3:
                    kv_toks = [K.tok(s) for s in list(K.cnt) if s.startswith(("kh", "kl", "qv"))]
                    cck = K.cc("AllToAll", ALU.bypass, RG, [cin_k16.opt()], [cout_k16.opt()],
                               wait=kv_toks)
                    ccv = K.cc("AllToAll", ALU.bypass, RG, [cin_v32.opt()], [cout_v32.opt()])
                    tok_a2a_kv_holder[0] = K.tok("CC")
                if oi + 2 < 8:
                    wq_cons[og_order[oi + 2] % 2] = K.tok("T")
                    load_wq(og_order[oi + 2])
            stage_toks = [K.tok(s) for s in list(K.cnt)
                          if s.startswith(("qh", "ql"))]
            tok_p1 = stage_toks
            sc_p1.close()
            sc_xnT.close()
            K.barrier(dma_sems=[s for s in K.cnt if s.startswith(("qh", "ql", "kh", "kl", "qv", "wq"))])

            if stage >= 2:
                ccq = K.cc("AllToAll", ALU.bypass, RG, [cin_q16.opt()], [cout_q16.opt()],
                           wait=stage_toks)
                tok_a2a_qkv = K.tok("CC")

        tok_a2a_o = None
        if stage >= 3:
            sc_p2 = ExitStack()
            qh = [sc_p2.enter_context(nc.sbuf_tensor(f"qh{c}", [128, S], F16)) for c in range(2)]
            ql = [sc_p2.enter_context(nc.sbuf_tensor(f"ql{c}", [128, S], F16)) for c in range(2)]
            kh = [sc_p2.enter_context(nc.sbuf_tensor(f"kh{c}", [128, S], F16)) for c in range(2)]
            kl = [sc_p2.enter_context(nc.sbuf_tensor(f"kl{c}", [128, S], F16)) for c in range(2)]
            vT = sc_p2.enter_context(nc.sbuf_tensor("vT", [128, 32 * 256], F32))
            s1b = sc_p2.enter_context(nc.sbuf_tensor("s1b", [128, S], F32))
            s2b = sc_p2.enter_context(nc.sbuf_tensor("s2b", [128, S], F32))
            s1c = sc_p2.enter_context(nc.sbuf_tensor("s1c", [128, S], F32))
            s2c = sc_p2.enter_context(nc.sbuf_tensor("s2c", [128, S], F32))
            vtmp = [s1b[:, 0:1024], s1b[:, 1024:2048]]
            ptT = [sc_p2.enter_context(nc.sbuf_tensor(f"ptT{j}", [128, 512], F32)) for j in range(2)]
            oS = [sc_p2.enter_context(nc.sbuf_tensor(f"oS{j}", [128, 512], F32)) for j in range(2)]
            osc4 = sc_p2.enter_context(nc.sbuf_tensor("osc4", [128, 1024], F32))
            osc = sc_p2.enter_context(nc.sbuf_tensor("osc", [128, 256], F32))
            co16 = cout_q16[:].rearrange("r (s p t) -> r s p t", s=4, p=128)
            ck16 = cout_k16[:].rearrange("r (s p t) -> r s p t", s=4, p=128)
            co32 = cout_v32[:].rearrange("r (s p t) -> r s p t", s=2, p=128)
            cino = cin_o[:].rearrange("r (c p t) -> r c p t", c=2, p=128)

            u_done = []
            u0_osh = None
            vtr_done = [None, None]
            vtr_prev_ev = None
            for u in range(2):
                w_ld = [tok_a2a_qkv] + u_done
                w_kv = [tok_a2a_kv_holder[0]] + u_done
                if u == 1:
                    w_kv = w_kv + u0_osh
                    w_ld = w_ld + u0_osh
                for c in range(2):
                    for r4 in range(4):
                        r = 4 * u + r4
                        K.dma("dS", kh[c][:, r4 * 1024:(r4 + 1) * 1024], ck16[r, 0 + c], sem=f"ak{u}", wait=w_kv)
                        K.dma("dS", kl[c][:, r4 * 1024:(r4 + 1) * 1024], ck16[r, 2 + c], sem=f"ak{u}", wait=w_kv)
                for c in range(2):
                    for r4 in range(4):
                        r = 4 * u + r4
                        K.dma("dS", qh[c][:, r4 * 1024:(r4 + 1) * 1024], co16[r, 0 + c], sem=f"aq{u}", wait=w_ld)
                        K.dma("dS", ql[c][:, r4 * 1024:(r4 + 1) * 1024], co16[r, 2 + c], sem=f"aq{u}", wait=w_ld)
                tok_ak = K.tok(f"ak{u}")
                tok_aq0 = K.tok(f"aq{u}")
                tok_aq = [tok_ak, tok_aq0]
                for half in range(2):
                    for r4 in range(4):
                        r = 4 * u + r4
                        jj = (half * 4 + r4) % 2
                        wv = w_kv + ([vtr_done[jj]] if vtr_done[jj] is not None else [])
                        vdma = K.dma("dS", vtmp[jj], co32[r, half], sem=f"av{u}_{half}_{r4}", wait=wv)
                        for grp in range(2):
                            wtr = [vdma, tok_idc] + u_done
                            if vtr_prev_ev is not None:
                                wtr.append(vtr_prev_ev)
                            for sl in range(4):
                                t8 = grp * 4 + sl
                                K.t("transpose", pb[7][:, sl * 128:(sl + 1) * 128],
                                    s1b[:, jj * 1024 + t8 * 128: jj * 1024 + (t8 + 1) * 128],
                                    ident[:], wait=wtr if sl == 0 else ())
                            tbk = K.tok("T")
                            kb0 = r4 * 8 + grp * 4
                            dst = vT[:].rearrange("p (k e) -> p k e", k=32)[
                                :, kb0:kb0 + 4, half * 128:half * 128 + 128]
                            vtr_prev_ev = K.v("tensor_copy", dst, pb[7][:, 0:512].rearrange(
                                "p (k e) -> p k e", k=4), wait=(tbk,))
                        vtr_done[jj] = K.tok("T")
                tok_vT = K.tok("V")

                tok_bounds = K.tok("V")

                prev_ptT_cons = [None, None]
                prev_ptT_ev = [None, None]
                prev_s_cons = [None, None]
                prev_sq = None
                prev_o_ev = None
                osc4_free = None
                SB1 = [s1b, s1c]
                SB2 = [s2b, s2c]
                state = [None, None]
                for qbi in range(33):
                    if qbi >= 1:
                        qb = qbi - 1
                        par = qb % 2
                        st = state[par]
                        cq = lambda j: sm2[:, 96 + 32 * par + j:96 + 32 * par + j + 1]
                        nkb = qb // 4 + 1
                        ncols = nkb * 512
                        sA, sB = SB1[par], SB2[par]
                        t_e1 = K.s("activation", sA[:, 0:ncols], sA[:, 0:ncols], AF.Exp,
                                   bias=cq(14), scale=1.0, accum_out=cq(1),
                                   wait=[st["ev1"], st["tmx"]])
                        t_e2 = K.s("activation", sB[:, 0:ncols], sB[:, 0:ncols], AF.Exp,
                                   bias=cq(15), scale=1.0, accum_out=cq(2), wait=(t_e1,))
                        t_a = K.v("reciprocal", cq(3), cq(1), wait=(t_e1,))
                        t_b0 = K.v("reciprocal", cq(4), cq(2), wait=(t_e2,))
                        t_b = K.v("tensor_mul", cq(5), cq(4), lam_col[:], wait=(t_b0, tok_lam))
                        t_c1 = K.v("tensor_scalar_mul", sB[:, 0:ncols], sB[:, 0:ncols],
                                   cq(5), wait=(t_b, t_e2))
                        t_c2 = K.v("scalar_tensor_tensor", sA[:, 0:ncols], sA[:, 0:ncols],
                                   cq(3), sB[:, 0:ncols], ALU.mult, ALU.subtract,
                                   wait=(t_c1, t_a))
                        state[par]["c2"] = t_c2
                    if qbi < 32:
                        qb = qbi
                        par = qb % 2
                        cq = lambda j: sm2[:, 96 + 32 * par + j:96 + 32 * par + j + 1]
                        nkb = qb // 4 + 1
                        diag = qb // 4
                        sA, sB = SB1[par], SB2[par]
                        ev1 = [None] * nkb
                        ev2 = [None] * nkb
                        tmx = None
                        for kb in range(nkb):
                            b1 = pb[kb % 2]
                            b2 = pb[2 + kb % 2]
                            wmm = list(tok_aq)
                            if prev_s_cons[par] is not None:
                                wmm.append(prev_s_cons[par])
                            if kb >= 2:
                                wmm += [ev1[kb - 2], ev2[kb - 2]]
                            elif qbi >= 1:
                                wmm += [prev_ev1_last, prev_ev2_last]
                            K.wait("tensor", *wmm)
                            for (lh, rh, st_, sp_) in ((qh, kh, True, False), (qh, kl, False, False), (ql, kh, False, True)):
                                K.t("matmul", b1[:, 0:512], lh[0][:, qb * 128:(qb + 1) * 128],
                                    rh[0][:, kb * 512:(kb + 1) * 512], start=st_, stop=sp_)
                            for (lh, rh, st_, sp_) in ((qh, kh, True, False), (qh, kl, False, False), (ql, kh, False, True)):
                                K.t("matmul", b2[:, 0:512], lh[1][:, qb * 128:(qb + 1) * 128],
                                    rh[1][:, kb * 512:(kb + 1) * 512], start=st_, stop=sp_)
                            tmm = K.tok("T")
                            if kb == diag:
                                msl = mask_sb[:, (qb % 4) * 512:(qb % 4 + 1) * 512]
                                ev1[kb] = K.v("scalar_tensor_tensor", sA[:, kb * 512:(kb + 1) * 512],
                                              b1[:, 0:512], SCALE, msl, ALU.mult, ALU.add,
                                              wait=(tmm,))
                                ev2[kb] = K.v("scalar_tensor_tensor", sB[:, kb * 512:(kb + 1) * 512],
                                              b2[:, 0:512], SCALE, msl, ALU.mult, ALU.add,
                                              wait=(tmm,))
                            else:
                                ev1[kb] = K.s("activation", sA[:, kb * 512:(kb + 1) * 512], b1[:, 0:512],
                                              AF.Identity, bias=0.0, scale=SCALE, wait=(tmm,))
                                ev2[kb] = K.v("tensor_scalar_mul", sB[:, kb * 512:(kb + 1) * 512],
                                              b2[:, 0:512], SCALE, wait=(tmm,))
                            K.v("reduce_max", cq(16 + (kb % 2)), sA[:, kb * 512:(kb + 1) * 512],
                                axis=AX.X, negate=True, wait=(ev1[kb],))
                            if kb == 0:
                                K.v("tensor_copy", cq(14), cq(16), wait=(K.tok("V"), K.tok("SC")))
                            else:
                                K.v("tensor_tensor", cq(14), cq(14), cq(16 + (kb % 2)), ALU.min, wait=(K.tok("V"),))
                            K.v("reduce_max", cq(18 + (kb % 2)), sB[:, kb * 512:(kb + 1) * 512],
                                axis=AX.X, negate=True, wait=(ev2[kb],))
                            if kb == 0:
                                tmx = K.v("tensor_copy", cq(15), cq(18), wait=(K.tok("V"), K.tok("SC")))
                            else:
                                tmx = K.v("tensor_tensor", cq(15), cq(15), cq(18 + (kb % 2)), ALU.min, wait=(K.tok("V"),))
                        prev_ev1_last, prev_ev2_last = ev1[-1], ev2[-1]
                        state[par] = {"ev1": ev1[-1], "ev2": ev2[-1], "tmx": tmx}
                    if qbi >= 1:
                        qb = qbi - 1
                        par = qb % 2
                        cq = lambda j: sm2[:, 96 + 32 * par + j:96 + 32 * par + j + 1]
                        sA = SB1[par]
                        t_c2 = state[par]["c2"]
                        n128 = qb + 1
                        for g4 in range(0, n128, 4):
                            cnt4 = min(4, n128 - g4)
                            sl_ = (g4 // 4) % 2
                            bkt = pb[4 + sl_]
                            wtr = [t_c2, prev_ptT_ev[sl_]]
                            for j in range(cnt4):
                                kb1 = g4 + j
                                K.t("transpose", bkt[:, j * 128:(j + 1) * 128],
                                    sA[:, kb1 * 128:(kb1 + 1) * 128], ident[:],
                                    wait=wtr if j == 0 else ())
                            trs_t = K.tok("T")
                            t_ev = K.v("tensor_copy", ptT[sl_][:, 0:cnt4 * 128], bkt[:, 0:cnt4 * 128],
                                       wait=[trs_t, prev_ptT_cons[sl_]])
                            prev_ptT_ev[sl_] = t_ev
                            K.wait("tensor", t_ev, tok_vT, prev_sq)
                            for j in range(cnt4):
                                kb1 = g4 + j
                                K.t("matmul", pb[6][:, 0:256], ptT[sl_][:, j * 128:(j + 1) * 128],
                                    vT[:, kb1 * 256:(kb1 + 1) * 256],
                                    start=(kb1 == 0), stop=(kb1 == n128 - 1))
                            prev_ptT_cons[sl_] = K.tok("T")
                        t_pv = K.tok("T")
                        prev_s_cons[par] = t_pv
                        t_sq = K.s("activation", osc[:], pb[6][:, 0:256], AF.Square,
                                   accum_out=cq(6), wait=[t_pv, prev_o_ev])
                        prev_sq = t_sq
                        tv2 = K.v("tensor_scalar", cq(7), cq(6), 1.0 / 256.0, EPS,
                                  ALU.mult, ALU.add, wait=(t_sq,))
                        tu2 = K.s("activation", cq(8), cq(7), AF.Ln, wait=(tv2,))
                        tl2 = K.s("activation", cq(9), cq(8), AF.Exp, scale=-0.5, wait=(tu2,))
                        t_o1 = K.v("tensor_scalar", osc4[:, (qb % 4) * 256:(qb % 4 + 1) * 256],
                                   pb[6][:, 0:256], cq(9), 1.0 - LAMBDA_INIT,
                                   ALU.mult, ALU.mult, wait=[tl2, t_sq, osc4_free])
                        t_o2 = K.v("tensor_mul", osc4[:, (qb % 4) * 256:(qb % 4 + 1) * 256],
                                   osc4[:, (qb % 4) * 256:(qb % 4 + 1) * 256], subln_bc[:],
                                   wait=(K.tok("V"),))
                        if qb % 4 == 3:
                            qb0_ = qb - 3
                            K.wait("tensor", t_o2, prev_o_ev)
                            for half4 in range(2):
                                if half4 == 1:
                                    K.wait("tensor", prev_o_ev)
                                for j4 in range(4):
                                    qq = qb0_ + (half4 * 4 + j4) // 2
                                    cch = (half4 * 4 + j4) % 2
                                    K.t("transpose", pb[7][:, j4 * 128:(j4 + 1) * 128],
                                        osc4[:, (qq % 4) * 256 + cch * 128:(qq % 4) * 256 + (cch + 1) * 128],
                                        ident[:])
                                t_tro = K.tok("T")
                                slo_ = half4
                                t_oev2 = K.v("tensor_copy", oS[slo_][:, 0:512], pb[7][:, 0:512],
                                             wait=[t_tro, (f"osh{u}_{slo_}", K.cnt.get(f"osh{u}_{slo_}", 0))])
                                prev_o_ev = t_oev2
                                r4o = qb0_ // 8
                                for j4 in range(4):
                                    qq = qb0_ + (half4 * 4 + j4) // 2
                                    cch = (half4 * 4 + j4) % 2
                                    K.dma("dA", cino[4 * u + r4o, cch][:, (qq % 8) * 128:(qq % 8 + 1) * 128],
                                          oS[slo_][:, j4 * 128:(j4 + 1) * 128], sem=f"osh{u}_{slo_}",
                                          wait=(t_oev2,))
                            osc4_free = K.tok("T")
                if u == 0:
                    u0_osh = [K.tok("osh0_0"), K.tok("osh0_1")]
                    u_done = [K.tok("V"), K.tok("SC"), K.tok("T")]
                    prev_ptT_cons = [None, None]
            tok_osh1 = [K.tok("osh1_0"), K.tok("osh1_1")]
            cc_o = K.cc("AllToAll", ALU.bypass, RG, [cin_o.opt()], [cout_o.opt()],
                        wait=u0_osh + tok_osh1)
            tok_a2a_o = K.tok("CC")
            sc_p2.close()
            sc_mask.close()
            K.barrier(dma_sems=[sname for sname in K.cnt if sname.startswith(("aq", "ak", "av", "osh"))])

        if "d_oT" in dbg_out and stage >= 3:
            coo = cout_o[:].rearrange("r (c p t) -> r c p t", c=2, p=128)
            tmpo = nc.alloc_sbuf_tensor("tmpo", [128, TOK], F32)
            for r in range(8):
                for cch in range(2):
                    tld = K.dma("dS", tmpo[:], coo[r, cch], sem="dbo",
                                wait=[tok_a2a_o, ("dbo2", K.cnt.get("dbo2", 0))])
                    K.dma("dS", dbg_out["d_oT"][(r * 2 + cch) * 128:(r * 2 + cch + 1) * 128, :],
                          tmpo[:], sem="dbo2", wait=(tld,))

        if "d_xnT" in dbg_out:
            for f in range(16):
                K.dma("dS", dbg_out["d_xnT"][f * 128:(f + 1) * 128, :], xnT[f][:],
                      sem="dbg1", wait=(K.tok("V"),))
        if "d_xq" in dbg_out:
            xqf = xnb
            for f in range(16):
                tcp = K.v("tensor_copy", xqf[f % 2][:, 0:TOK], xqT[f][:],
                          wait=(K.tok("V"), (f"dbg2_{f % 2}", 16 * (f // 2)) if f >= 2 else None))
                K.dma("dS", dbg_out["d_xq"][f * 128:(f + 1) * 128, :], xqf[f % 2][:, 0:TOK],
                      sem=f"dbg2_{f % 2}", wait=(tcp,))
        if "d_dq" in dbg_out:
            K.dma("dS", dbg_out["d_dq"][0:1, :], dqrow[0:1, :], sem="dbg3", wait=(tok_dqrow,))

        if "d_q16" in dbg_out and stage >= 2:
            tmpf = nc.alloc_sbuf_tensor("tmpf", [128, NT], F32)
            tmph = nc.alloc_sbuf_tensor("tmph", [128, NT], F16)
            tmpl = nc.alloc_sbuf_tensor("tmpl", [128, NT], F16)
            co16 = cout_q16[:].rearrange("r (s p t) -> r s p t", s=8, p=128)
            co32 = cout_v32[:].rearrange("r (s p t) -> r s p t", s=2, p=128)
            for sec in range(4):
                vv_ = [co16, co16, ck16, ck16][sec]
                shi = [0, 1, 0, 1][sec]
                slo = [2, 3, 2, 3][sec]
                for r in range(8):
                    K.dma("dS", tmph[:, r * 1024:(r + 1) * 1024], vv_[r, shi], sem="dbh",
                          wait=(tok_a2a_qkv, K.tok("V")))
                    K.dma("dS", tmpl[:, r * 1024:(r + 1) * 1024], vv_[r, slo], sem="dbh")
                tld = K.tok("dbh")
                tadd = K.v("tensor_add", tmpf[:], tmph[:], tmpl[:],
                           wait=(tld, ("dbq", K.cnt.get("dbq", 0))))
                K.dma("dS", dbg_out["d_q16"][sec * 128:(sec + 1) * 128, :], tmpf[:],
                      sem="dbq", wait=(tadd,))
            for half in range(2):
                for r in range(8):
                    K.dma("dS", tmpf[:, r * 1024:(r + 1) * 1024], co32[r, half], sem="dbv",
                          wait=(tok_a2a_qkv, ("dbq", K.cnt.get("dbq", 0)), K.tok("V")))
                tld = K.tok("dbv")
                K.dma("dS", dbg_out["d_q16"][(4 + half) * 128:(5 + half) * 128, :], tmpf[:],
                      sem="dbq", wait=(tld,))

        tok_ffn_scr = None
        if stage >= 4:
            if stage < 90 and not (stage >= 3):
                pass
            if not (stage >= 3):
                sc_mask.close()
            sc_p3 = ExitStack()
            W2p = sc_p3.enter_context(nc.sbuf_tensor("W2p", [128, 16384], F32))
            Zb = sc_p3.enter_context(nc.sbuf_tensor("Zb", [128, 16384], F32))
            W2qp = Zb[:].bitcast(BF16)
            wstr = [sc_p3.enter_context(nc.sbuf_tensor(f"wstr{j}", [128, 4096], BF16)) for j in range(4)]
            xqs = sc_p3.enter_context(nc.sbuf_tensor("xqs", [128, 4096], BF16))
            itile = [sc_p3.enter_context(nc.sbuf_tensor(f"it{j}", [128, 256], F32)) for j in range(4)]
            utile = [sc_p3.enter_context(nc.sbuf_tensor(f"ut{j}", [128, 256], F32)) for j in range(2)]
            sgt = sc_p3.enter_context(nc.sbuf_tensor("sgt", [128, 256], F32))
            sut = sc_p3.enter_context(nc.sbuf_tensor("sut", [128, 256], F32))
            sdt = sc_p3.enter_context(nc.sbuf_tensor("sdt", [128, 256], F32))
            scl = sc_p3.enter_context(nc.sbuf_tensor("scl", [128, 128], F32))
            zti = [sc_p3.enter_context(nc.sbuf_tensor(f"zti{j}", [128, 512], F32)) for j in range(2)]
            fout = [sc_p3.enter_context(nc.sbuf_tensor(f"fout{j}", [128, 256], F32)) for j in range(2)]
            cowq_blk = cout_wq[:].rearrange("w (s p f) -> w s p f", s=4, p=128)
            xqv = xq_scr[:]

            gu_cons = [None, None]
            dn_cons = None
            fg_done = None
            pb03_free = None
            it_free = [None] * 4
            ut_free = [None] * 2
            fo_free = [None] * 2
            prev_gbc = None
            st2_zti_cons = [None, None]
            for g in range(NGF):
                K.dma("dS", xqs[:], xq_scr[g], sem="fxq",
                      wait=[tok_xq_scr] + ([fg_done] if fg_done else []))
                tok_xqs = K.tok("fxq")
                K.wait("tensor", tok_dqgu, prev_gbc)
                t_g1 = K.t("matmul", pb[6][:, 0:256], onerow[:], dqg_row[0:1, g * 256:(g + 1) * 256],
                           start=True, stop=True)
                t_g2 = K.t("matmul", pb[7][:, 0:256], onerow[:], dqu_row[0:1, g * 256:(g + 1) * 256],
                           start=True, stop=True)
                t_gb = K.v("tensor_copy", sgt[:], pb[6][:, 0:256], wait=[t_g1, prev_gbc])
                t_ub = K.v("tensor_copy", sut[:], pb[7][:, 0:256], wait=(t_g2,))
                prev_gbc = K.tok("V")
                st1_sl = 0
                st1_ev_last = None
                st1_bank_tok = [None, None]
                for ob in range(32):
                    jj = ob % 2
                    r_ = ob >> 2
                    sub = ob & 3
                    wld = [tok_ag] + ([gu_cons[jj]] if gu_cons[jj] else []) \
                          + ([dn_cons] if (ob < 2 and dn_cons) else [])
                    K.dma("dS", wstr[jj][:], cowq_blk[3 * r_, sub], sem=f"fg{jj}", wait=wld)
                    K.dma("dA", wstr[2 + jj][:], cowq_blk[3 * r_ + 1, sub], sem=f"fu{jj}", wait=wld)
                    tok_wg = K.tok(f"fg{jj}")
                    tok_wu = K.tok(f"fu{jj}")
                    if "d_ops" in dbg_out and g == 0 and ob == 0:
                        tmpo1 = W2p[:, 0:4096]
                        tc1 = K.v("tensor_copy", tmpo1, xqs[:], wait=(tok_xqs,))
                        K.dma("dS", dbg_out["d_ops"][0:128, :], tmpo1, sem="dbo1", wait=(tc1,))
                        tc2 = K.v("tensor_copy", tmpo1, wstr[jj][:], wait=(tok_wg, ("dbo1", 16)))
                        K.dma("dS", dbg_out["d_ops"][128:256, :], tmpo1, sem="dbo1", wait=(tc2,))
                        K.wait("vector", ("dbo1", 32))
                    K.wait("tensor", tok_wg, tok_wu, tok_xqs, pb03_free)
                    for oc2 in range(2):
                        for kc in range(16):
                            K.t("matmul", pb[oc2][:, 0:256],
                                wstr[jj][:, kc * 256 + oc2 * 128: kc * 256 + oc2 * 128 + 128],
                                xqs[:, kc * 256:(kc + 1) * 256],
                                start=(kc == 0), stop=(kc == 15))
                        for kc in range(16):
                            K.t("matmul", pb[2 + oc2][:, 0:256],
                                wstr[2 + jj][:, kc * 256 + oc2 * 128: kc * 256 + oc2 * 128 + 128],
                                xqs[:, kc * 256:(kc + 1) * 256],
                                start=(kc == 0), stop=(kc == 15))
                    gu_cons[jj] = K.tok("T")
                    tmmgu = gu_cons[jj]
                    for oc2 in range(2):
                        a_ = ob * 2 + oc2
                        isl = a_ % 4
                        usl = a_ % 2
                        t_ga = K.v("tensor_mul", itile[isl][:], pb[oc2][:, 0:256], sgt[:],
                                   wait=[tmmgu, t_gb, it_free[isl]])
                        if "d_g0" in dbg_out and g == 0 and ob == 0 and oc2 == 0:
                            K.dma("dS", dbg_out["d_g0"][0:128, :], itile[isl][:], sem="dbg0", wait=(t_ga,))
                            K.wait("scalar", ("dbg0", 16))
                        t_si = K.s("activation", itile[isl][:], itile[isl][:], AF.Silu, wait=(t_ga,))
                        t_uv = K.v("tensor_mul", utile[usl][:], pb[2 + oc2][:, 0:256], sut[:],
                                   wait=[tmmgu, t_ub, ut_free[usl]])
                        t_iv = K.v("tensor_mul", itile[isl][:], itile[isl][:], utile[usl][:],
                                   wait=(t_si, t_uv))
                        ut_free[usl] = t_iv
                        sb_ = 4 + (a_ % 2)
                        K.wait("tensor", t_iv, st1_bank_tok[a_ % 2])
                        t_s1 = K.t("matmul", pb[sb_][:, 0:256], h128[:], itile[isl][:],
                                   start=True, stop=True)
                        it_free[isl] = t_s1
                        if "d_int" in dbg_out and g == 0:
                            dtok = K.dma("dS", dbg_out["d_int"][a_ * 128:(a_ + 1) * 128, :],
                                         itile[isl][:], sem=f"dbi{isl}", wait=(t_iv,))
                            it_free[isl] = dtok
                        st1_ev_last = K.v("tensor_copy", Zb[:, a_::64],
                                          pb[sb_][:, 0:256], wait=(t_s1,))
                        st1_bank_tok[a_ % 2] = st1_ev_last
                    pb03_free = K.tok("V")
                tok_z = K.tok("V")
                st2_bank = [None, None]
                for p4 in range(32):
                    slz = p4 % 2
                    K.wait("tensor", tok_z, st2_zti_cons[slz])
                    for j4 in range(4):
                        p_ = p4 * 4 + j4
                        K.t("transpose", pb[0 + slz][:, j4 * 128:(j4 + 1) * 128],
                            Zb[:, p_ * 128:(p_ + 1) * 128], ident[:])
                    ttr = K.tok("T")
                    tze = K.v("tensor_copy", zti[slz][:], pb[0 + slz][:, 0:512],
                              wait=[ttr, st2_zti_cons[slz]])
                    K.wait("tensor", tze, st2_bank[slz])
                    for j4 in range(4):
                        K.t("matmul", pb[2 + slz][:, j4 * 128:(j4 + 1) * 128], bd64[:],
                            zti[slz][:, j4 * 128:(j4 + 1) * 128],
                            start=(j4 == 0), stop=(j4 == 3), skip_group_check=True)
                    tm2 = K.tok("T")
                    st2_zti_cons[slz] = tm2
                    w2_ev = K.v("tensor_copy", W2p[:, p4 * 512:(p4 + 1) * 512], pb[2 + slz][:, 0:512],
                                wait=(tm2,))
                    st2_bank[slz] = w2_ev
                tok_w2 = K.tok("V")
                t_cm = K.v("tensor_reduce", sm2[:, 120:121] if False else scl[:, 0:128],
                           W2p[:].rearrange("p (q b) -> p q b", q=128), axis=AX.X, op=ALU.max,
                           apply_absolute_value=True, wait=(tok_w2,))
                K.wait("tensor", t_cm)
                t_trc = K.t("transpose", pb[6][:, 0:128], scl[:, 0:128], ident[:])
                t_cm2 = K.v("tensor_copy", scl[:, 0:128], pb[6][:, 0:128], wait=(t_cm, t_trc))
                t_M = K.v("tensor_reduce", sm2[:, 120:122],
                          scl[:, 0:128].rearrange("p (t a) -> p t a", t=2), axis=AX.X, op=ALU.max,
                          wait=(t_cm2,))
                t_Mc = K.v("tensor_scalar_max", sm2[:, 120:122], sm2[:, 120:122], EPS, wait=(K.tok("V"),))
                t_s = K.v("reciprocal", sm2[:, 122:124], sm2[:, 120:122], wait=(t_Mc,))
                t_s2 = K.v("tensor_scalar_mul", sm2[:, 122:124], sm2[:, 122:124], 128.0, wait=(K.tok("V"),))
                t_dq = K.v("tensor_scalar", sm2[:, 124:126], sm2[:, 120:122], swcol[:, 3:4], None,
                           ALU.mult, wait=(t_Mc, t_cdc))
                K.wait("tensor", t_s2, t_dq)
                t_ts = K.t("transpose", pb[6][0:2, 0:128], sm2[:, 122:124], ident[:])
                t_sr = K.v("tensor_copy", rowbuf[0:2, 0:128], pb[6][0:2, 0:128], wait=(t_ts,))
                t_t1 = K.t("transpose", pb[6][0:1, 128:256], sm2[:, 124:125], ident[:], wait=(t_sr,))
                t_t2 = K.t("transpose", pb[6][0:1, 256:384], sm2[:, 125:126], ident[:])
                t_f1 = K.v("tensor_copy", rowbuf[0:1, 128:384:2], pb[6][0:1, 128:256], wait=(t_t1, t_t2))
                t_f2 = K.v("tensor_copy", rowbuf[0:1, 129:384:2], pb[6][0:1, 256:384], wait=(K.tok("V"),))
                K.wait("tensor", t_sr, t_f1, t_f2)
                t_dm = K.t("matmul", pb[7][:, 0:128], p2c[:], rowbuf[0:2, 0:128][0:2, :],
                           start=True, stop=True)
                t_scl = K.v("tensor_copy", scl[:, 0:128], pb[7][:, 0:128], wait=(t_dm, t_M))
                t_sd = K.t("matmul", pb[6][:, 0:256], onerow[:], rowbuf[0:1, 128:384],
                           start=True, stop=True)
                t_sdt = K.v("tensor_copy", sdt[:], pb[6][:, 0:256], wait=(t_sd,))
                q1 = K.v("tensor_tensor", W2p[:].rearrange("p (q b) -> p q b", q=128),
                         W2p[:].rearrange("p (q b) -> p q b", q=128),
                         scl[:, 0:128].rearrange("p q -> p q 1" if False else "p (q o) -> p q o", o=1).broadcast_to([128, 128, 128]),
                         ALU.mult, wait=(t_scl, tok_w2))
                q2 = K.v("tensor_scalar_add", W2p[:], W2p[:], MAGIC, wait=(q1,))
                q3 = K.v("tensor_scalar", W2p[:], W2p[:], MAGIC, 127.0,
                         ALU.subtract, ALU.min, wait=(q2,))
                tq_bank = [None, None]
                for p4 in range(32):
                    slz = p4 % 2
                    K.wait("tensor", q3, tq_bank[slz])
                    for j4 in range(4):
                        p_ = p4 * 4 + j4
                        K.t("transpose", pb[2 + slz][:, j4 * 128:(j4 + 1) * 128],
                            W2p[:, p_ * 128:(p_ + 1) * 128], ident[:])
                    ttr = K.tok("T")
                    tq_ev = K.v("tensor_copy", W2qp[:, 16384 + p4 * 512:16384 + (p4 + 1) * 512],
                                pb[2 + slz][:, 0:512], wait=[ttr, tq_bank[slz]])
                    tq_bank[slz] = tq_ev
                tok_w2q = K.tok("V")
                dn_bank = [None, None]
                for ocp in range(8):
                    wldd = [tok_ag, gu_cons[0], gu_cons[1]]
                    if ocp > 0:
                        wldd.append(dn_b_cons)
                    elif dn_cons is not None:
                        wldd.append(dn_cons)
                    for qtr in range(4):
                        K.dma("dS" if qtr % 2 == 0 else "dA", wstr[qtr][:],
                              cowq_blk[3 * ocp + 2, qtr], sem="fd", wait=wldd)
                    tok_wd = K.tok("fd")
                    for oc2 in range(2):
                        oc = ocp * 2 + oc2
                        db_ = oc % 2
                        K.wait("tensor", tok_wd, tok_w2q, dn_bank[db_])
                        for a_ in range(64):
                            K.t("matmul", pb[4 + db_][:, 0:256],
                                wstr[a_ >> 4][:, (a_ & 15) * 256 + oc2 * 128:(a_ & 15) * 256 + oc2 * 128 + 128],
                                W2qp[:, 16384 + a_:16384 + 16384:128][:, 0:256] if False else
                                W2qp[:].rearrange("p (h q t a) -> p h q t a", h=2, q=128, t=2)[:, 1, :, :, a_].rearrange("p q t -> p (q t)"),
                                start=(a_ == 0), stop=(a_ == 63))
                        tdn = K.tok("T")
                        fsl = oc % 2
                        t_fo = K.v("tensor_mul", fout[fsl][:], pb[4 + db_][:, 0:256], sdt[:],
                                   wait=[tdn, t_sdt, fo_free[fsl]])
                        dn_bank[db_] = t_fo
                        fo_free[fsl] = K.dma("dA", ffn_scr[oc][:, g * 256:(g + 1) * 256],
                                             fout[fsl][:], sem=f"ffs{fsl}", wait=(t_fo,))
                    dn_b_cons = K.tok("T")
                dn_cons = K.tok("T")
                fg_done = K.tok("T")
            tok_ffn_scr = None
            for _t in [K.tok("ffs0"), K.tok("ffs1")]:
                pass
            tok_ffn_scr = [K.tok("ffs0"), K.tok("ffs1")]
            sc_p3.close()
            K.barrier(dma_sems=["fxq", "fg0", "fg1", "fu0", "fu1", "fd", "ffs0", "ffs1"])

        if "d_wq" in dbg_out and stage >= 4:
            tmpw = nc.alloc_sbuf_tensor("tmpw", [128, 1024], F32)
            tmpwb = nc.alloc_sbuf_tensor("tmpwb", [128, 1024], BF16)
            cowq2 = cout_wq[:].rearrange("w (k p c) -> w k p c", k=16, p=128)
            for kc in range(16):
                tld = K.dma("dS", tmpwb[:], cowq2[0, kc], sem="dbw",
                            wait=[tok_ag, ("dbw2", K.cnt.get("dbw2", 0)), K.tok("V")])
                tcv = K.v("tensor_copy", tmpw[:], tmpwb[:], wait=(tld,))
                K.dma("dS", dbg_out["d_wq"][kc * 128:(kc + 1) * 128, :], tmpw[:], sem="dbw2", wait=(tcv,))
            K.dma("dS", dbg_out["d_wq"][2048:2049, 0:8], mean_sb[0:1, 0:8], sem="dbw3",
                  wait=(K.tok("V"), t_cdc))
        if "d_ffn" in dbg_out and stage >= 4:
            tmpf3 = nc.alloc_sbuf_tensor("tmpf3", [128, TOK], F32)
            for f in range(16):
                tld = K.dma("dS", tmpf3[:], ffn_scr[f], sem="dbf",
                            wait=tok_ffn_scr + [("dbf2", K.cnt.get("dbf2", 0))])
                K.dma("dS", dbg_out["d_ffn"][f * 128:(f + 1) * 128, :], tmpf3[:], sem="dbf2", wait=(tld,))

        if stage >= 5:
            sc_p4 = ExitStack()
            ybuf = [sc_p4.enter_context(nc.sbuf_tensor(f"yb{k}", [128, TOK], F32)) for k in range(16)]
            sc_p4a = ExitStack()
            oTf = [sc_p4a.enter_context(nc.sbuf_tensor(f"oTf{k}", [128, TOK], F32)) for k in range(16)]
            wo = sc_p4a.enter_context(nc.sbuf_tensor("wo", [128, 8192], F32))
            s4a = [sc_p4a.enter_context(nc.sbuf_tensor(f"s4a{j}", [128, 512], F32)) for j in range(2)]
            fnt = [sc_p4a.enter_context(nc.sbuf_tensor(f"fnt{j}", [128, 512], F32)) for j in range(2)]
            coo = cout_o[:].rearrange("r (c p t) -> r c p t", c=2, p=128)
            for k in range(16):
                K.dma("dS", oTf[k][:], coo[k // 2, k % 2], sem="p4o", wait=(tok_a2a_o,))
            tok_oT = K.tok("p4o")
            wo_tok = None
            wo_cons = None
            bank_tok4 = [None] * 4
            s4_free = [None, None]
            fnt_free = [None, None]
            for og in range(4):
                wldw = [wo_cons] if wo_cons else []
                K.dma("dS", wo[:], owT[og], sem="p4w", wait=wldw)
                wo_tok = K.tok("p4w")
                for tg in range(2):
                    for oc4 in range(4):
                        oc = og * 4 + oc4
                        K.wait("tensor", wo_tok, tok_oT, bank_tok4[oc4])
                        for kc in range(16):
                            K.t("matmul", pb[oc4][:, 0:512],
                                wo[:, kc * 512 + oc4 * 128:kc * 512 + (oc4 + 1) * 128],
                                oTf[kc][:, tg * 512:(tg + 1) * 512],
                                start=(kc == 0), stop=(kc == 15))
                        tmm4 = K.tok("T")
                        ssl = (tg * 4 + oc4) % 2
                        t_b4 = K.s("activation", s4a[ssl][:], pb[oc4][:, 0:512], AF.Identity,
                                   bias=outb_sb[:, oc:oc + 1], scale=1.0,
                                   wait=[tmm4, s4_free[ssl]])
                        bank_tok4[oc4] = t_b4
                        fnl = K.dma("dA", fnt[ssl][:], ffn_scr[oc][:, tg * 512:(tg + 1) * 512],
                                    sem=f"p4f{ssl}", wait=(tok_ffn_scr or []) + [fnt_free[ssl]])
                        t_a4 = K.v("tensor_add", ybuf[oc][:, tg * 512:(tg + 1) * 512],
                                   s4a[ssl][:], fnt[ssl][:], wait=[t_b4, fnl])
                        s4_free[ssl] = t_a4
                        fnt_free[ssl] = t_a4
                wo_cons = K.tok("T")
            tok_yb = K.tok("V")
            sc_p4a.close()
            K.barrier(dma_sems=["p4o", "p4w", "p4f0", "p4f1"])
            xrl = [sc_p4.enter_context(nc.sbuf_tensor(f"xrl{j}", [128, DM], F32)) for j in range(2)]
            yst = [sc_p4.enter_context(nc.sbuf_tensor(f"yst{j}", [128, DM], F32)) for j in range(2)]
            xld = [None] * NTILE
            for i in range(2):
                xld[i] = K.dma("dS", xrl[i][:], x_in[i * 128:(i + 1) * 128, :], sem=f"p4x{i % 2}")
            y_done_v = [None, None]
            for i in range(NTILE):
                bs4 = i % 2
                evs4 = []
                for bg in range(4):
                    bk = pb[4 + bg]
                    w = [tok_yb]
                    if i >= 1:
                        w.append(prev_evs4_last)
                    for sl in range(4):
                        f = bg * 4 + sl
                        K.t("transpose", bk[:, sl * 128:(sl + 1) * 128],
                            ybuf[f][:, i * 128:(i + 1) * 128], ident[:],
                            wait=w if sl == 0 else ())
                    tbk = K.tok("T")
                    evs4.append(K.v("tensor_add", yst[bs4][:, bg * 512:(bg + 1) * 512],
                                    bk[:, 0:512], xrl[bs4][:, bg * 512:(bg + 1) * 512],
                                    wait=[tbk, xld[i], y_done_v[bs4]]))
                prev_evs4_last = evs4[-1]
                tok_y_i = K.tok("V")
                ydma = K.dma("dA", y_out[i * 128:(i + 1) * 128, :], yst[bs4][:], sem=f"p4y{bs4}",
                             wait=(tok_y_i,))
                y_done_v[bs4] = ydma
                if i + 2 < NTILE:
                    xld[i + 2] = K.dma("dS", xrl[bs4][:], x_in[(i + 2) * 128:(i + 3) * 128, :],
                                       sem=f"p4x{bs4}", wait=(tok_y_i,))


        if stage < 5:
            zf = nc.alloc_sbuf_tensor("zf", [128, DM], F32)
            tz = K.v("memset", zf[:], 0.0, wait=(K.tok("V"), K.tok("SC"), K.tok("T")))
            for i in range(NTILE):
                K.dma("dS", y_out[i * 128:(i + 1) * 128, :], zf[:], sem="yw", wait=(tz,))
        for sname, cval in list(K.cnt.items()):
            if sname not in ("V", "SC", "T", "CC") and cval > 0:
                K.wait("sync", (sname, cval))
    return nc, dbg_out


def make_consts():
    ident = np.eye(128, dtype=np.float32)
    idx = np.arange(128)
    pc = np.array([bin(i & j).count("1") for i in idx for j in idx]).reshape(128, 128)
    h128 = ((-1.0) ** pc).astype(np.float32)
    i64 = np.arange(64)
    pc64 = np.array([bin(a & b).count("1") for a in i64 for b in i64]).reshape(64, 64)
    h64 = ((-1.0) ** pc64).astype(np.float32)
    bd64 = np.zeros((128, 128), np.float32)
    bd64[:64, :64] = h64
    bd64[64:, 64:] = h64
    p2 = np.zeros((2, 128), np.float32)
    p2[0, :64] = 1.0
    p2[1, 64:] = 1.0
    cmat = np.concatenate([ident, h128, bd64, p2], axis=0)
    masks = np.zeros((4, 128, 512), np.float32)
    for m_ in range(4):
        for p in range(128):
            masks[m_, p, m_ * 128 + p + 1:] = NEG
    return cmat, masks.reshape(512, 512)


def prep_inputs(inputs):
    x = np.ascontiguousarray(np.asarray(inputs["L0_attn_hidden_states"], np.float32).reshape(NT, DM))
    _wq = np.asarray(inputs["L0_attn_Wqkv_w"], np.float32).T
    wqkvT = np.ascontiguousarray(_wq.reshape(16, 128, 8, 512).transpose(2, 1, 0, 3)).reshape(8, 128, 8192)
    qkvb = np.ascontiguousarray(np.asarray(inputs["L0_attn_Wqkv_b"], np.float32).reshape(32, 128).T)
    _ow = np.asarray(inputs["L0_attn_out_w"], np.float32).T
    owT = np.ascontiguousarray(_ow.reshape(16, 128, 4, 512).transpose(2, 1, 0, 3)).reshape(4, 128, 8192)
    outb = np.ascontiguousarray(np.asarray(inputs["L0_attn_out_b"], np.float32).reshape(16, 128).T)
    lamv = np.concatenate([np.asarray(inputs["L0_attn_lambda_q1"], np.float32),
                           np.asarray(inputs["L0_attn_lambda_k1"], np.float32),
                           np.asarray(inputs["L0_attn_lambda_q2"], np.float32),
                           np.asarray(inputs["L0_attn_lambda_k2"], np.float32)]).reshape(1, 512)
    sublnw = np.asarray(inputs["L0_attn_subln_w"], np.float32).reshape(1, 256)
    wgT = np.asarray(inputs["L0_ffn_w_gate"], np.float32).T
    wuT = np.asarray(inputs["L0_ffn_w_up"], np.float32).T
    wdT = np.asarray(inputs["L0_ffn_w_down"], np.float32).T
    cmat, masks = make_consts()
    in_maps = []
    for cid in range(8):
        in_maps.append({
            "x": np.ascontiguousarray(x[cid * TOK:(cid + 1) * TOK]),
            "wqkvT": wqkvT,
            "qkv_b": qkvb,
            "owT": owT,
            "out_b": outb,
            "lamv": lamv,
            "sublnw": sublnw,
            "wgs": np.ascontiguousarray(wgT[:, cid * 1024:(cid + 1) * 1024]),
            "wus": np.ascontiguousarray(wuT[:, cid * 1024:(cid + 1) * 1024]),
            "wds": np.ascontiguousarray(wdT[:, cid * 256:(cid + 1) * 256]),
            "cmat": cmat,
            "masks": masks,
        })
    return in_maps


_NC_CACHE = {}


def _get_nc():
    if "nc" not in _NC_CACHE:
        _NC_CACHE["nc"] = build(stage=5, dbg=())[0]
    return _NC_CACHE["nc"]


def kernel(**inputs):
    in_maps = prep_inputs(inputs)
    nc = _get_nc()
    res = run_bass_kernel_spmd(nc, in_maps, core_ids=list(range(8)))
    y = np.concatenate([np.asarray(r["y"]) for r in res.results], axis=0)
    return np.ascontiguousarray(y.reshape(B, S, DM).astype(np.float32, copy=False))


# version 11
# speedup vs baseline: 1.0143x; 1.0143x over previous
import math
from contextlib import ExitStack
import numpy as np
import ml_dtypes
import concourse.bass as bass
import concourse.mybir as mybir
from concourse.bass_utils import run_bass_kernel_spmd
from concourse.aot_env import allow_ici_all_to_all

F32 = mybir.dt.float32
F16 = mybir.dt.float16
BF16 = mybir.dt.bfloat16
I32 = mybir.dt.int32
AF = mybir.ActivationFunctionType
ALU = mybir.AluOpType
AX = mybir.AxisListType

B, S, DM = 2, 4096, 2048
NT = B * S
TOK = NT // 8
NTILE = TOK // 128
QKVO = 4096
D = 128
E = 256
I_FF = 8192
LAMBDA_INIT = 0.8 - 0.6 * math.exp(-0.3)
SCALE = D ** -0.5
EPS = 1e-5
NEG = float(np.finfo(np.float32).min)
MAGIC = 12582912.0
TGF = 256
NGF = TOK // TGF


class Build:

    def __init__(self, nc):
        self.nc = nc
        self.sem = {}
        self.cnt = {}
        self.waited = {}
        for s in ["V", "SC", "T", "CC"]:
            self.sem[s] = nc.alloc_semaphore("sem_" + s)
            self.cnt[s] = 0
        for e in ["sync", "vector", "scalar", "tensor", "gpsimd"]:
            self.waited[e] = {}

    def dsem(self, name):
        if name not in self.sem:
            self.sem[name] = self.nc.alloc_semaphore("sem_" + name)
            self.cnt[name] = 0
        return name

    def tok(self, s):
        return (s, self.cnt[s])

    def _eng(self, name):
        return getattr(self.nc, name)

    def wait(self, engname, *tokens):
        w = self.waited[engname]
        for t in tokens:
            if t is None:
                continue
            s, v = t
            if v <= 0 or w.get(s, 0) >= v:
                continue
            self._eng(engname).wait_ge(self.sem[s], v)
            w[s] = v

    def bump(self, s, instr, k=1):
        instr.then_inc(self.sem[s], k)
        self.cnt[s] += k
        return (s, self.cnt[s])

    def dma(self, q, out, in_, sem=None, wait=()):
        eng = {"dS": "sync", "dA": "scalar", "dG": "gpsimd"}[q]
        s = self.dsem(sem if sem is not None else "d_" + q)
        self.wait(eng, *wait)
        return self.bump(s, self._eng(eng).dma_start(out=out, in_=in_), 16)

    def v(self, fn, *a, wait=(), **kw):
        self.wait("vector", *wait)
        return self.bump("V", getattr(self.nc.vector, fn)(*a, **kw))

    def s(self, fn, *a, wait=(), **kw):
        self.wait("scalar", *wait)
        return self.bump("SC", getattr(self.nc.scalar, fn)(*a, **kw))

    def t(self, fn, *a, wait=(), **kw):
        self.wait("tensor", *wait)
        return self.bump("T", getattr(self.nc.tensor, fn)(*a, **kw))

    def barrier(self, dma_sems=()):
        toks = [self.tok("V"), self.tok("SC"), self.tok("T")]
        toks += [(s, self.cnt[s]) for s in dma_sems if s in self.cnt]
        for e in ["sync", "vector", "scalar", "tensor"]:
            self.wait(e, *toks)

    def cc(self, kind, op, rg, ins, outs, wait=()):
        self.wait("gpsimd", *wait)
        return self.bump("CC", self.nc.gpsimd.collective_compute(
            kind, op, replica_groups=rg, ins=ins, outs=outs))


P0CUT = 99


def build(stage=9, dbg=(), dbg_qbmax=0):
    nc = bass.Bass("TRN2", target_bir_lowering=False, debug=False, num_devices=8)
    K = Build(nc)
    RG = [list(range(8))]

    x_in = nc.dram_tensor("x", [TOK, DM], F32, kind="ExternalInput").ap()
    wqkvT = nc.dram_tensor("wqkvT", [8, 128, 8192], F32, kind="ExternalInput").ap()
    qkv_b = nc.dram_tensor("qkv_b", [128, 32], F32, kind="ExternalInput").ap()
    owT = nc.dram_tensor("owT", [4, 128, 8192], F32, kind="ExternalInput").ap()
    out_b = nc.dram_tensor("out_b", [128, 16], F32, kind="ExternalInput").ap()
    lamv = nc.dram_tensor("lamv", [1, 512], F32, kind="ExternalInput").ap()
    sublnw = nc.dram_tensor("sublnw", [1, 256], F32, kind="ExternalInput").ap()
    wgs = nc.dram_tensor("wgs", [DM, 1024], F32, kind="ExternalInput").ap()
    wus = nc.dram_tensor("wus", [DM, 1024], F32, kind="ExternalInput").ap()
    wds = nc.dram_tensor("wds", [I_FF, 256], F32, kind="ExternalInput").ap()
    cmat = nc.dram_tensor("cmat", [386, 128], F32, kind="ExternalInput").ap()
    masks = nc.dram_tensor("masks", [512, 512], F32, kind="ExternalInput").ap()

    y_out = nc.dram_tensor("y", [TOK, DM], F32, kind="ExternalOutput").ap()
    dbg_out = {}
    for name, shape in dbg:
        dbg_out[name] = nc.dram_tensor(name, list(shape), F32, kind="ExternalOutput").ap()

    cin_mean = nc.dram_tensor("cin_mean", [1, 8], F32).ap()
    cout_mean = nc.dram_tensor("cout_mean", [1, 8], F32).ap()
    cin_wq = nc.dram_tensor("cin_wq", [3, DM * 1024], BF16).ap()
    cout_wq = nc.dram_tensor("cout_wq", [24, DM * 1024], BF16).ap()
    SH16 = 2 * 2 * 128 * TOK
    cin_q16 = nc.dram_tensor("cin_q16", [8, SH16], F16).ap()
    cout_q16 = nc.dram_tensor("cout_q16", [8, SH16], F16).ap()
    cin_k16 = nc.dram_tensor("cin_k16", [8, SH16], F16).ap()
    cout_k16 = nc.dram_tensor("cout_k16", [8, SH16], F16).ap()
    SH32 = 2 * 128 * TOK
    cin_v32 = nc.dram_tensor("cin_v32", [8, SH32], F32).ap()
    cout_v32 = nc.dram_tensor("cout_v32", [8, SH32], F32).ap()
    SHO = 2 * 128 * TOK
    cin_o = nc.dram_tensor("cin_o", [8, SHO], F32).ap()
    cout_o = nc.dram_tensor("cout_o", [8, SHO], F32).ap()
    xq_scr = nc.dram_tensor("xq_scr", [4, 128, 4096], BF16).ap()
    ffn_scr = nc.dram_tensor("ffn_scr", [16, 128, TOK], F32).ap()

    ident = nc.alloc_sbuf_tensor("ident", [128, 128], F32)
    identb = nc.alloc_sbuf_tensor("identb", [128, 128], BF16)
    h128 = nc.alloc_sbuf_tensor("h128", [128, 128], F32)
    bd64 = nc.alloc_sbuf_tensor("bd64", [128, 128], F32)
    p2c = nc.alloc_sbuf_tensor("p2c", [2, 128], F32)
    onerow = nc.alloc_sbuf_tensor("onerow", [1, 128], F32)
    qkvb_sb = nc.alloc_sbuf_tensor("qkvb_sb", [128, 32], F32)
    outb_sb = nc.alloc_sbuf_tensor("outb_sb", [128, 16], F32)
    lam_sb = nc.alloc_sbuf_tensor("lam_sb", [1, 512], F32)
    subln_sb = nc.alloc_sbuf_tensor("subln_sb", [1, 256], F32)
    subln_bc = nc.alloc_sbuf_tensor("subln_bc", [128, 256], F32)
    lam_col = nc.alloc_sbuf_tensor("lam_col", [128, 1], F32)
    nlam_col = nc.alloc_sbuf_tensor("nlam_col", [128, 1], F32)
    sm = nc.alloc_sbuf_tensor("sm", [128, 192], F32)
    sm2 = nc.alloc_sbuf_tensor("sm2", [128, 160], F32)
    rowbuf = nc.alloc_sbuf_tensor("rowbuf", [2, 512], F32)
    dq_t = nc.alloc_sbuf_tensor("dq_t", [128, 8], F32)
    dqg_row = nc.alloc_sbuf_tensor("dqg_row", [1, TOK], F32)
    dqu_row = nc.alloc_sbuf_tensor("dqu_row", [1, TOK], F32)
    dqd_row = nc.alloc_sbuf_tensor("dqd_row", [1, TOK], F32)
    mean_sb = nc.alloc_sbuf_tensor("mean_sb", [1, 8], F32)
    swcol = nc.alloc_sbuf_tensor("swcol", [128, 4], F32)

    sc_mask = ExitStack()
    mask_sb = sc_mask.enter_context(nc.sbuf_tensor("mask_sb", [128, 2048], F32))
    dqrow = sc_mask.enter_context(nc.sbuf_tensor("dqrow", [1, TOK], F32))
    pb = [nc.alloc_psum_tensor(f"pb{i}", [128, 512], F32) for i in range(8)]

    with nc.Block() as block, allow_ici_all_to_all():
        K.dma("dS", ident[:], cmat[0:128, :], sem="cst")
        K.dma("dS", h128[:], cmat[128:256, :], sem="cst")
        K.dma("dS", bd64[:], cmat[256:384, :], sem="cst")
        K.dma("dS", p2c[:], cmat[384:386, :], sem="cst")
        for _m in range(4):
            K.dma("dS", mask_sb[:, _m * 512:(_m + 1) * 512], masks[_m * 128:(_m + 1) * 128, :], sem="cst")
        K.dma("dS", qkvb_sb[:], qkv_b, sem="cst")
        K.dma("dS", outb_sb[:], out_b, sem="cst")
        K.dma("dS", lam_sb[:], lamv, sem="cst")
        K.dma("dS", subln_sb[:], sublnw, sem="cst")
        tok_const = K.tok("cst")

        t_ones = K.v("memset", onerow[:], 1.0)
        K.v("tensor_copy", identb[:], ident[:], wait=(tok_const,))
        tok_idc = K.tok("V")
        K.wait("tensor", tok_const, t_ones)
        t_mm = K.t("matmul", pb[6][:, 0:256], onerow[:], subln_sb[:], start=True, stop=True)
        t_sbc = K.v("tensor_copy", subln_bc[:], pb[6][:, 0:256], wait=(t_mm,))

        t1 = K.v("tensor_mul", rowbuf[0:1, 0:128], lam_sb[0:1, 0:128], lam_sb[0:1, 128:256], wait=(tok_const,))
        t2 = K.v("tensor_mul", rowbuf[0:1, 128:256], lam_sb[0:1, 256:384], lam_sb[0:1, 384:512])
        t3 = K.v("reduce_sum", sm2[0:1, 140:141], rowbuf[0:1, 0:128], axis=AX.X, wait=(t1,))
        t4 = K.v("reduce_sum", sm2[0:1, 141:142], rowbuf[0:1, 128:256], axis=AX.X, wait=(t2,))
        t5 = K.s("activation", sm2[0:1, 142:143], sm2[0:1, 140:141], AF.Exp, wait=(t3,))
        t6 = K.s("activation", sm2[0:1, 143:144], sm2[0:1, 141:142], AF.Exp, wait=(t4,))
        t7 = K.v("tensor_scalar", sm2[0:1, 138:139], sm2[0:1, 143:144], -1.0, LAMBDA_INIT,
                 ALU.mult, ALU.add, wait=(t5, t6))
        t8 = K.v("tensor_add", sm2[0:1, 139:140], sm2[0:1, 138:139], sm2[0:1, 142:143], wait=(t7,))
        K.wait("tensor", t8, t_sbc)
        t9 = K.t("matmul", pb[6][:, 256:257], onerow[:], sm2[0:1, 139:140], start=True, stop=True)
        K.v("tensor_copy", lam_col[:], pb[6][:, 256:257], wait=(t9,))
        K.v("tensor_scalar_mul", nlam_col[:], pb[6][:, 256:257], -1.0, wait=(K.tok("V"),))
        tok_lam = K.tok("V")
        tok_pb6_free = K.tok("V")

        sc_xnT = ExitStack()
        sc_p0 = ExitStack()
        xnT = [sc_xnT.enter_context(nc.sbuf_tensor(f"xnT{f}", [128, TOK], F32)) for f in range(16)]
        xqT = [sc_p0.enter_context(nc.sbuf_tensor(f"xqT{f}", [128, TOK], BF16)) for f in range(16)]
        xt = [sc_p0.enter_context(nc.sbuf_tensor(f"xt{j}", [128, DM], F32)) for j in range(2)]
        xc = [sc_p0.enter_context(nc.sbuf_tensor(f"xc{j}", [128, DM], F32)) for j in range(2)]
        xnb = [sc_p0.enter_context(nc.sbuf_tensor(f"xnb{j}", [128, DM], F32)) for j in range(2)]
        sqd = xt
        scrf = xc

        xdma = [None] * NTILE
        tile_vdone = [None] * NTILE
        tile_tdone = [None] * NTILE
        last_xq_ev = [None] * NTILE
        for i in range(2):
            xdma[i] = K.dma("dS", xt[i][:], x_in[i * 128:(i + 1) * 128, :], sem=f"x{i % 2}")
        for i in range(NTILE):
            bs_ = i % 2
            cb = 16 * i
            c = lambda j: sm[:, cb + j:cb + j + 1]
            K.v("reduce_sum", c(0), xt[bs_][:], axis=AX.X, wait=(xdma[i],))
            tm = K.v("tensor_scalar_mul", c(1), c(0), -1.0 / DM, wait=(K.tok("V"),))
            t_xc = K.s("activation", xc[bs_][:], xt[bs_][:], AF.Identity,
                       bias=c(1), scale=1.0, wait=(tm, xdma[i]))
            t_sq = K.s("activation", sqd[bs_][:], xc[bs_][:], AF.Square,
                       accum_out=c(2), wait=(t_xc,))
            tv = K.v("tensor_scalar", c(3), c(2), 1.0 / DM, EPS, ALU.mult, ALU.add, wait=(t_sq,))
            tu = K.s("activation", c(4), c(3), AF.Ln, wait=(tv,))
            tlast = K.s("activation", c(5), c(4), AF.Exp, scale=-0.5, wait=(tu,))
            rcur = 5
            for it in range(1):
                K.v("tensor_mul", c(6), c(rcur), c(rcur), wait=(tlast,))
                K.v("tensor_mul", c(6), c(6), c(3), wait=(K.tok("V"),))
                K.v("tensor_scalar", c(6), c(6), -0.5, 1.5, ALU.mult, ALU.add, wait=(K.tok("V"),))
                tlast = K.v("tensor_mul", c(7), c(rcur), c(6), wait=(K.tok("V"),))
                rcur = 7
            wprev = [tile_tdone[i - 2], tile_vdone[i - 2]] if i >= 2 else []
            t_xn = K.v("tensor_scalar_mul", xnb[bs_][:], xc[bs_][:], c(rcur),
                       wait=[tlast] + wprev)
            if P0CUT < 2:
                tile_vdone[i] = K.tok("V")
                tile_tdone[i] = K.tok("T")
                last_xq_ev[i] = None
                if i + 2 < NTILE:
                    xdma[i + 2] = K.dma("dS", xt[bs_][:], x_in[(i + 2) * 128:(i + 3) * 128, :],
                                        sem=f"x{bs_}", wait=(tile_vdone[i], K.tok("V")))
                continue
            K.v("tensor_reduce", c(8), xnb[bs_][:], axis=AX.X, op=ALU.max,
                apply_absolute_value=True, wait=(t_xn,))
            K.v("tensor_scalar_max", c(9), c(8), EPS, wait=(K.tok("V"),))
            K.v("reciprocal", c(10), c(9), wait=(K.tok("V"),))
            tsc2 = K.v("tensor_scalar_mul", c(11), c(10), 128.0, wait=(K.tok("V"),))
            K.v("tensor_scalar_mul", dq_t[:, i:i + 1], c(9), 1.0 / 128.0)
            tq1 = K.v("tensor_scalar_mul", scrf[bs_][:], xnb[bs_][:], c(11), wait=(tsc2, t_xn))
            tq2 = K.v("tensor_scalar_add", scrf[bs_][:], scrf[bs_][:], MAGIC, wait=(tq1,))
            tq4 = K.v("tensor_scalar", sqd[bs_][:], scrf[bs_][:], MAGIC, 127.0,
                      ALU.subtract, ALU.min, wait=(tq2, t_sq))
            tile_vdone[i] = K.tok("V")
            if P0CUT < 3:
                tile_tdone[i] = K.tok("T")
                last_xq_ev[i] = None
                if i + 2 < NTILE:
                    xdma[i + 2] = K.dma("dS", xt[bs_][:], x_in[(i + 2) * 128:(i + 3) * 128, :],
                                        sem=f"x{bs_}", wait=(tile_vdone[i], K.tok("V")))
                continue
            ev_xn = []
            for bg in range(4):
                bk = pb[4 + bg]
                w = [t_xn, tok_idc]
                if i >= 1:
                    w.append(last_xq_ev[i - 1])
                for sl in range(4):
                    f = bg * 4 + sl
                    K.t("transpose", bk[:, sl * 128:(sl + 1) * 128],
                        xnb[bs_][:, f * 128:(f + 1) * 128], ident[:],
                        wait=w if sl == 0 else ())
                tbk = K.tok("T")
                for sl in range(4):
                    f = bg * 4 + sl
                    ev_xn.append(K.v("tensor_copy", xnT[f][:, i * 128:(i + 1) * 128],
                                     bk[:, sl * 128:(sl + 1) * 128], wait=(tbk,)))
            ev_xq = []
            if P0CUT < 4:
                tile_tdone[i] = K.tok("T")
                last_xq_ev[i] = ev_xn[-1]
                if i + 2 < NTILE:
                    xdma[i + 2] = K.dma("dS", xt[bs_][:], x_in[(i + 2) * 128:(i + 3) * 128, :],
                                        sem=f"x{bs_}", wait=(tile_vdone[i], tile_tdone[i]))
                continue
            for bg in range(4):
                bk = pb[4 + bg]
                for sl in range(4):
                    f = bg * 4 + sl
                    K.t("transpose", bk[:, sl * 128:(sl + 1) * 128],
                        sqd[bs_][:, f * 128:(f + 1) * 128], ident[:],
                        wait=(tq4, ev_xn[bg * 4 + 3]) if sl == 0 else ())
                tbk = K.tok("T")
                for sl in range(4):
                    f = bg * 4 + sl
                    ev_xq.append(K.v("tensor_copy", xqT[f][:, i * 128:(i + 1) * 128],
                                     bk[:, sl * 128:(sl + 1) * 128], wait=(tbk,)))
            last_xq_ev[i] = ev_xq[-1]
            tile_tdone[i] = K.tok("T")
            if i + 2 < NTILE:
                xdma[i + 2] = K.dma("dS", xt[bs_][:], x_in[(i + 2) * 128:(i + 3) * 128, :],
                                    sem=f"x{bs_}", wait=(tile_vdone[i], tile_tdone[i]))
        K.wait("tensor", tok_pb6_free, K.tok("V"))
        for half in range(2 if P0CUT >= 5 else 0):
            for i4 in range(4):
                i = half * 4 + i4
                K.t("transpose", pb[6][0:1, i4 * 128:(i4 + 1) * 128], dq_t[:, i:i + 1], ident[:])
            ttr = K.tok("T")
            K.v("tensor_copy", dqrow[0:1, half * 512:(half + 1) * 512], pb[6][0:1, 0:512], wait=(ttr,))
            K.wait("tensor", K.tok("V"))
        tok_dqrow = K.tok("V")

        for f in range(16 if P0CUT >= 6 else 0):
            for g_ in range(4):
                K.dma("dA", xq_scr[:].rearrange("g p (k c) -> g p k c", k=16)[g_, :, f],
                      xqT[f][:, g_ * 256:(g_ + 1) * 256], sem="xqw", wait=(K.tok("V"),))
        tok_xq_scr = K.tok("xqw") if "xqw" in K.cnt else None

        tok_ag = None
        if stage >= 4:
            sc_pp = ExitStack()
            ptile = [sc_pp.enter_context(nc.sbuf_tensor(f"ptile{j}", [128, 1024], F32)) for j in range(2)]
            qtile = [sc_pp.enter_context(nc.sbuf_tensor(f"qtile{j}", [128, 1024], BF16)) for j in range(2)]
            wds_v = wds[:].rearrange("(a p) c -> a p c", p=128)
            wgs_v = wgs[:].rearrange("(a p) c -> a p c", p=128)
            wus_v = wus[:].rearrange("(a p) c -> a p c", p=128)
            ciwq_blk = cin_wq[:].rearrange("w (s p k c) -> w s p k c", s=4, p=128, k=16)

            pdma = [None, None]
            pcons = [None, None]
            pmm = [None, None]
            onepc = sm2[:, 33:34]
            t_one2 = K.v("memset", onepc, 1.0)
            for wi, (wv, ntl) in enumerate([(wgs_v, 16), (wus_v, 16), (wds_v, 64)]):
                for a in range(ntl):
                    j = (a + wi) % 2
                    w = [pcons[j]] if pcons[j] else []
                    srcv = wv[a] if ntl == 16 else wv[a][:, 0:256]
                    pdma[j] = K.dma("dG", ptile[j][:, 0:(1024 if ntl == 16 else 256)], srcv,
                                    sem=f"pp{j}", wait=w)
                    tr = K.v("tensor_reduce", sm2[:, 34 + j:35 + j], ptile[j][:, 0:(1024 if ntl == 16 else 256)],
                             axis=AX.X, op=ALU.add, apply_absolute_value=True,
                             wait=[pdma[j], pmm[j]])
                    pcons[j] = tr
                    K.wait("tensor", tr, t_one2)
                    pmm[j] = K.t("matmul", pb[0][0:1, wi:wi + 1], sm2[:, 34 + j:35 + j], onepc,
                                 start=(wi == 0 and a == 0), stop=(wi == 2 and a == ntl - 1),
                                 skip_group_check=True)
            t_sums = K.v("tensor_copy", mean_sb[0:1, 0:3], pb[0][0:1, 0:3], wait=(K.tok("T"),))
            tz = K.v("memset", mean_sb[0:1, 3:8], 0.0)
            tmdma = K.dma("dG", cin_mean, mean_sb[0:1, :], sem="ppm", wait=(t_sums, tz))
            cc_m = K.cc("AllReduce", ALU.add, RG, [cin_mean.opt()], [cout_mean.opt()],
                        wait=[tmdma])
            tok_ar = K.tok("CC")
            tmback = K.dma("dG", mean_sb[0:1, :], cout_mean, sem="ppm2", wait=(tok_ar,))
            NELT = float(DM * I_FF)
            t_m1 = K.v("tensor_scalar_mul", mean_sb[0:1, 4:7], mean_sb[0:1, 0:3], 1.0 / NELT, wait=(tmback,))
            t_m2 = K.v("tensor_scalar_max", mean_sb[0:1, 4:7], mean_sb[0:1, 4:7], EPS, wait=(K.tok("V"),))
            t_m3 = K.v("reciprocal", mean_sb[0:1, 0:3], mean_sb[0:1, 4:7], wait=(K.tok("V"),))
            K.wait("tensor", t_m3)
            t_bcm = K.t("matmul", pb[0][:, 0:3], onerow[:], mean_sb[0:1, 0:3], start=True, stop=True)
            t_swc = K.v("tensor_copy", swcol[:, 0:3], pb[0][:, 0:3], wait=(t_bcm,))
            t_cd = K.v("tensor_scalar_mul", mean_sb[0:1, 7:8], mean_sb[0:1, 6:7],
                       1.0 / (128.0 * math.sqrt(float(I_FF))), wait=(t_m2,))
            K.wait("tensor", t_cd, t_swc)
            t_bcd = K.t("matmul", pb[0][:, 4:5], onerow[:], mean_sb[0:1, 7:8], start=True, stop=True)
            t_cdc = K.v("tensor_copy", swcol[:, 3:4], pb[0][:, 4:5], wait=(t_bcd,))
            t_dg = K.v("tensor_scalar", dqg_row[0:1, :], dqrow[0:1, :], mean_sb[0:1, 4:5], None,
                       ALU.mult, wait=(tok_dqrow, t_m2))
            t_du = K.v("tensor_scalar", dqu_row[0:1, :], dqrow[0:1, :], mean_sb[0:1, 5:6], None,
                       ALU.mult, wait=(K.tok("V"),))
            tok_dqgu = K.tok("V")
            pdma = [None, None]
            pcons = [None, None]
            qfree = [None, None]
            for wi, (wv, ntl) in enumerate([(wgs_v, 16), (wus_v, 16), (wds_v, 64)]):
                for a in range(ntl):
                    j = (a + wi) % 2
                    ww = 1024 if ntl == 16 else 256
                    w = [pcons[j]] if pcons[j] else []
                    pdma[j] = K.dma("dG", ptile[j][:, 0:ww], wv[a] if ntl == 16 else wv[a][:, 0:256],
                                    sem=f"pq{j}", wait=w)
                    t1 = K.v("tensor_scalar", ptile[j][:, 0:ww], ptile[j][:, 0:ww],
                             swcol[:, wi:wi + 1], MAGIC, ALU.mult, ALU.add,
                             wait=(pdma[j], t_swc, qfree[j]))
                    t2 = K.v("tensor_scalar", ptile[j][:, 0:ww], ptile[j][:, 0:ww],
                             MAGIC, 1.0, ALU.subtract, ALU.min, wait=(K.tok("V"),))
                    t3 = K.v("tensor_scalar_max", qtile[j][:, 0:ww], ptile[j][:, 0:ww],
                             -1.0, wait=(K.tok("V"),))
                    pcons[j] = t3
                    if ntl == 16:
                        for sub in range(4):
                            K.dma("dA", ciwq_blk[wi, sub, :, a],
                                  qtile[j][:, sub * 256:(sub + 1) * 256], sem=f"pw{j}",
                                  wait=(t3,))
                        qfree[j] = K.tok(f"pw{j}")
                    else:
                        qfree[j] = K.dma("dA", ciwq_blk[2, a >> 4, :, a & 15],
                                         qtile[j][:, 0:256], sem=f"pw{j}", wait=(t3,))
            tok_pw = [K.tok("pw0"), K.tok("pw1")]
            cc_ag = K.cc("AllGather", ALU.bypass, RG, [cin_wq.opt()], [cout_wq.opt()],
                         wait=tok_pw)
            tok_ag = K.tok("CC")
            sc_pp.close()
            K.barrier(dma_sems=["pp0", "pp1", "pq0", "pq1", "pw0", "pw1", "ppm", "ppm2"])

        q16v = cin_q16[:].rearrange("r (s p t) -> r s p t", s=4, p=128)
        k16v = cin_k16[:].rearrange("r (s p t) -> r s p t", s=4, p=128)
        v32v = cin_v32[:].rearrange("r (s p t) -> r s p t", s=2, p=128)
        tok_p1 = None
        if stage >= 2:
            sc_p0.close()
            K.barrier(dma_sems=["xqw", "x0", "x1"])
            sc_p1 = ExitStack()
            wqb = [sc_p1.enter_context(nc.sbuf_tensor(f"wqb{j}", [128, 8192], F32))
                   for j in range(2)]
            stq = [sc_p1.enter_context(nc.sbuf_tensor(f"stq{j}", [128, 512], F32)) for j in range(2)]
            sthi = [sc_p1.enter_context(nc.sbuf_tensor(f"sthi{j}", [128, 512], F16)) for j in range(2)]
            stlo = [sc_p1.enter_context(nc.sbuf_tensor(f"stlo{j}", [128, 512], F16)) for j in range(2)]

            wq_tok = [None, None]
            wq_cons = [None, None]

            def load_wq(og):
                j = og % 2
                w = [wq_cons[j]] if wq_cons[j] else []
                K.dma("dS", wqb[j][:, 0:4096], wqkvT[og][:, 0:4096], sem=f"wq{j}", wait=w)
                K.dma("dA", wqb[j][:, 4096:8192], wqkvT[og][:, 4096:8192], sem=f"wqa{j}", wait=w)
                wq_tok[j] = [K.tok(f"wq{j}"), K.tok(f"wqa{j}")]

            load_wq(4)
            load_wq(5)
            bank_tok = [None] * 4
            slot_q_free = [None, None]
            slot_h_free = [None, None]
            slot_l_free = [None, None]
            m = 0
            tok_a2a_kv_holder = [None]
            og_order = [4, 5, 6, 7, 0, 1, 2, 3]
            for og in og_order:
                j = og % 2
                for tg in range(2):
                    for oc4 in range(4):
                        oc = og * 4 + oc4
                        bk = pb[oc4]
                        K.wait("tensor", *(wq_tok[j] + [bank_tok[oc4]]))
                        for kc in range(16):
                            K.t("matmul", bk[:, 0:512],
                                wqb[j][:, kc * 512 + oc4 * 128:kc * 512 + (oc4 + 1) * 128],
                                xnT[kc][:, tg * 512:(tg + 1) * 512],
                                start=(kc == 0), stop=(kc == 15))
                        tmm = K.tok("T")
                        sq_ = m % 2
                        t_qf = K.s("activation", stq[sq_][:], bk[:, 0:512], AF.Identity,
                                   bias=qkvb_sb[:, oc:oc + 1], scale=1.0,
                                   wait=(tmm, slot_q_free[sq_]))
                        bank_tok[oc4] = t_qf
                        if oc < 24:
                            t_hi = K.s("activation", sthi[sq_][:], stq[sq_][:], AF.Identity,
                                       bias=0.0, scale=1.0, wait=(t_qf, slot_h_free[sq_]))
                            t_lo = K.v("tensor_sub", stlo[sq_][:], stq[sq_][:], sthi[sq_][:],
                                       wait=(t_hi, slot_l_free[sq_]))
                            slot_q_free[sq_] = t_lo
                            if oc < 16:
                                dsts = [oc >> 1]
                                dstv = q16v
                                pre = "q"
                            else:
                                g = (oc - 16) >> 1
                                dsts = [2 * g, 2 * g + 1]
                                dstv = k16v
                                pre = "k"
                            sec = 0 + (oc & 1)
                            secl = 2 + (oc & 1)
                            for dj in dsts:
                                K.dma("dA", dstv[dj, sec, :, tg * 512:(tg + 1) * 512],
                                      sthi[sq_][:], sem=f"{pre}h{sq_}", wait=(t_hi,))
                                K.dma("dA", dstv[dj, secl, :, tg * 512:(tg + 1) * 512],
                                      stlo[sq_][:], sem=f"{pre}l{sq_}", wait=(t_lo,))
                            slot_h_free[sq_] = K.tok(f"{pre}h{sq_}")
                            slot_l_free[sq_] = K.tok(f"{pre}l{sq_}")
                        else:
                            g = (oc - 24) >> 1
                            half = (oc - 24) & 1
                            for dj in (2 * g, 2 * g + 1):
                                K.dma("dA", v32v[dj, half, :, tg * 512:(tg + 1) * 512],
                                      stq[sq_][:], sem=f"qv{sq_}", wait=(t_qf,))
                            slot_q_free[sq_] = K.tok(f"qv{sq_}")
                        m += 1
                oi = og_order.index(og)
                if oi == 3:
                    kv_toks = [K.tok(s) for s in list(K.cnt) if s.startswith(("kh", "kl", "qv"))]
                    cck = K.cc("AllToAll", ALU.bypass, RG, [cin_k16.opt()], [cout_k16.opt()],
                               wait=kv_toks)
                    ccv = K.cc("AllToAll", ALU.bypass, RG, [cin_v32.opt()], [cout_v32.opt()])
                    tok_a2a_kv_holder[0] = K.tok("CC")
                if oi + 2 < 8:
                    wq_cons[og_order[oi + 2] % 2] = K.tok("T")
                    load_wq(og_order[oi + 2])
            stage_toks = [K.tok(s) for s in list(K.cnt)
                          if s.startswith(("qh", "ql"))]
            tok_p1 = stage_toks
            sc_p1.close()
            sc_xnT.close()
            K.barrier(dma_sems=[s for s in K.cnt if s.startswith(("qh", "ql", "kh", "kl", "qv", "wq"))])

            if stage >= 2:
                ccq = K.cc("AllToAll", ALU.bypass, RG, [cin_q16.opt()], [cout_q16.opt()],
                           wait=stage_toks)
                tok_a2a_qkv = K.tok("CC")

        tok_a2a_o = None
        if stage >= 3:
            sc_p2 = ExitStack()
            qh = [sc_p2.enter_context(nc.sbuf_tensor(f"qh{c}", [128, S], F16)) for c in range(2)]
            ql = [sc_p2.enter_context(nc.sbuf_tensor(f"ql{c}", [128, S], F16)) for c in range(2)]
            kh = [sc_p2.enter_context(nc.sbuf_tensor(f"kh{c}", [128, S], F16)) for c in range(2)]
            kl = [sc_p2.enter_context(nc.sbuf_tensor(f"kl{c}", [128, S], F16)) for c in range(2)]
            vT = sc_p2.enter_context(nc.sbuf_tensor("vT", [128, 32 * 256], F32))
            s1b = sc_p2.enter_context(nc.sbuf_tensor("s1b", [128, S], F32))
            s2b = sc_p2.enter_context(nc.sbuf_tensor("s2b", [128, S], F32))
            s1c = sc_p2.enter_context(nc.sbuf_tensor("s1c", [128, S], F32))
            s2c = sc_p2.enter_context(nc.sbuf_tensor("s2c", [128, S], F32))
            vtmp = [s1b[:, 0:1024], s1b[:, 1024:2048]]
            ptT = [sc_p2.enter_context(nc.sbuf_tensor(f"ptT{j}", [128, 512], F32)) for j in range(2)]
            oS = [sc_p2.enter_context(nc.sbuf_tensor(f"oS{j}", [128, 512], F32)) for j in range(2)]
            osc4 = sc_p2.enter_context(nc.sbuf_tensor("osc4", [128, 1024], F32))
            osc = sc_p2.enter_context(nc.sbuf_tensor("osc", [128, 256], F32))
            co16 = cout_q16[:].rearrange("r (s p t) -> r s p t", s=4, p=128)
            ck16 = cout_k16[:].rearrange("r (s p t) -> r s p t", s=4, p=128)
            co32 = cout_v32[:].rearrange("r (s p t) -> r s p t", s=2, p=128)
            cino = cin_o[:].rearrange("r (c p t) -> r c p t", c=2, p=128)

            u_done = []
            u0_osh = None
            vtr_done = [None, None]
            vtr_prev_ev = None
            for u in range(2):
                w_ld = [tok_a2a_qkv] + u_done
                if u == 1:
                    w_ld += u0_osh
                for c in range(2):
                    for r4 in range(4):
                        r = 4 * u + r4
                        K.dma("dS", qh[c][:, r4 * 1024:(r4 + 1) * 1024], co16[r, 0 + c], sem=f"aq{u}", wait=w_ld)
                        K.dma("dS", ql[c][:, r4 * 1024:(r4 + 1) * 1024], co16[r, 2 + c], sem=f"aq{u}", wait=w_ld)
                        K.dma("dS", kh[c][:, r4 * 1024:(r4 + 1) * 1024], ck16[r, 0 + c], sem=f"aq{u}", wait=w_ld)
                        K.dma("dS", kl[c][:, r4 * 1024:(r4 + 1) * 1024], ck16[r, 2 + c], sem=f"aq{u}", wait=w_ld)
                tok_aq = K.tok(f"aq{u}")
                for half in range(2):
                    for r4 in range(4):
                        r = 4 * u + r4
                        jj = (half * 4 + r4) % 2
                        wv = w_ld + ([vtr_done[jj]] if vtr_done[jj] is not None else [])
                        vdma = K.dma("dS", vtmp[jj], co32[r, half], sem=f"av{u}_{half}_{r4}", wait=wv)
                        for grp in range(2):
                            wtr = [vdma, tok_idc] + u_done
                            if vtr_prev_ev is not None:
                                wtr.append(vtr_prev_ev)
                            for sl in range(4):
                                t8 = grp * 4 + sl
                                K.t("transpose", pb[7][:, sl * 128:(sl + 1) * 128],
                                    s1b[:, jj * 1024 + t8 * 128: jj * 1024 + (t8 + 1) * 128],
                                    ident[:], wait=wtr if sl == 0 else ())
                            tbk = K.tok("T")
                            kb0 = r4 * 8 + grp * 4
                            dst = vT[:].rearrange("p (k e) -> p k e", k=32)[
                                :, kb0:kb0 + 4, half * 128:half * 128 + 128]
                            vtr_prev_ev = K.v("tensor_copy", dst, pb[7][:, 0:512].rearrange(
                                "p (k e) -> p k e", k=4), wait=(tbk,))
                        vtr_done[jj] = K.tok("T")
                tok_vT = K.tok("V")

                tok_bounds = K.tok("V")

                prev_ptT_cons = [None, None]
                prev_ptT_ev = [None, None]
                prev_s_cons = [None, None]
                prev_sq = None
                prev_o_ev = None
                osc4_free = None
                SB1 = [s1b, s1c]
                SB2 = [s2b, s2c]
                state = [None, None]
                for qbi in range(33):
                    if qbi >= 1:
                        qb = qbi - 1
                        par = qb % 2
                        st = state[par]
                        cq = lambda j: sm2[:, 96 + 32 * par + j:96 + 32 * par + j + 1]
                        nkb = qb // 4 + 1
                        ncols = nkb * 512
                        sA, sB = SB1[par], SB2[par]
                        t_e1 = K.s("activation", sA[:, 0:ncols], sA[:, 0:ncols], AF.Exp,
                                   bias=cq(14), scale=1.0, accum_out=cq(1),
                                   wait=[st["ev1"], st["tmx"]])
                        t_e2 = K.s("activation", sB[:, 0:ncols], sB[:, 0:ncols], AF.Exp,
                                   bias=cq(15), scale=1.0, accum_out=cq(2), wait=(t_e1,))
                        t_a = K.v("reciprocal", cq(3), cq(1), wait=(t_e1,))
                        t_c0 = K.v("tensor_scalar_mul", sA[:, 0:ncols], sA[:, 0:ncols],
                                   cq(3), wait=(t_a,))
                        t_b0 = K.v("reciprocal", cq(4), cq(2), wait=(t_e2,))
                        t_b = K.v("tensor_mul", cq(5), cq(4), nlam_col[:], wait=(t_b0, tok_lam))
                        t_c2 = K.v("scalar_tensor_tensor", sA[:, 0:ncols], sB[:, 0:ncols],
                                   cq(5), sA[:, 0:ncols], ALU.mult, ALU.add,
                                   wait=(t_b, t_c0))
                        state[par]["c2"] = t_c2
                    if qbi < 32:
                        qb = qbi
                        par = qb % 2
                        cq = lambda j: sm2[:, 96 + 32 * par + j:96 + 32 * par + j + 1]
                        nkb = qb // 4 + 1
                        diag = qb // 4
                        sA, sB = SB1[par], SB2[par]
                        ev1 = [None] * nkb
                        ev2 = [None] * nkb
                        tmx = None
                        for kb in range(nkb):
                            b1 = pb[kb % 2]
                            b2 = pb[2 + kb % 2]
                            wmm = [tok_aq]
                            if prev_s_cons[par] is not None:
                                wmm.append(prev_s_cons[par])
                            if kb >= 2:
                                wmm += [ev1[kb - 2], ev2[kb - 2]]
                            elif qbi >= 1:
                                wmm += [prev_ev1_last, prev_ev2_last]
                            K.wait("tensor", *wmm)
                            for (lh, rh, st_, sp_) in ((qh, kh, True, False), (qh, kl, False, False), (ql, kh, False, True)):
                                K.t("matmul", b1[:, 0:512], lh[0][:, qb * 128:(qb + 1) * 128],
                                    rh[0][:, kb * 512:(kb + 1) * 512], start=st_, stop=sp_)
                            for (lh, rh, st_, sp_) in ((qh, kh, True, False), (qh, kl, False, False), (ql, kh, False, True)):
                                K.t("matmul", b2[:, 0:512], lh[1][:, qb * 128:(qb + 1) * 128],
                                    rh[1][:, kb * 512:(kb + 1) * 512], start=st_, stop=sp_)
                            tmm = K.tok("T")
                            if kb == diag:
                                msl = mask_sb[:, (qb % 4) * 512:(qb % 4 + 1) * 512]
                                ev1[kb] = K.v("scalar_tensor_tensor", sA[:, kb * 512:(kb + 1) * 512],
                                              b1[:, 0:512], SCALE, msl, ALU.mult, ALU.add,
                                              wait=(tmm,))
                                ev2[kb] = K.v("scalar_tensor_tensor", sB[:, kb * 512:(kb + 1) * 512],
                                              b2[:, 0:512], SCALE, msl, ALU.mult, ALU.add,
                                              wait=(tmm,))
                            else:
                                ev1[kb] = K.s("activation", sA[:, kb * 512:(kb + 1) * 512], b1[:, 0:512],
                                              AF.Identity, bias=0.0, scale=SCALE, wait=(tmm,))
                                ev2[kb] = K.v("tensor_scalar_mul", sB[:, kb * 512:(kb + 1) * 512],
                                              b2[:, 0:512], SCALE, wait=(tmm,))
                            K.v("reduce_max", cq(16 + (kb % 2)), sA[:, kb * 512:(kb + 1) * 512],
                                axis=AX.X, negate=True, wait=(ev1[kb],))
                            if kb == 0:
                                K.v("tensor_copy", cq(14), cq(16), wait=(K.tok("V"), K.tok("SC")))
                            else:
                                K.v("tensor_tensor", cq(14), cq(14), cq(16 + (kb % 2)), ALU.min, wait=(K.tok("V"),))
                            K.v("reduce_max", cq(18 + (kb % 2)), sB[:, kb * 512:(kb + 1) * 512],
                                axis=AX.X, negate=True, wait=(ev2[kb],))
                            if kb == 0:
                                tmx = K.v("tensor_copy", cq(15), cq(18), wait=(K.tok("V"), K.tok("SC")))
                            else:
                                tmx = K.v("tensor_tensor", cq(15), cq(15), cq(18 + (kb % 2)), ALU.min, wait=(K.tok("V"),))
                        prev_ev1_last, prev_ev2_last = ev1[-1], ev2[-1]
                        state[par] = {"ev1": ev1[-1], "ev2": ev2[-1], "tmx": tmx}
                    if qbi >= 1:
                        qb = qbi - 1
                        par = qb % 2
                        cq = lambda j: sm2[:, 96 + 32 * par + j:96 + 32 * par + j + 1]
                        sA = SB1[par]
                        t_c2 = state[par]["c2"]
                        n128 = qb + 1
                        for g4 in range(0, n128, 4):
                            cnt4 = min(4, n128 - g4)
                            sl_ = (g4 // 4) % 2
                            bkt = pb[4 + sl_]
                            wtr = [t_c2, prev_ptT_ev[sl_]]
                            for j in range(cnt4):
                                kb1 = g4 + j
                                K.t("transpose", bkt[:, j * 128:(j + 1) * 128],
                                    sA[:, kb1 * 128:(kb1 + 1) * 128], ident[:],
                                    wait=wtr if j == 0 else ())
                            trs_t = K.tok("T")
                            t_ev = K.v("tensor_copy", ptT[sl_][:, 0:cnt4 * 128], bkt[:, 0:cnt4 * 128],
                                       wait=[trs_t, prev_ptT_cons[sl_]])
                            prev_ptT_ev[sl_] = t_ev
                            K.wait("tensor", t_ev, tok_vT, prev_sq)
                            for j in range(cnt4):
                                kb1 = g4 + j
                                K.t("matmul", pb[6][:, 0:256], ptT[sl_][:, j * 128:(j + 1) * 128],
                                    vT[:, kb1 * 256:(kb1 + 1) * 256],
                                    start=(kb1 == 0), stop=(kb1 == n128 - 1))
                            prev_ptT_cons[sl_] = K.tok("T")
                        t_pv = K.tok("T")
                        prev_s_cons[par] = t_pv
                        t_sq = K.s("activation", osc[:], pb[6][:, 0:256], AF.Square,
                                   accum_out=cq(6), wait=[t_pv, prev_o_ev])
                        prev_sq = t_sq
                        tv2 = K.v("tensor_scalar", cq(7), cq(6), 1.0 / 256.0, EPS,
                                  ALU.mult, ALU.add, wait=(t_sq,))
                        tu2 = K.s("activation", cq(8), cq(7), AF.Ln, wait=(tv2,))
                        tl2 = K.s("activation", cq(9), cq(8), AF.Exp, scale=-0.5, wait=(tu2,))
                        t_o1 = K.v("tensor_scalar", osc4[:, (qb % 4) * 256:(qb % 4 + 1) * 256],
                                   pb[6][:, 0:256], cq(9), 1.0 - LAMBDA_INIT,
                                   ALU.mult, ALU.mult, wait=[tl2, t_sq, osc4_free])
                        t_o2 = K.v("tensor_mul", osc4[:, (qb % 4) * 256:(qb % 4 + 1) * 256],
                                   osc4[:, (qb % 4) * 256:(qb % 4 + 1) * 256], subln_bc[:],
                                   wait=(K.tok("V"),))
                        if qb % 4 == 3:
                            qb0_ = qb - 3
                            K.wait("tensor", t_o2, prev_o_ev)
                            for half4 in range(2):
                                if half4 == 1:
                                    K.wait("tensor", prev_o_ev)
                                for j4 in range(4):
                                    qq = qb0_ + (half4 * 4 + j4) // 2
                                    cch = (half4 * 4 + j4) % 2
                                    K.t("transpose", pb[7][:, j4 * 128:(j4 + 1) * 128],
                                        osc4[:, (qq % 4) * 256 + cch * 128:(qq % 4) * 256 + (cch + 1) * 128],
                                        ident[:])
                                t_tro = K.tok("T")
                                slo_ = half4
                                t_oev2 = K.v("tensor_copy", oS[slo_][:, 0:512], pb[7][:, 0:512],
                                             wait=[t_tro, (f"osh{u}_{slo_}", K.cnt.get(f"osh{u}_{slo_}", 0))])
                                prev_o_ev = t_oev2
                                r4o = qb0_ // 8
                                for j4 in range(4):
                                    qq = qb0_ + (half4 * 4 + j4) // 2
                                    cch = (half4 * 4 + j4) % 2
                                    K.dma("dA", cino[4 * u + r4o, cch][:, (qq % 8) * 128:(qq % 8 + 1) * 128],
                                          oS[slo_][:, j4 * 128:(j4 + 1) * 128], sem=f"osh{u}_{slo_}",
                                          wait=(t_oev2,))
                            osc4_free = K.tok("T")
                if u == 0:
                    u0_osh = [K.tok("osh0_0"), K.tok("osh0_1")]
                    u_done = [K.tok("V"), K.tok("SC"), K.tok("T")]
                    prev_ptT_cons = [None, None]
            tok_osh1 = [K.tok("osh1_0"), K.tok("osh1_1")]
            cc_o = K.cc("AllToAll", ALU.bypass, RG, [cin_o.opt()], [cout_o.opt()],
                        wait=u0_osh + tok_osh1)
            tok_a2a_o = K.tok("CC")
            sc_p2.close()
            sc_mask.close()
            K.barrier(dma_sems=[sname for sname in K.cnt if sname.startswith(("aq", "av", "osh"))])

        if "d_oT" in dbg_out and stage >= 3:
            coo = cout_o[:].rearrange("r (c p t) -> r c p t", c=2, p=128)
            tmpo = nc.alloc_sbuf_tensor("tmpo", [128, TOK], F32)
            for r in range(8):
                for cch in range(2):
                    tld = K.dma("dS", tmpo[:], coo[r, cch], sem="dbo",
                                wait=[tok_a2a_o, ("dbo2", K.cnt.get("dbo2", 0))])
                    K.dma("dS", dbg_out["d_oT"][(r * 2 + cch) * 128:(r * 2 + cch + 1) * 128, :],
                          tmpo[:], sem="dbo2", wait=(tld,))

        if "d_xnT" in dbg_out:
            for f in range(16):
                K.dma("dS", dbg_out["d_xnT"][f * 128:(f + 1) * 128, :], xnT[f][:],
                      sem="dbg1", wait=(K.tok("V"),))
        if "d_xq" in dbg_out:
            xqf = xnb
            for f in range(16):
                tcp = K.v("tensor_copy", xqf[f % 2][:, 0:TOK], xqT[f][:],
                          wait=(K.tok("V"), (f"dbg2_{f % 2}", 16 * (f // 2)) if f >= 2 else None))
                K.dma("dS", dbg_out["d_xq"][f * 128:(f + 1) * 128, :], xqf[f % 2][:, 0:TOK],
                      sem=f"dbg2_{f % 2}", wait=(tcp,))
        if "d_dq" in dbg_out:
            K.dma("dS", dbg_out["d_dq"][0:1, :], dqrow[0:1, :], sem="dbg3", wait=(tok_dqrow,))

        if "d_q16" in dbg_out and stage >= 2:
            tmpf = nc.alloc_sbuf_tensor("tmpf", [128, NT], F32)
            tmph = nc.alloc_sbuf_tensor("tmph", [128, NT], F16)
            tmpl = nc.alloc_sbuf_tensor("tmpl", [128, NT], F16)
            co16 = cout_q16[:].rearrange("r (s p t) -> r s p t", s=8, p=128)
            co32 = cout_v32[:].rearrange("r (s p t) -> r s p t", s=2, p=128)
            for sec in range(4):
                vv_ = [co16, co16, ck16, ck16][sec]
                shi = [0, 1, 0, 1][sec]
                slo = [2, 3, 2, 3][sec]
                for r in range(8):
                    K.dma("dS", tmph[:, r * 1024:(r + 1) * 1024], vv_[r, shi], sem="dbh",
                          wait=(tok_a2a_qkv, K.tok("V")))
                    K.dma("dS", tmpl[:, r * 1024:(r + 1) * 1024], vv_[r, slo], sem="dbh")
                tld = K.tok("dbh")
                tadd = K.v("tensor_add", tmpf[:], tmph[:], tmpl[:],
                           wait=(tld, ("dbq", K.cnt.get("dbq", 0))))
                K.dma("dS", dbg_out["d_q16"][sec * 128:(sec + 1) * 128, :], tmpf[:],
                      sem="dbq", wait=(tadd,))
            for half in range(2):
                for r in range(8):
                    K.dma("dS", tmpf[:, r * 1024:(r + 1) * 1024], co32[r, half], sem="dbv",
                          wait=(tok_a2a_qkv, ("dbq", K.cnt.get("dbq", 0)), K.tok("V")))
                tld = K.tok("dbv")
                K.dma("dS", dbg_out["d_q16"][(4 + half) * 128:(5 + half) * 128, :], tmpf[:],
                      sem="dbq", wait=(tld,))

        tok_ffn_scr = None
        if stage >= 4:
            if stage < 90 and not (stage >= 3):
                pass
            if not (stage >= 3):
                sc_mask.close()
            sc_p3 = ExitStack()
            W2p = sc_p3.enter_context(nc.sbuf_tensor("W2p", [128, 16384], F32))
            Zb = sc_p3.enter_context(nc.sbuf_tensor("Zb", [128, 16384], F32))
            W2qp = Zb[:].bitcast(BF16)
            wstr = [sc_p3.enter_context(nc.sbuf_tensor(f"wstr{j}", [128, 4096], BF16)) for j in range(4)]
            xqs = sc_p3.enter_context(nc.sbuf_tensor("xqs", [128, 4096], BF16))
            itile = [sc_p3.enter_context(nc.sbuf_tensor(f"it{j}", [128, 256], F32)) for j in range(4)]
            utile = [sc_p3.enter_context(nc.sbuf_tensor(f"ut{j}", [128, 256], F32)) for j in range(2)]
            sgt = sc_p3.enter_context(nc.sbuf_tensor("sgt", [128, 256], F32))
            sut = sc_p3.enter_context(nc.sbuf_tensor("sut", [128, 256], F32))
            sdt = sc_p3.enter_context(nc.sbuf_tensor("sdt", [128, 256], F32))
            scl = sc_p3.enter_context(nc.sbuf_tensor("scl", [128, 128], F32))
            zti = [sc_p3.enter_context(nc.sbuf_tensor(f"zti{j}", [128, 512], F32)) for j in range(2)]
            fout = [sc_p3.enter_context(nc.sbuf_tensor(f"fout{j}", [128, 256], F32)) for j in range(2)]
            cowq_blk = cout_wq[:].rearrange("w (s p f) -> w s p f", s=4, p=128)
            xqv = xq_scr[:]

            gu_cons = [None, None]
            dn_cons = None
            fg_done = None
            pb03_free = None
            it_free = [None] * 4
            ut_free = [None] * 2
            fo_free = [None] * 2
            prev_gbc = None
            st2_zti_cons = [None, None]
            for g in range(NGF):
                K.dma("dS", xqs[:], xq_scr[g], sem="fxq",
                      wait=[tok_xq_scr] + ([fg_done] if fg_done else []))
                tok_xqs = K.tok("fxq")
                K.wait("tensor", tok_dqgu, prev_gbc)
                t_g1 = K.t("matmul", pb[6][:, 0:256], onerow[:], dqg_row[0:1, g * 256:(g + 1) * 256],
                           start=True, stop=True)
                t_g2 = K.t("matmul", pb[7][:, 0:256], onerow[:], dqu_row[0:1, g * 256:(g + 1) * 256],
                           start=True, stop=True)
                t_gb = K.v("tensor_copy", sgt[:], pb[6][:, 0:256], wait=[t_g1, prev_gbc])
                t_ub = K.v("tensor_copy", sut[:], pb[7][:, 0:256], wait=(t_g2,))
                prev_gbc = K.tok("V")
                st1_sl = 0
                st1_ev_last = None
                st1_bank_tok = [None, None]
                for ob in range(32):
                    jj = ob % 2
                    r_ = ob >> 2
                    sub = ob & 3
                    wld = [tok_ag] + ([gu_cons[jj]] if gu_cons[jj] else []) \
                          + ([dn_cons] if (ob < 2 and dn_cons) else [])
                    K.dma("dS", wstr[jj][:], cowq_blk[3 * r_, sub], sem=f"fg{jj}", wait=wld)
                    K.dma("dA", wstr[2 + jj][:], cowq_blk[3 * r_ + 1, sub], sem=f"fu{jj}", wait=wld)
                    tok_wg = K.tok(f"fg{jj}")
                    tok_wu = K.tok(f"fu{jj}")
                    if "d_ops" in dbg_out and g == 0 and ob == 0:
                        tmpo1 = W2p[:, 0:4096]
                        tc1 = K.v("tensor_copy", tmpo1, xqs[:], wait=(tok_xqs,))
                        K.dma("dS", dbg_out["d_ops"][0:128, :], tmpo1, sem="dbo1", wait=(tc1,))
                        tc2 = K.v("tensor_copy", tmpo1, wstr[jj][:], wait=(tok_wg, ("dbo1", 16)))
                        K.dma("dS", dbg_out["d_ops"][128:256, :], tmpo1, sem="dbo1", wait=(tc2,))
                        K.wait("vector", ("dbo1", 32))
                    K.wait("tensor", tok_wg, tok_wu, tok_xqs, pb03_free)
                    for oc2 in range(2):
                        for kc in range(16):
                            K.t("matmul", pb[oc2][:, 0:256],
                                wstr[jj][:, kc * 256 + oc2 * 128: kc * 256 + oc2 * 128 + 128],
                                xqs[:, kc * 256:(kc + 1) * 256],
                                start=(kc == 0), stop=(kc == 15))
                        for kc in range(16):
                            K.t("matmul", pb[2 + oc2][:, 0:256],
                                wstr[2 + jj][:, kc * 256 + oc2 * 128: kc * 256 + oc2 * 128 + 128],
                                xqs[:, kc * 256:(kc + 1) * 256],
                                start=(kc == 0), stop=(kc == 15))
                    gu_cons[jj] = K.tok("T")
                    tmmgu = gu_cons[jj]
                    for oc2 in range(2):
                        a_ = ob * 2 + oc2
                        isl = a_ % 4
                        usl = a_ % 2
                        t_ga = K.v("tensor_mul", itile[isl][:], pb[oc2][:, 0:256], sgt[:],
                                   wait=[tmmgu, t_gb, it_free[isl]])
                        if "d_g0" in dbg_out and g == 0 and ob == 0 and oc2 == 0:
                            K.dma("dS", dbg_out["d_g0"][0:128, :], itile[isl][:], sem="dbg0", wait=(t_ga,))
                            K.wait("scalar", ("dbg0", 16))
                        t_si = K.s("activation", itile[isl][:], itile[isl][:], AF.Silu, wait=(t_ga,))
                        t_uv = K.v("tensor_mul", utile[usl][:], pb[2 + oc2][:, 0:256], sut[:],
                                   wait=[tmmgu, t_ub, ut_free[usl]])
                        t_iv = K.v("tensor_mul", itile[isl][:], itile[isl][:], utile[usl][:],
                                   wait=(t_si, t_uv))
                        ut_free[usl] = t_iv
                        sb_ = 4 + (a_ % 2)
                        K.wait("tensor", t_iv, st1_bank_tok[a_ % 2])
                        t_s1 = K.t("matmul", pb[sb_][:, 0:256], h128[:], itile[isl][:],
                                   start=True, stop=True)
                        it_free[isl] = t_s1
                        if "d_int" in dbg_out and g == 0:
                            dtok = K.dma("dS", dbg_out["d_int"][a_ * 128:(a_ + 1) * 128, :],
                                         itile[isl][:], sem=f"dbi{isl}", wait=(t_iv,))
                            it_free[isl] = dtok
                        st1_ev_last = K.v("tensor_copy", Zb[:, a_::64],
                                          pb[sb_][:, 0:256], wait=(t_s1,))
                        st1_bank_tok[a_ % 2] = st1_ev_last
                    pb03_free = K.tok("V")
                tok_z = K.tok("V")
                st2_bank = [None, None]
                for p4 in range(32):
                    slz = p4 % 2
                    K.wait("tensor", tok_z, st2_zti_cons[slz])
                    for j4 in range(4):
                        p_ = p4 * 4 + j4
                        K.t("transpose", pb[0 + slz][:, j4 * 128:(j4 + 1) * 128],
                            Zb[:, p_ * 128:(p_ + 1) * 128], ident[:])
                    ttr = K.tok("T")
                    tze = K.v("tensor_copy", zti[slz][:], pb[0 + slz][:, 0:512],
                              wait=[ttr, st2_zti_cons[slz]])
                    K.wait("tensor", tze, st2_bank[slz])
                    for j4 in range(4):
                        K.t("matmul", pb[2 + slz][:, j4 * 128:(j4 + 1) * 128], bd64[:],
                            zti[slz][:, j4 * 128:(j4 + 1) * 128],
                            start=(j4 == 0), stop=(j4 == 3), skip_group_check=True)
                    tm2 = K.tok("T")
                    st2_zti_cons[slz] = tm2
                    w2_ev = K.v("tensor_copy", W2p[:, p4 * 512:(p4 + 1) * 512], pb[2 + slz][:, 0:512],
                                wait=(tm2,))
                    st2_bank[slz] = w2_ev
                tok_w2 = K.tok("V")
                t_cm = K.v("tensor_reduce", sm2[:, 120:121] if False else scl[:, 0:128],
                           W2p[:].rearrange("p (q b) -> p q b", q=128), axis=AX.X, op=ALU.max,
                           apply_absolute_value=True, wait=(tok_w2,))
                K.wait("tensor", t_cm)
                t_trc = K.t("transpose", pb[6][:, 0:128], scl[:, 0:128], ident[:])
                t_cm2 = K.v("tensor_copy", scl[:, 0:128], pb[6][:, 0:128], wait=(t_cm, t_trc))
                t_M = K.v("tensor_reduce", sm2[:, 120:122],
                          scl[:, 0:128].rearrange("p (t a) -> p t a", t=2), axis=AX.X, op=ALU.max,
                          wait=(t_cm2,))
                t_Mc = K.v("tensor_scalar_max", sm2[:, 120:122], sm2[:, 120:122], EPS, wait=(K.tok("V"),))
                t_s = K.v("reciprocal", sm2[:, 122:124], sm2[:, 120:122], wait=(t_Mc,))
                t_s2 = K.v("tensor_scalar_mul", sm2[:, 122:124], sm2[:, 122:124], 128.0, wait=(K.tok("V"),))
                t_dq = K.v("tensor_scalar", sm2[:, 124:126], sm2[:, 120:122], swcol[:, 3:4], None,
                           ALU.mult, wait=(t_Mc, t_cdc))
                K.wait("tensor", t_s2, t_dq)
                t_ts = K.t("transpose", pb[6][0:2, 0:128], sm2[:, 122:124], ident[:])
                t_sr = K.v("tensor_copy", rowbuf[0:2, 0:128], pb[6][0:2, 0:128], wait=(t_ts,))
                t_t1 = K.t("transpose", pb[6][0:1, 128:256], sm2[:, 124:125], ident[:], wait=(t_sr,))
                t_t2 = K.t("transpose", pb[6][0:1, 256:384], sm2[:, 125:126], ident[:])
                t_f1 = K.v("tensor_copy", rowbuf[0:1, 128:384:2], pb[6][0:1, 128:256], wait=(t_t1, t_t2))
                t_f2 = K.v("tensor_copy", rowbuf[0:1, 129:384:2], pb[6][0:1, 256:384], wait=(K.tok("V"),))
                K.wait("tensor", t_sr, t_f1, t_f2)
                t_dm = K.t("matmul", pb[7][:, 0:128], p2c[:], rowbuf[0:2, 0:128][0:2, :],
                           start=True, stop=True)
                t_scl = K.v("tensor_copy", scl[:, 0:128], pb[7][:, 0:128], wait=(t_dm, t_M))
                t_sd = K.t("matmul", pb[6][:, 0:256], onerow[:], rowbuf[0:1, 128:384],
                           start=True, stop=True)
                t_sdt = K.v("tensor_copy", sdt[:], pb[6][:, 0:256], wait=(t_sd,))
                q1 = K.v("tensor_tensor", W2p[:].rearrange("p (q b) -> p q b", q=128),
                         W2p[:].rearrange("p (q b) -> p q b", q=128),
                         scl[:, 0:128].rearrange("p q -> p q 1" if False else "p (q o) -> p q o", o=1).broadcast_to([128, 128, 128]),
                         ALU.mult, wait=(t_scl, tok_w2))
                q2 = K.v("tensor_scalar_add", W2p[:], W2p[:], MAGIC, wait=(q1,))
                q3 = K.v("tensor_scalar", W2p[:], W2p[:], MAGIC, 127.0,
                         ALU.subtract, ALU.min, wait=(q2,))
                tq_bank = [None, None]
                for p4 in range(32):
                    slz = p4 % 2
                    K.wait("tensor", q3, tq_bank[slz])
                    for j4 in range(4):
                        p_ = p4 * 4 + j4
                        K.t("transpose", pb[2 + slz][:, j4 * 128:(j4 + 1) * 128],
                            W2p[:, p_ * 128:(p_ + 1) * 128], ident[:])
                    ttr = K.tok("T")
                    tq_ev = K.v("tensor_copy", W2qp[:, 16384 + p4 * 512:16384 + (p4 + 1) * 512],
                                pb[2 + slz][:, 0:512], wait=[ttr, tq_bank[slz]])
                    tq_bank[slz] = tq_ev
                tok_w2q = K.tok("V")
                dn_bank = [None, None]
                for ocp in range(8):
                    wldd = [tok_ag, gu_cons[0], gu_cons[1]]
                    if ocp > 0:
                        wldd.append(dn_b_cons)
                    elif dn_cons is not None:
                        wldd.append(dn_cons)
                    for qtr in range(4):
                        K.dma("dS" if qtr % 2 == 0 else "dA", wstr[qtr][:],
                              cowq_blk[3 * ocp + 2, qtr], sem="fd", wait=wldd)
                    tok_wd = K.tok("fd")
                    for oc2 in range(2):
                        oc = ocp * 2 + oc2
                        db_ = oc % 2
                        K.wait("tensor", tok_wd, tok_w2q, dn_bank[db_])
                        for a_ in range(64):
                            K.t("matmul", pb[4 + db_][:, 0:256],
                                wstr[a_ >> 4][:, (a_ & 15) * 256 + oc2 * 128:(a_ & 15) * 256 + oc2 * 128 + 128],
                                W2qp[:, 16384 + a_:16384 + 16384:128][:, 0:256] if False else
                                W2qp[:].rearrange("p (h q t a) -> p h q t a", h=2, q=128, t=2)[:, 1, :, :, a_].rearrange("p q t -> p (q t)"),
                                start=(a_ == 0), stop=(a_ == 63))
                        tdn = K.tok("T")
                        fsl = oc % 2
                        t_fo = K.v("tensor_mul", fout[fsl][:], pb[4 + db_][:, 0:256], sdt[:],
                                   wait=[tdn, t_sdt, fo_free[fsl]])
                        dn_bank[db_] = t_fo
                        fo_free[fsl] = K.dma("dA", ffn_scr[oc][:, g * 256:(g + 1) * 256],
                                             fout[fsl][:], sem=f"ffs{fsl}", wait=(t_fo,))
                    dn_b_cons = K.tok("T")
                dn_cons = K.tok("T")
                fg_done = K.tok("T")
            tok_ffn_scr = None
            for _t in [K.tok("ffs0"), K.tok("ffs1")]:
                pass
            tok_ffn_scr = [K.tok("ffs0"), K.tok("ffs1")]
            sc_p3.close()
            K.barrier(dma_sems=["fxq", "fg0", "fg1", "fu0", "fu1", "fd", "ffs0", "ffs1"])

        if "d_wq" in dbg_out and stage >= 4:
            tmpw = nc.alloc_sbuf_tensor("tmpw", [128, 1024], F32)
            tmpwb = nc.alloc_sbuf_tensor("tmpwb", [128, 1024], BF16)
            cowq2 = cout_wq[:].rearrange("w (k p c) -> w k p c", k=16, p=128)
            for kc in range(16):
                tld = K.dma("dS", tmpwb[:], cowq2[0, kc], sem="dbw",
                            wait=[tok_ag, ("dbw2", K.cnt.get("dbw2", 0)), K.tok("V")])
                tcv = K.v("tensor_copy", tmpw[:], tmpwb[:], wait=(tld,))
                K.dma("dS", dbg_out["d_wq"][kc * 128:(kc + 1) * 128, :], tmpw[:], sem="dbw2", wait=(tcv,))
            K.dma("dS", dbg_out["d_wq"][2048:2049, 0:8], mean_sb[0:1, 0:8], sem="dbw3",
                  wait=(K.tok("V"), t_cdc))
        if "d_ffn" in dbg_out and stage >= 4:
            tmpf3 = nc.alloc_sbuf_tensor("tmpf3", [128, TOK], F32)
            for f in range(16):
                tld = K.dma("dS", tmpf3[:], ffn_scr[f], sem="dbf",
                            wait=tok_ffn_scr + [("dbf2", K.cnt.get("dbf2", 0))])
                K.dma("dS", dbg_out["d_ffn"][f * 128:(f + 1) * 128, :], tmpf3[:], sem="dbf2", wait=(tld,))

        if stage >= 5:
            sc_p4 = ExitStack()
            ybuf = [sc_p4.enter_context(nc.sbuf_tensor(f"yb{k}", [128, TOK], F32)) for k in range(16)]
            sc_p4a = ExitStack()
            oTf = [sc_p4a.enter_context(nc.sbuf_tensor(f"oTf{k}", [128, TOK], F32)) for k in range(16)]
            wo = sc_p4a.enter_context(nc.sbuf_tensor("wo", [128, 8192], F32))
            s4a = [sc_p4a.enter_context(nc.sbuf_tensor(f"s4a{j}", [128, 512], F32)) for j in range(2)]
            fnt = [sc_p4a.enter_context(nc.sbuf_tensor(f"fnt{j}", [128, 512], F32)) for j in range(2)]
            coo = cout_o[:].rearrange("r (c p t) -> r c p t", c=2, p=128)
            for k in range(16):
                K.dma("dS", oTf[k][:], coo[k // 2, k % 2], sem="p4o", wait=(tok_a2a_o,))
            tok_oT = K.tok("p4o")
            wo_tok = None
            wo_cons = None
            bank_tok4 = [None] * 4
            s4_free = [None, None]
            fnt_free = [None, None]
            for og in range(4):
                wldw = [wo_cons] if wo_cons else []
                K.dma("dS", wo[:], owT[og], sem="p4w", wait=wldw)
                wo_tok = K.tok("p4w")
                for tg in range(2):
                    for oc4 in range(4):
                        oc = og * 4 + oc4
                        K.wait("tensor", wo_tok, tok_oT, bank_tok4[oc4])
                        for kc in range(16):
                            K.t("matmul", pb[oc4][:, 0:512],
                                wo[:, kc * 512 + oc4 * 128:kc * 512 + (oc4 + 1) * 128],
                                oTf[kc][:, tg * 512:(tg + 1) * 512],
                                start=(kc == 0), stop=(kc == 15))
                        tmm4 = K.tok("T")
                        ssl = (tg * 4 + oc4) % 2
                        t_b4 = K.s("activation", s4a[ssl][:], pb[oc4][:, 0:512], AF.Identity,
                                   bias=outb_sb[:, oc:oc + 1], scale=1.0,
                                   wait=[tmm4, s4_free[ssl]])
                        bank_tok4[oc4] = t_b4
                        fnl = K.dma("dA", fnt[ssl][:], ffn_scr[oc][:, tg * 512:(tg + 1) * 512],
                                    sem=f"p4f{ssl}", wait=(tok_ffn_scr or []) + [fnt_free[ssl]])
                        t_a4 = K.v("tensor_add", ybuf[oc][:, tg * 512:(tg + 1) * 512],
                                   s4a[ssl][:], fnt[ssl][:], wait=[t_b4, fnl])
                        s4_free[ssl] = t_a4
                        fnt_free[ssl] = t_a4
                wo_cons = K.tok("T")
            tok_yb = K.tok("V")
            sc_p4a.close()
            K.barrier(dma_sems=["p4o", "p4w", "p4f0", "p4f1"])
            xrl = [sc_p4.enter_context(nc.sbuf_tensor(f"xrl{j}", [128, DM], F32)) for j in range(2)]
            yst = [sc_p4.enter_context(nc.sbuf_tensor(f"yst{j}", [128, DM], F32)) for j in range(2)]
            xld = [None] * NTILE
            for i in range(2):
                xld[i] = K.dma("dS", xrl[i][:], x_in[i * 128:(i + 1) * 128, :], sem=f"p4x{i % 2}")
            y_done_v = [None, None]
            for i in range(NTILE):
                bs4 = i % 2
                evs4 = []
                for bg in range(4):
                    bk = pb[4 + bg]
                    w = [tok_yb]
                    if i >= 1:
                        w.append(prev_evs4_last)
                    for sl in range(4):
                        f = bg * 4 + sl
                        K.t("transpose", bk[:, sl * 128:(sl + 1) * 128],
                            ybuf[f][:, i * 128:(i + 1) * 128], ident[:],
                            wait=w if sl == 0 else ())
                    tbk = K.tok("T")
                    evs4.append(K.v("tensor_add", yst[bs4][:, bg * 512:(bg + 1) * 512],
                                    bk[:, 0:512], xrl[bs4][:, bg * 512:(bg + 1) * 512],
                                    wait=[tbk, xld[i], y_done_v[bs4]]))
                prev_evs4_last = evs4[-1]
                tok_y_i = K.tok("V")
                ydma = K.dma("dA", y_out[i * 128:(i + 1) * 128, :], yst[bs4][:], sem=f"p4y{bs4}",
                             wait=(tok_y_i,))
                y_done_v[bs4] = ydma
                if i + 2 < NTILE:
                    xld[i + 2] = K.dma("dS", xrl[bs4][:], x_in[(i + 2) * 128:(i + 3) * 128, :],
                                       sem=f"p4x{bs4}", wait=(tok_y_i,))


        if stage < 5:
            zf = nc.alloc_sbuf_tensor("zf", [128, DM], F32)
            tz = K.v("memset", zf[:], 0.0, wait=(K.tok("V"), K.tok("SC"), K.tok("T")))
            for i in range(NTILE):
                K.dma("dS", y_out[i * 128:(i + 1) * 128, :], zf[:], sem="yw", wait=(tz,))
        for sname, cval in list(K.cnt.items()):
            if sname not in ("V", "SC", "T", "CC") and cval > 0:
                K.wait("sync", (sname, cval))
    return nc, dbg_out


def make_consts():
    ident = np.eye(128, dtype=np.float32)
    idx = np.arange(128)
    pc = np.array([bin(i & j).count("1") for i in idx for j in idx]).reshape(128, 128)
    h128 = ((-1.0) ** pc).astype(np.float32)
    i64 = np.arange(64)
    pc64 = np.array([bin(a & b).count("1") for a in i64 for b in i64]).reshape(64, 64)
    h64 = ((-1.0) ** pc64).astype(np.float32)
    bd64 = np.zeros((128, 128), np.float32)
    bd64[:64, :64] = h64
    bd64[64:, 64:] = h64
    p2 = np.zeros((2, 128), np.float32)
    p2[0, :64] = 1.0
    p2[1, 64:] = 1.0
    cmat = np.concatenate([ident, h128, bd64, p2], axis=0)
    masks = np.zeros((4, 128, 512), np.float32)
    for m_ in range(4):
        for p in range(128):
            masks[m_, p, m_ * 128 + p + 1:] = NEG
    return cmat, masks.reshape(512, 512)


def prep_inputs(inputs):
    x = np.ascontiguousarray(np.asarray(inputs["L0_attn_hidden_states"], np.float32).reshape(NT, DM))
    _wq = np.asarray(inputs["L0_attn_Wqkv_w"], np.float32).T
    wqkvT = np.ascontiguousarray(_wq.reshape(16, 128, 8, 512).transpose(2, 1, 0, 3)).reshape(8, 128, 8192)
    qkvb = np.ascontiguousarray(np.asarray(inputs["L0_attn_Wqkv_b"], np.float32).reshape(32, 128).T)
    _ow = np.asarray(inputs["L0_attn_out_w"], np.float32).T
    owT = np.ascontiguousarray(_ow.reshape(16, 128, 4, 512).transpose(2, 1, 0, 3)).reshape(4, 128, 8192)
    outb = np.ascontiguousarray(np.asarray(inputs["L0_attn_out_b"], np.float32).reshape(16, 128).T)
    lamv = np.concatenate([np.asarray(inputs["L0_attn_lambda_q1"], np.float32),
                           np.asarray(inputs["L0_attn_lambda_k1"], np.float32),
                           np.asarray(inputs["L0_attn_lambda_q2"], np.float32),
                           np.asarray(inputs["L0_attn_lambda_k2"], np.float32)]).reshape(1, 512)
    sublnw = np.asarray(inputs["L0_attn_subln_w"], np.float32).reshape(1, 256)
    wgT = np.asarray(inputs["L0_ffn_w_gate"], np.float32).T
    wuT = np.asarray(inputs["L0_ffn_w_up"], np.float32).T
    wdT = np.asarray(inputs["L0_ffn_w_down"], np.float32).T
    cmat, masks = make_consts()
    in_maps = []
    for cid in range(8):
        in_maps.append({
            "x": np.ascontiguousarray(x[cid * TOK:(cid + 1) * TOK]),
            "wqkvT": wqkvT,
            "qkv_b": qkvb,
            "owT": owT,
            "out_b": outb,
            "lamv": lamv,
            "sublnw": sublnw,
            "wgs": np.ascontiguousarray(wgT[:, cid * 1024:(cid + 1) * 1024]),
            "wus": np.ascontiguousarray(wuT[:, cid * 1024:(cid + 1) * 1024]),
            "wds": np.ascontiguousarray(wdT[:, cid * 256:(cid + 1) * 256]),
            "cmat": cmat,
            "masks": masks,
        })
    return in_maps


_NC_CACHE = {}


def _get_nc():
    if "nc" not in _NC_CACHE:
        _NC_CACHE["nc"] = build(stage=5, dbg=())[0]
    return _NC_CACHE["nc"]


def kernel(**inputs):
    in_maps = prep_inputs(inputs)
    nc = _get_nc()
    res = run_bass_kernel_spmd(nc, in_maps, core_ids=list(range(8)))
    y = np.concatenate([np.asarray(r["y"]) for r in res.results], axis=0)
    return np.ascontiguousarray(y.reshape(B, S, DM).astype(np.float32, copy=False))


# version 12
# speedup vs baseline: 1.0257x; 1.0112x over previous
import math
from contextlib import ExitStack
import numpy as np
import ml_dtypes
import concourse.bass as bass
import concourse.mybir as mybir
from concourse.bass_utils import run_bass_kernel_spmd
from concourse.aot_env import allow_ici_all_to_all

F32 = mybir.dt.float32
F16 = mybir.dt.float16
BF16 = mybir.dt.bfloat16
I32 = mybir.dt.int32
AF = mybir.ActivationFunctionType
ALU = mybir.AluOpType
AX = mybir.AxisListType

B, S, DM = 2, 4096, 2048
NT = B * S
TOK = NT // 8
NTILE = TOK // 128
QKVO = 4096
D = 128
E = 256
I_FF = 8192
LAMBDA_INIT = 0.8 - 0.6 * math.exp(-0.3)
SCALE = D ** -0.5
EPS = 1e-5
NEG = float(np.finfo(np.float32).min)
MAGIC = 12582912.0
TGF = 256
NGF = TOK // TGF


class Build:

    def __init__(self, nc):
        self.nc = nc
        self.sem = {}
        self.cnt = {}
        self.waited = {}
        for s in ["V", "SC", "T", "CC"]:
            self.sem[s] = nc.alloc_semaphore("sem_" + s)
            self.cnt[s] = 0
        for e in ["sync", "vector", "scalar", "tensor", "gpsimd"]:
            self.waited[e] = {}

    def dsem(self, name):
        if name not in self.sem:
            self.sem[name] = self.nc.alloc_semaphore("sem_" + name)
            self.cnt[name] = 0
        return name

    def tok(self, s):
        return (s, self.cnt[s])

    def _eng(self, name):
        return getattr(self.nc, name)

    def wait(self, engname, *tokens):
        w = self.waited[engname]
        for t in tokens:
            if t is None:
                continue
            s, v = t
            if v <= 0 or w.get(s, 0) >= v:
                continue
            self._eng(engname).wait_ge(self.sem[s], v)
            w[s] = v

    def bump(self, s, instr, k=1):
        instr.then_inc(self.sem[s], k)
        self.cnt[s] += k
        return (s, self.cnt[s])

    def dma(self, q, out, in_, sem=None, wait=()):
        eng = {"dS": "sync", "dA": "scalar", "dG": "gpsimd"}[q]
        s = self.dsem(sem if sem is not None else "d_" + q)
        self.wait(eng, *wait)
        return self.bump(s, self._eng(eng).dma_start(out=out, in_=in_), 16)

    def v(self, fn, *a, wait=(), **kw):
        self.wait("vector", *wait)
        return self.bump("V", getattr(self.nc.vector, fn)(*a, **kw))

    def s(self, fn, *a, wait=(), **kw):
        self.wait("scalar", *wait)
        return self.bump("SC", getattr(self.nc.scalar, fn)(*a, **kw))

    def t(self, fn, *a, wait=(), **kw):
        self.wait("tensor", *wait)
        return self.bump("T", getattr(self.nc.tensor, fn)(*a, **kw))

    def barrier(self, dma_sems=()):
        toks = [self.tok("V"), self.tok("SC"), self.tok("T")]
        toks += [(s, self.cnt[s]) for s in dma_sems if s in self.cnt]
        for e in ["sync", "vector", "scalar", "tensor"]:
            self.wait(e, *toks)

    def cc(self, kind, op, rg, ins, outs, wait=()):
        self.wait("gpsimd", *wait)
        return self.bump("CC", self.nc.gpsimd.collective_compute(
            kind, op, replica_groups=rg, ins=ins, outs=outs))


P0CUT = 99


def build(stage=9, dbg=(), dbg_qbmax=0):
    nc = bass.Bass("TRN2", target_bir_lowering=False, debug=False, num_devices=8)
    K = Build(nc)
    RG = [list(range(8))]

    x_in = nc.dram_tensor("x", [TOK, DM], F32, kind="ExternalInput").ap()
    wqkvT = nc.dram_tensor("wqkvT", [8, 128, 8192], F32, kind="ExternalInput").ap()
    qkv_b = nc.dram_tensor("qkv_b", [128, 32], F32, kind="ExternalInput").ap()
    owT = nc.dram_tensor("owT", [4, 128, 8192], F32, kind="ExternalInput").ap()
    out_b = nc.dram_tensor("out_b", [128, 16], F32, kind="ExternalInput").ap()
    lamv = nc.dram_tensor("lamv", [1, 512], F32, kind="ExternalInput").ap()
    sublnw = nc.dram_tensor("sublnw", [1, 256], F32, kind="ExternalInput").ap()
    wgs = nc.dram_tensor("wgs", [DM, 1024], F32, kind="ExternalInput").ap()
    wus = nc.dram_tensor("wus", [DM, 1024], F32, kind="ExternalInput").ap()
    wds = nc.dram_tensor("wds", [I_FF, 256], F32, kind="ExternalInput").ap()
    cmat = nc.dram_tensor("cmat", [386, 128], F32, kind="ExternalInput").ap()
    masks = nc.dram_tensor("masks", [512, 512], F32, kind="ExternalInput").ap()

    y_out = nc.dram_tensor("y", [TOK, DM], F32, kind="ExternalOutput").ap()
    dbg_out = {}
    for name, shape in dbg:
        dbg_out[name] = nc.dram_tensor(name, list(shape), F32, kind="ExternalOutput").ap()

    cin_mean = nc.dram_tensor("cin_mean", [1, 8], F32).ap()
    cout_mean = nc.dram_tensor("cout_mean", [1, 8], F32).ap()
    cin_wq = nc.dram_tensor("cin_wq", [3, DM * 1024], BF16).ap()
    cout_wq = nc.dram_tensor("cout_wq", [24, DM * 1024], BF16).ap()
    SH16 = 2 * 2 * 128 * TOK
    cin_q16 = nc.dram_tensor("cin_q16", [8, SH16], F16).ap()
    cout_q16 = nc.dram_tensor("cout_q16", [8, SH16], F16).ap()
    cin_k16 = nc.dram_tensor("cin_k16", [8, SH16], F16).ap()
    cout_k16 = nc.dram_tensor("cout_k16", [8, SH16], F16).ap()
    SH32 = 2 * 128 * TOK
    cin_v32 = nc.dram_tensor("cin_v32", [8, SH32], F32).ap()
    cout_v32 = nc.dram_tensor("cout_v32", [8, SH32], F32).ap()
    SHO = 2 * 128 * TOK
    cin_o = nc.dram_tensor("cin_o", [8, SHO], F32).ap()
    cout_o = nc.dram_tensor("cout_o", [8, SHO], F32).ap()
    xq_scr = nc.dram_tensor("xq_scr", [4, 128, 4096], BF16).ap()
    ffn_scr = nc.dram_tensor("ffn_scr", [16, 128, TOK], F32).ap()

    ident = nc.alloc_sbuf_tensor("ident", [128, 128], F32)
    identb = nc.alloc_sbuf_tensor("identb", [128, 128], BF16)
    h128 = nc.alloc_sbuf_tensor("h128", [128, 128], F32)
    bd64 = nc.alloc_sbuf_tensor("bd64", [128, 128], F32)
    p2c = nc.alloc_sbuf_tensor("p2c", [2, 128], F32)
    onerow = nc.alloc_sbuf_tensor("onerow", [1, 128], F32)
    qkvb_sb = nc.alloc_sbuf_tensor("qkvb_sb", [128, 32], F32)
    outb_sb = nc.alloc_sbuf_tensor("outb_sb", [128, 16], F32)
    lam_sb = nc.alloc_sbuf_tensor("lam_sb", [1, 512], F32)
    subln_sb = nc.alloc_sbuf_tensor("subln_sb", [1, 256], F32)
    subln_bc = nc.alloc_sbuf_tensor("subln_bc", [128, 256], F32)
    lam_col = nc.alloc_sbuf_tensor("lam_col", [128, 1], F32)
    nlam_col = nc.alloc_sbuf_tensor("nlam_col", [128, 1], F32)
    epsc = nc.alloc_sbuf_tensor("epsc", [128, 2], F32)
    sm = nc.alloc_sbuf_tensor("sm", [128, 192], F32)
    sm2 = nc.alloc_sbuf_tensor("sm2", [128, 160], F32)
    rowbuf = nc.alloc_sbuf_tensor("rowbuf", [2, 512], F32)
    dq_t = nc.alloc_sbuf_tensor("dq_t", [128, 8], F32)
    dqg_row = nc.alloc_sbuf_tensor("dqg_row", [1, TOK], F32)
    dqu_row = nc.alloc_sbuf_tensor("dqu_row", [1, TOK], F32)
    dqd_row = nc.alloc_sbuf_tensor("dqd_row", [1, TOK], F32)
    mean_sb = nc.alloc_sbuf_tensor("mean_sb", [1, 8], F32)
    swcol = nc.alloc_sbuf_tensor("swcol", [128, 4], F32)

    sc_mask = ExitStack()
    mask_sb = sc_mask.enter_context(nc.sbuf_tensor("mask_sb", [128, 2048], F32))
    dqrow = sc_mask.enter_context(nc.sbuf_tensor("dqrow", [1, TOK], F32))
    pb = [nc.alloc_psum_tensor(f"pb{i}", [128, 512], F32) for i in range(8)]

    with nc.Block() as block, allow_ici_all_to_all():
        K.dma("dS", ident[:], cmat[0:128, :], sem="cst")
        K.dma("dS", h128[:], cmat[128:256, :], sem="cst")
        K.dma("dS", bd64[:], cmat[256:384, :], sem="cst")
        K.dma("dS", p2c[:], cmat[384:386, :], sem="cst")
        for _m in range(4):
            K.dma("dS", mask_sb[:, _m * 512:(_m + 1) * 512], masks[_m * 128:(_m + 1) * 128, :], sem="cst")
        K.dma("dS", qkvb_sb[:], qkv_b, sem="cst")
        K.dma("dS", outb_sb[:], out_b, sem="cst")
        K.dma("dS", lam_sb[:], lamv, sem="cst")
        K.dma("dS", subln_sb[:], sublnw, sem="cst")
        tok_const = K.tok("cst")

        t_ones = K.v("memset", onerow[:], 1.0)
        K.v("memset", epsc[:, 0:1], EPS)
        K.v("memset", epsc[:, 1:2], 1.0 / 256.0)
        tok_eps = K.tok("V")
        K.v("tensor_copy", identb[:], ident[:], wait=(tok_const,))
        tok_idc = K.tok("V")
        K.wait("tensor", tok_const, t_ones)
        t_mm = K.t("matmul", pb[6][:, 0:256], onerow[:], subln_sb[:], start=True, stop=True)
        t_sbc = K.v("tensor_scalar_mul", subln_bc[:], pb[6][:, 0:256], 1.0 - LAMBDA_INIT, wait=(t_mm,))

        t1 = K.v("tensor_mul", rowbuf[0:1, 0:128], lam_sb[0:1, 0:128], lam_sb[0:1, 128:256], wait=(tok_const,))
        t2 = K.v("tensor_mul", rowbuf[0:1, 128:256], lam_sb[0:1, 256:384], lam_sb[0:1, 384:512])
        t3 = K.v("reduce_sum", sm2[0:1, 140:141], rowbuf[0:1, 0:128], axis=AX.X, wait=(t1,))
        t4 = K.v("reduce_sum", sm2[0:1, 141:142], rowbuf[0:1, 128:256], axis=AX.X, wait=(t2,))
        t5 = K.s("activation", sm2[0:1, 142:143], sm2[0:1, 140:141], AF.Exp, wait=(t3,))
        t6 = K.s("activation", sm2[0:1, 143:144], sm2[0:1, 141:142], AF.Exp, wait=(t4,))
        t7 = K.v("tensor_scalar", sm2[0:1, 138:139], sm2[0:1, 143:144], -1.0, LAMBDA_INIT,
                 ALU.mult, ALU.add, wait=(t5, t6))
        t8 = K.v("tensor_add", sm2[0:1, 139:140], sm2[0:1, 138:139], sm2[0:1, 142:143], wait=(t7,))
        K.wait("tensor", t8, t_sbc)
        t9 = K.t("matmul", pb[6][:, 256:257], onerow[:], sm2[0:1, 139:140], start=True, stop=True)
        K.v("tensor_copy", lam_col[:], pb[6][:, 256:257], wait=(t9,))
        K.v("tensor_scalar_mul", nlam_col[:], pb[6][:, 256:257], -1.0, wait=(K.tok("V"),))
        tok_lam = K.tok("V")
        tok_pb6_free = K.tok("V")

        sc_xnT = ExitStack()
        sc_p0 = ExitStack()
        xnT = [sc_xnT.enter_context(nc.sbuf_tensor(f"xnT{f}", [128, TOK], F32)) for f in range(16)]
        xqT = [sc_p0.enter_context(nc.sbuf_tensor(f"xqT{f}", [128, TOK], BF16)) for f in range(16)]
        xt = [sc_p0.enter_context(nc.sbuf_tensor(f"xt{j}", [128, DM], F32)) for j in range(2)]
        xc = [sc_p0.enter_context(nc.sbuf_tensor(f"xc{j}", [128, DM], F32)) for j in range(2)]
        xnb = [sc_p0.enter_context(nc.sbuf_tensor(f"xnb{j}", [128, DM], F32)) for j in range(2)]
        sqd = xt
        scrf = xc

        xdma = [None] * NTILE
        tile_vdone = [None] * NTILE
        tile_tdone = [None] * NTILE
        last_xq_ev = [None] * NTILE
        for i in range(2):
            xdma[i] = K.dma("dS", xt[i][:], x_in[i * 128:(i + 1) * 128, :], sem=f"x{i % 2}")
        for i in range(NTILE):
            bs_ = i % 2
            cb = 16 * i
            c = lambda j: sm[:, cb + j:cb + j + 1]
            K.v("reduce_sum", c(0), xt[bs_][:], axis=AX.X, wait=(xdma[i],))
            tm = K.v("tensor_scalar_mul", c(1), c(0), -1.0 / DM, wait=(K.tok("V"),))
            t_xc = K.s("activation", xc[bs_][:], xt[bs_][:], AF.Identity,
                       bias=c(1), scale=1.0, wait=(tm, xdma[i]))
            t_sq = K.s("activation", sqd[bs_][:], xc[bs_][:], AF.Square,
                       accum_out=c(2), wait=(t_xc,))
            tv = K.v("tensor_scalar", c(3), c(2), 1.0 / DM, EPS, ALU.mult, ALU.add, wait=(t_sq,))
            tu = K.s("activation", c(4), c(3), AF.Ln, wait=(tv,))
            tlast = K.s("activation", c(5), c(4), AF.Exp, scale=-0.5, wait=(tu,))
            rcur = 5
            for it in range(1):
                K.v("tensor_mul", c(6), c(rcur), c(rcur), wait=(tlast,))
                K.v("tensor_mul", c(6), c(6), c(3), wait=(K.tok("V"),))
                K.v("tensor_scalar", c(6), c(6), -0.5, 1.5, ALU.mult, ALU.add, wait=(K.tok("V"),))
                tlast = K.v("tensor_mul", c(7), c(rcur), c(6), wait=(K.tok("V"),))
                rcur = 7
            wprev = [tile_tdone[i - 2], tile_vdone[i - 2]] if i >= 2 else []
            t_xn = K.v("tensor_scalar_mul", xnb[bs_][:], xc[bs_][:], c(rcur),
                       wait=[tlast] + wprev)
            if P0CUT < 2:
                tile_vdone[i] = K.tok("V")
                tile_tdone[i] = K.tok("T")
                last_xq_ev[i] = None
                if i + 2 < NTILE:
                    xdma[i + 2] = K.dma("dS", xt[bs_][:], x_in[(i + 2) * 128:(i + 3) * 128, :],
                                        sem=f"x{bs_}", wait=(tile_vdone[i], K.tok("V")))
                continue
            K.v("tensor_reduce", c(8), xnb[bs_][:], axis=AX.X, op=ALU.max,
                apply_absolute_value=True, wait=(t_xn,))
            K.v("tensor_scalar_max", c(9), c(8), EPS, wait=(K.tok("V"),))
            K.v("reciprocal", c(10), c(9), wait=(K.tok("V"),))
            tsc2 = K.v("tensor_scalar_mul", c(11), c(10), 128.0, wait=(K.tok("V"),))
            K.v("tensor_scalar_mul", dq_t[:, i:i + 1], c(9), 1.0 / 128.0)
            tq1 = K.v("tensor_scalar_mul", scrf[bs_][:], xnb[bs_][:], c(11), wait=(tsc2, t_xn))
            tq2 = K.v("tensor_scalar_add", scrf[bs_][:], scrf[bs_][:], MAGIC, wait=(tq1,))
            tq4 = K.v("tensor_scalar", sqd[bs_][:], scrf[bs_][:], MAGIC, 127.0,
                      ALU.subtract, ALU.min, wait=(tq2, t_sq))
            tile_vdone[i] = K.tok("V")
            if P0CUT < 3:
                tile_tdone[i] = K.tok("T")
                last_xq_ev[i] = None
                if i + 2 < NTILE:
                    xdma[i + 2] = K.dma("dS", xt[bs_][:], x_in[(i + 2) * 128:(i + 3) * 128, :],
                                        sem=f"x{bs_}", wait=(tile_vdone[i], K.tok("V")))
                continue
            ev_xn = []
            for bg in range(4):
                bk = pb[4 + bg]
                w = [t_xn, tok_idc]
                if i >= 1:
                    w.append(last_xq_ev[i - 1])
                for sl in range(4):
                    f = bg * 4 + sl
                    K.t("transpose", bk[:, sl * 128:(sl + 1) * 128],
                        xnb[bs_][:, f * 128:(f + 1) * 128], ident[:],
                        wait=w if sl == 0 else ())
                tbk = K.tok("T")
                for sl in range(4):
                    f = bg * 4 + sl
                    ev_xn.append(K.v("tensor_copy", xnT[f][:, i * 128:(i + 1) * 128],
                                     bk[:, sl * 128:(sl + 1) * 128], wait=(tbk,)))
            ev_xq = []
            if P0CUT < 4:
                tile_tdone[i] = K.tok("T")
                last_xq_ev[i] = ev_xn[-1]
                if i + 2 < NTILE:
                    xdma[i + 2] = K.dma("dS", xt[bs_][:], x_in[(i + 2) * 128:(i + 3) * 128, :],
                                        sem=f"x{bs_}", wait=(tile_vdone[i], tile_tdone[i]))
                continue
            for bg in range(4):
                bk = pb[4 + bg]
                for sl in range(4):
                    f = bg * 4 + sl
                    K.t("transpose", bk[:, sl * 128:(sl + 1) * 128],
                        sqd[bs_][:, f * 128:(f + 1) * 128], ident[:],
                        wait=(tq4, ev_xn[bg * 4 + 3]) if sl == 0 else ())
                tbk = K.tok("T")
                for sl in range(4):
                    f = bg * 4 + sl
                    ev_xq.append(K.v("tensor_copy", xqT[f][:, i * 128:(i + 1) * 128],
                                     bk[:, sl * 128:(sl + 1) * 128], wait=(tbk,)))
            last_xq_ev[i] = ev_xq[-1]
            tile_tdone[i] = K.tok("T")
            if i + 2 < NTILE:
                xdma[i + 2] = K.dma("dS", xt[bs_][:], x_in[(i + 2) * 128:(i + 3) * 128, :],
                                    sem=f"x{bs_}", wait=(tile_vdone[i], tile_tdone[i]))
        K.wait("tensor", tok_pb6_free, K.tok("V"))
        for half in range(2 if P0CUT >= 5 else 0):
            for i4 in range(4):
                i = half * 4 + i4
                K.t("transpose", pb[6][0:1, i4 * 128:(i4 + 1) * 128], dq_t[:, i:i + 1], ident[:])
            ttr = K.tok("T")
            K.v("tensor_copy", dqrow[0:1, half * 512:(half + 1) * 512], pb[6][0:1, 0:512], wait=(ttr,))
            K.wait("tensor", K.tok("V"))
        tok_dqrow = K.tok("V")

        for f in range(16 if P0CUT >= 6 else 0):
            for g_ in range(4):
                K.dma("dA", xq_scr[:].rearrange("g p (k c) -> g p k c", k=16)[g_, :, f],
                      xqT[f][:, g_ * 256:(g_ + 1) * 256], sem="xqw", wait=(K.tok("V"),))
        tok_xq_scr = K.tok("xqw") if "xqw" in K.cnt else None

        tok_ag = None
        if stage >= 4:
            sc_pp = ExitStack()
            ptile = [sc_pp.enter_context(nc.sbuf_tensor(f"ptile{j}", [128, 1024], F32)) for j in range(2)]
            qtile = [sc_pp.enter_context(nc.sbuf_tensor(f"qtile{j}", [128, 1024], BF16)) for j in range(2)]
            wds_v = wds[:].rearrange("(a p) c -> a p c", p=128)
            wgs_v = wgs[:].rearrange("(a p) c -> a p c", p=128)
            wus_v = wus[:].rearrange("(a p) c -> a p c", p=128)
            ciwq_blk = cin_wq[:].rearrange("w (s p k c) -> w s p k c", s=4, p=128, k=16)

            pdma = [None, None]
            pcons = [None, None]
            pmm = [None, None]
            onepc = sm2[:, 33:34]
            t_one2 = K.v("memset", onepc, 1.0)
            for wi, (wv, ntl) in enumerate([(wgs_v, 16), (wus_v, 16), (wds_v, 64)]):
                for a in range(ntl):
                    j = (a + wi) % 2
                    w = [pcons[j]] if pcons[j] else []
                    srcv = wv[a] if ntl == 16 else wv[a][:, 0:256]
                    pdma[j] = K.dma("dG", ptile[j][:, 0:(1024 if ntl == 16 else 256)], srcv,
                                    sem=f"pp{j}", wait=w)
                    tr = K.v("tensor_reduce", sm2[:, 34 + j:35 + j], ptile[j][:, 0:(1024 if ntl == 16 else 256)],
                             axis=AX.X, op=ALU.add, apply_absolute_value=True,
                             wait=[pdma[j], pmm[j]])
                    pcons[j] = tr
                    K.wait("tensor", tr, t_one2)
                    pmm[j] = K.t("matmul", pb[0][0:1, wi:wi + 1], sm2[:, 34 + j:35 + j], onepc,
                                 start=(wi == 0 and a == 0), stop=(wi == 2 and a == ntl - 1),
                                 skip_group_check=True)
            t_sums = K.v("tensor_copy", mean_sb[0:1, 0:3], pb[0][0:1, 0:3], wait=(K.tok("T"),))
            tz = K.v("memset", mean_sb[0:1, 3:8], 0.0)
            tmdma = K.dma("dG", cin_mean, mean_sb[0:1, :], sem="ppm", wait=(t_sums, tz))
            cc_m = K.cc("AllReduce", ALU.add, RG, [cin_mean.opt()], [cout_mean.opt()],
                        wait=[tmdma])
            tok_ar = K.tok("CC")
            tmback = K.dma("dG", mean_sb[0:1, :], cout_mean, sem="ppm2", wait=(tok_ar,))
            NELT = float(DM * I_FF)
            t_m1 = K.v("tensor_scalar_mul", mean_sb[0:1, 4:7], mean_sb[0:1, 0:3], 1.0 / NELT, wait=(tmback,))
            t_m2 = K.v("tensor_scalar_max", mean_sb[0:1, 4:7], mean_sb[0:1, 4:7], EPS, wait=(K.tok("V"),))
            t_m3 = K.v("reciprocal", mean_sb[0:1, 0:3], mean_sb[0:1, 4:7], wait=(K.tok("V"),))
            K.wait("tensor", t_m3)
            t_bcm = K.t("matmul", pb[0][:, 0:3], onerow[:], mean_sb[0:1, 0:3], start=True, stop=True)
            t_swc = K.v("tensor_copy", swcol[:, 0:3], pb[0][:, 0:3], wait=(t_bcm,))
            t_cd = K.v("tensor_scalar_mul", mean_sb[0:1, 7:8], mean_sb[0:1, 6:7],
                       1.0 / (128.0 * math.sqrt(float(I_FF))), wait=(t_m2,))
            K.wait("tensor", t_cd, t_swc)
            t_bcd = K.t("matmul", pb[0][:, 4:5], onerow[:], mean_sb[0:1, 7:8], start=True, stop=True)
            t_cdc = K.v("tensor_copy", swcol[:, 3:4], pb[0][:, 4:5], wait=(t_bcd,))
            t_dg = K.v("tensor_scalar", dqg_row[0:1, :], dqrow[0:1, :], mean_sb[0:1, 4:5], None,
                       ALU.mult, wait=(tok_dqrow, t_m2))
            t_du = K.v("tensor_scalar", dqu_row[0:1, :], dqrow[0:1, :], mean_sb[0:1, 5:6], None,
                       ALU.mult, wait=(K.tok("V"),))
            tok_dqgu = K.tok("V")
            pdma = [None, None]
            pcons = [None, None]
            qfree = [None, None]
            for wi, (wv, ntl) in enumerate([(wgs_v, 16), (wus_v, 16), (wds_v, 64)]):
                for a in range(ntl):
                    j = (a + wi) % 2
                    ww = 1024 if ntl == 16 else 256
                    w = [pcons[j]] if pcons[j] else []
                    pdma[j] = K.dma("dG", ptile[j][:, 0:ww], wv[a] if ntl == 16 else wv[a][:, 0:256],
                                    sem=f"pq{j}", wait=w)
                    t1 = K.v("tensor_scalar", ptile[j][:, 0:ww], ptile[j][:, 0:ww],
                             swcol[:, wi:wi + 1], MAGIC, ALU.mult, ALU.add,
                             wait=(pdma[j], t_swc, qfree[j]))
                    t2 = K.v("tensor_scalar", ptile[j][:, 0:ww], ptile[j][:, 0:ww],
                             MAGIC, 1.0, ALU.subtract, ALU.min, wait=(K.tok("V"),))
                    t3 = K.v("tensor_scalar_max", qtile[j][:, 0:ww], ptile[j][:, 0:ww],
                             -1.0, wait=(K.tok("V"),))
                    pcons[j] = t3
                    if ntl == 16:
                        for sub in range(4):
                            K.dma("dA", ciwq_blk[wi, sub, :, a],
                                  qtile[j][:, sub * 256:(sub + 1) * 256], sem=f"pw{j}",
                                  wait=(t3,))
                        qfree[j] = K.tok(f"pw{j}")
                    else:
                        qfree[j] = K.dma("dA", ciwq_blk[2, a >> 4, :, a & 15],
                                         qtile[j][:, 0:256], sem=f"pw{j}", wait=(t3,))
            tok_pw = [K.tok("pw0"), K.tok("pw1")]
            cc_ag = K.cc("AllGather", ALU.bypass, RG, [cin_wq.opt()], [cout_wq.opt()],
                         wait=tok_pw)
            tok_ag = K.tok("CC")
            sc_pp.close()
            K.barrier(dma_sems=["pp0", "pp1", "pq0", "pq1", "pw0", "pw1", "ppm", "ppm2"])

        q16v = cin_q16[:].rearrange("r (s p t) -> r s p t", s=4, p=128)
        k16v = cin_k16[:].rearrange("r (s p t) -> r s p t", s=4, p=128)
        v32v = cin_v32[:].rearrange("r (s p t) -> r s p t", s=2, p=128)
        tok_p1 = None
        if stage >= 2:
            sc_p0.close()
            K.barrier(dma_sems=["xqw", "x0", "x1"])
            sc_p1 = ExitStack()
            wqb = [sc_p1.enter_context(nc.sbuf_tensor(f"wqb{j}", [128, 8192], F32))
                   for j in range(2)]
            stq = [sc_p1.enter_context(nc.sbuf_tensor(f"stq{j}", [128, 512], F32)) for j in range(2)]
            sthi = [sc_p1.enter_context(nc.sbuf_tensor(f"sthi{j}", [128, 512], F16)) for j in range(2)]
            stlo = [sc_p1.enter_context(nc.sbuf_tensor(f"stlo{j}", [128, 512], F16)) for j in range(2)]

            wq_tok = [None, None]
            wq_cons = [None, None]

            def load_wq(og):
                j = og % 2
                w = [wq_cons[j]] if wq_cons[j] else []
                K.dma("dS", wqb[j][:, 0:4096], wqkvT[og][:, 0:4096], sem=f"wq{j}", wait=w)
                K.dma("dA", wqb[j][:, 4096:8192], wqkvT[og][:, 4096:8192], sem=f"wqa{j}", wait=w)
                wq_tok[j] = [K.tok(f"wq{j}"), K.tok(f"wqa{j}")]

            load_wq(4)
            load_wq(5)
            bank_tok = [None] * 4
            slot_q_free = [None, None]
            slot_h_free = [None, None]
            slot_l_free = [None, None]
            m = 0
            tok_a2a_kv_holder = [None]
            og_order = [4, 5, 6, 7, 0, 1, 2, 3]
            for og in og_order:
                j = og % 2
                for tg in range(2):
                    for oc4 in range(4):
                        oc = og * 4 + oc4
                        bk = pb[oc4]
                        K.wait("tensor", *(wq_tok[j] + [bank_tok[oc4]]))
                        for kc in range(16):
                            K.t("matmul", bk[:, 0:512],
                                wqb[j][:, kc * 512 + oc4 * 128:kc * 512 + (oc4 + 1) * 128],
                                xnT[kc][:, tg * 512:(tg + 1) * 512],
                                start=(kc == 0), stop=(kc == 15))
                        tmm = K.tok("T")
                        sq_ = m % 2
                        t_qf = K.s("activation", stq[sq_][:], bk[:, 0:512], AF.Identity,
                                   bias=qkvb_sb[:, oc:oc + 1], scale=1.0,
                                   wait=(tmm, slot_q_free[sq_]))
                        bank_tok[oc4] = t_qf
                        if oc < 24:
                            t_hi = K.s("activation", sthi[sq_][:], stq[sq_][:], AF.Identity,
                                       bias=0.0, scale=1.0, wait=(t_qf, slot_h_free[sq_]))
                            t_lo = K.v("tensor_sub", stlo[sq_][:], stq[sq_][:], sthi[sq_][:],
                                       wait=(t_hi, slot_l_free[sq_]))
                            slot_q_free[sq_] = t_lo
                            if oc < 16:
                                dsts = [oc >> 1]
                                dstv = q16v
                                pre = "q"
                            else:
                                g = (oc - 16) >> 1
                                dsts = [2 * g, 2 * g + 1]
                                dstv = k16v
                                pre = "k"
                            sec = 0 + (oc & 1)
                            secl = 2 + (oc & 1)
                            for dj in dsts:
                                K.dma("dA", dstv[dj, sec, :, tg * 512:(tg + 1) * 512],
                                      sthi[sq_][:], sem=f"{pre}h{sq_}", wait=(t_hi,))
                                K.dma("dA", dstv[dj, secl, :, tg * 512:(tg + 1) * 512],
                                      stlo[sq_][:], sem=f"{pre}l{sq_}", wait=(t_lo,))
                            slot_h_free[sq_] = K.tok(f"{pre}h{sq_}")
                            slot_l_free[sq_] = K.tok(f"{pre}l{sq_}")
                        else:
                            g = (oc - 24) >> 1
                            half = (oc - 24) & 1
                            for dj in (2 * g, 2 * g + 1):
                                K.dma("dA", v32v[dj, half, :, tg * 512:(tg + 1) * 512],
                                      stq[sq_][:], sem=f"qv{sq_}", wait=(t_qf,))
                            slot_q_free[sq_] = K.tok(f"qv{sq_}")
                        m += 1
                oi = og_order.index(og)
                if oi == 3:
                    kv_toks = [K.tok(s) for s in list(K.cnt) if s.startswith(("kh", "kl", "qv"))]
                    cck = K.cc("AllToAll", ALU.bypass, RG, [cin_k16.opt()], [cout_k16.opt()],
                               wait=kv_toks)
                    ccv = K.cc("AllToAll", ALU.bypass, RG, [cin_v32.opt()], [cout_v32.opt()])
                    tok_a2a_kv_holder[0] = K.tok("CC")
                if oi + 2 < 8:
                    wq_cons[og_order[oi + 2] % 2] = K.tok("T")
                    load_wq(og_order[oi + 2])
            stage_toks = [K.tok(s) for s in list(K.cnt)
                          if s.startswith(("qh", "ql"))]
            tok_p1 = stage_toks
            sc_p1.close()
            sc_xnT.close()
            K.barrier(dma_sems=[s for s in K.cnt if s.startswith(("qh", "ql", "kh", "kl", "qv", "wq"))])

            if stage >= 2:
                ccq = K.cc("AllToAll", ALU.bypass, RG, [cin_q16.opt()], [cout_q16.opt()],
                           wait=stage_toks)
                tok_a2a_qkv = K.tok("CC")

        tok_a2a_o = None
        if stage >= 3:
            sc_p2 = ExitStack()
            qh = [sc_p2.enter_context(nc.sbuf_tensor(f"qh{c}", [128, S], F16)) for c in range(2)]
            ql = [sc_p2.enter_context(nc.sbuf_tensor(f"ql{c}", [128, S], F16)) for c in range(2)]
            kh = [sc_p2.enter_context(nc.sbuf_tensor(f"kh{c}", [128, S], F16)) for c in range(2)]
            kl = [sc_p2.enter_context(nc.sbuf_tensor(f"kl{c}", [128, S], F16)) for c in range(2)]
            vT = sc_p2.enter_context(nc.sbuf_tensor("vT", [128, 32 * 256], F32))
            s1b = sc_p2.enter_context(nc.sbuf_tensor("s1b", [128, S], F32))
            s2b = sc_p2.enter_context(nc.sbuf_tensor("s2b", [128, S], F32))
            s1c = sc_p2.enter_context(nc.sbuf_tensor("s1c", [128, S], F32))
            s2c = sc_p2.enter_context(nc.sbuf_tensor("s2c", [128, S], F32))
            vtmp = [s1b[:, 0:1024], s1b[:, 1024:2048]]
            ptT = [sc_p2.enter_context(nc.sbuf_tensor(f"ptT{j}", [128, 512], F32)) for j in range(2)]
            oS = [sc_p2.enter_context(nc.sbuf_tensor(f"oS{j}", [128, 512], F32)) for j in range(2)]
            osc4 = sc_p2.enter_context(nc.sbuf_tensor("osc4", [128, 1024], F32))
            osc = sc_p2.enter_context(nc.sbuf_tensor("osc", [128, 256], F32))
            co16 = cout_q16[:].rearrange("r (s p t) -> r s p t", s=4, p=128)
            ck16 = cout_k16[:].rearrange("r (s p t) -> r s p t", s=4, p=128)
            co32 = cout_v32[:].rearrange("r (s p t) -> r s p t", s=2, p=128)
            cino = cin_o[:].rearrange("r (c p t) -> r c p t", c=2, p=128)

            u_done = []
            u0_osh = None
            vtr_done = [None, None]
            vtr_prev_ev = None
            for u in range(2):
                w_ld = [tok_a2a_qkv] + u_done
                if u == 1:
                    w_ld += u0_osh
                for c in range(2):
                    for r4 in range(4):
                        r = 4 * u + r4
                        K.dma("dS", qh[c][:, r4 * 1024:(r4 + 1) * 1024], co16[r, 0 + c], sem=f"aq{u}", wait=w_ld)
                        K.dma("dS", ql[c][:, r4 * 1024:(r4 + 1) * 1024], co16[r, 2 + c], sem=f"aq{u}", wait=w_ld)
                        K.dma("dS", kh[c][:, r4 * 1024:(r4 + 1) * 1024], ck16[r, 0 + c], sem=f"aq{u}", wait=w_ld)
                        K.dma("dS", kl[c][:, r4 * 1024:(r4 + 1) * 1024], ck16[r, 2 + c], sem=f"aq{u}", wait=w_ld)
                tok_aq = K.tok(f"aq{u}")
                for half in range(2):
                    for r4 in range(4):
                        r = 4 * u + r4
                        jj = (half * 4 + r4) % 2
                        wv = w_ld + ([vtr_done[jj]] if vtr_done[jj] is not None else [])
                        vdma = K.dma("dS", vtmp[jj], co32[r, half], sem=f"av{u}_{half}_{r4}", wait=wv)
                        for grp in range(2):
                            wtr = [vdma, tok_idc] + u_done
                            if vtr_prev_ev is not None:
                                wtr.append(vtr_prev_ev)
                            for sl in range(4):
                                t8 = grp * 4 + sl
                                K.t("transpose", pb[7][:, sl * 128:(sl + 1) * 128],
                                    s1b[:, jj * 1024 + t8 * 128: jj * 1024 + (t8 + 1) * 128],
                                    ident[:], wait=wtr if sl == 0 else ())
                            tbk = K.tok("T")
                            kb0 = r4 * 8 + grp * 4
                            dst = vT[:].rearrange("p (k e) -> p k e", k=32)[
                                :, kb0:kb0 + 4, half * 128:half * 128 + 128]
                            vtr_prev_ev = K.v("tensor_copy", dst, pb[7][:, 0:512].rearrange(
                                "p (k e) -> p k e", k=4), wait=(tbk,))
                        vtr_done[jj] = K.tok("T")
                tok_vT = K.tok("V")

                tok_bounds = K.tok("V")

                prev_ptT_cons = [None, None]
                prev_ptT_ev = [None, None]
                prev_s_cons = [None, None]
                prev_sq = None
                prev_o_ev = None
                osc4_free = None
                SB1 = [s1b, s1c]
                SB2 = [s2b, s2c]
                state = [None, None]
                for qbi in range(33):
                    if qbi >= 1:
                        qb = qbi - 1
                        par = qb % 2
                        st = state[par]
                        cq = lambda j: sm2[:, 96 + 32 * par + j:96 + 32 * par + j + 1]
                        nkb = qb // 4 + 1
                        ncols = nkb * 512
                        sA, sB = SB1[par], SB2[par]
                        t_e1 = K.s("activation", sA[:, 0:ncols], sA[:, 0:ncols], AF.Exp,
                                   bias=cq(14), scale=1.0, accum_out=cq(1),
                                   wait=[st["ev1"], st["tmx"]])
                        t_e2 = K.s("activation", sB[:, 0:ncols], sB[:, 0:ncols], AF.Exp,
                                   bias=cq(15), scale=1.0, accum_out=cq(2), wait=(t_e1,))
                        t_a = K.v("reciprocal", cq(3), cq(1), wait=(t_e1,))
                        t_c0 = K.v("tensor_scalar_mul", sA[:, 0:ncols], sA[:, 0:ncols],
                                   cq(3), wait=(t_a,))
                        t_b0 = K.v("reciprocal", cq(4), cq(2), wait=(t_e2,))
                        t_b = K.v("tensor_mul", cq(5), cq(4), nlam_col[:], wait=(t_b0, tok_lam))
                        t_c2 = K.v("scalar_tensor_tensor", sA[:, 0:ncols], sB[:, 0:ncols],
                                   cq(5), sA[:, 0:ncols], ALU.mult, ALU.add,
                                   wait=(t_b, t_c0))
                        state[par]["c2"] = t_c2
                    if qbi < 32:
                        qb = qbi
                        par = qb % 2
                        cq = lambda j: sm2[:, 96 + 32 * par + j:96 + 32 * par + j + 1]
                        nkb = qb // 4 + 1
                        diag = qb // 4
                        sA, sB = SB1[par], SB2[par]
                        ev1 = [None] * nkb
                        ev2 = [None] * nkb
                        tmx = None
                        for kb in range(nkb):
                            b1 = pb[kb % 2]
                            b2 = pb[2 + kb % 2]
                            wmm = [tok_aq]
                            if prev_s_cons[par] is not None:
                                wmm.append(prev_s_cons[par])
                            if kb >= 2:
                                wmm += [ev1[kb - 2], ev2[kb - 2]]
                            elif qbi >= 1:
                                wmm += [prev_ev1_last, prev_ev2_last]
                            K.wait("tensor", *wmm)
                            for (lh, rh, st_, sp_) in ((qh, kh, True, False), (qh, kl, False, False), (ql, kh, False, True)):
                                K.t("matmul", b1[:, 0:512], lh[0][:, qb * 128:(qb + 1) * 128],
                                    rh[0][:, kb * 512:(kb + 1) * 512], start=st_, stop=sp_)
                            for (lh, rh, st_, sp_) in ((qh, kh, True, False), (qh, kl, False, False), (ql, kh, False, True)):
                                K.t("matmul", b2[:, 0:512], lh[1][:, qb * 128:(qb + 1) * 128],
                                    rh[1][:, kb * 512:(kb + 1) * 512], start=st_, stop=sp_)
                            tmm = K.tok("T")
                            if kb == diag:
                                msl = mask_sb[:, (qb % 4) * 512:(qb % 4 + 1) * 512]
                                ev1[kb] = K.v("scalar_tensor_tensor", sA[:, kb * 512:(kb + 1) * 512],
                                              b1[:, 0:512], SCALE, msl, ALU.mult, ALU.add,
                                              wait=(tmm,))
                                ev2[kb] = K.v("scalar_tensor_tensor", sB[:, kb * 512:(kb + 1) * 512],
                                              b2[:, 0:512], SCALE, msl, ALU.mult, ALU.add,
                                              wait=(tmm,))
                            else:
                                ev1[kb] = K.s("activation", sA[:, kb * 512:(kb + 1) * 512], b1[:, 0:512],
                                              AF.Identity, bias=0.0, scale=SCALE, wait=(tmm,))
                                ev2[kb] = K.v("tensor_scalar_mul", sB[:, kb * 512:(kb + 1) * 512],
                                              b2[:, 0:512], SCALE, wait=(tmm,))
                            K.v("reduce_max", cq(16 + (kb % 2)), sA[:, kb * 512:(kb + 1) * 512],
                                axis=AX.X, negate=True, wait=(ev1[kb],))
                            if kb == 0:
                                K.v("tensor_copy", cq(14), cq(16), wait=(K.tok("V"), K.tok("SC")))
                            else:
                                K.v("tensor_tensor", cq(14), cq(14), cq(16 + (kb % 2)), ALU.min, wait=(K.tok("V"),))
                            K.v("reduce_max", cq(18 + (kb % 2)), sB[:, kb * 512:(kb + 1) * 512],
                                axis=AX.X, negate=True, wait=(ev2[kb],))
                            if kb == 0:
                                tmx = K.v("tensor_copy", cq(15), cq(18), wait=(K.tok("V"), K.tok("SC")))
                            else:
                                tmx = K.v("tensor_tensor", cq(15), cq(15), cq(18 + (kb % 2)), ALU.min, wait=(K.tok("V"),))
                        prev_ev1_last, prev_ev2_last = ev1[-1], ev2[-1]
                        state[par] = {"ev1": ev1[-1], "ev2": ev2[-1], "tmx": tmx}
                    if qbi >= 1:
                        qb = qbi - 1
                        par = qb % 2
                        cq = lambda j: sm2[:, 96 + 32 * par + j:96 + 32 * par + j + 1]
                        sA = SB1[par]
                        t_c2 = state[par]["c2"]
                        n128 = qb + 1
                        for g4 in range(0, n128, 4):
                            cnt4 = min(4, n128 - g4)
                            sl_ = (g4 // 4) % 2
                            bkt = pb[4 + sl_]
                            wtr = [t_c2, prev_ptT_ev[sl_]]
                            for j in range(cnt4):
                                kb1 = g4 + j
                                K.t("transpose", bkt[:, j * 128:(j + 1) * 128],
                                    sA[:, kb1 * 128:(kb1 + 1) * 128], ident[:],
                                    wait=wtr if j == 0 else ())
                            trs_t = K.tok("T")
                            t_ev = K.v("tensor_copy", ptT[sl_][:, 0:cnt4 * 128], bkt[:, 0:cnt4 * 128],
                                       wait=[trs_t, prev_ptT_cons[sl_]])
                            prev_ptT_ev[sl_] = t_ev
                            K.wait("tensor", t_ev, tok_vT, prev_sq)
                            for j in range(cnt4):
                                kb1 = g4 + j
                                K.t("matmul", pb[6][:, 0:256], ptT[sl_][:, j * 128:(j + 1) * 128],
                                    vT[:, kb1 * 256:(kb1 + 1) * 256],
                                    start=(kb1 == 0), stop=(kb1 == n128 - 1))
                            prev_ptT_cons[sl_] = K.tok("T")
                        t_pv = K.tok("T")
                        prev_s_cons[par] = t_pv
                        t_sq = K.s("activation", osc[:], pb[6][:, 0:256], AF.Square,
                                   accum_out=cq(6), wait=[t_pv, prev_o_ev])
                        prev_sq = t_sq
                        tu2 = K.s("activation", cq(8), cq(6), AF.Ln, bias=epsc[:, 0:1],
                                  scale=epsc[:, 1:2], wait=(t_sq, tok_eps))
                        tl2 = K.s("activation", cq(9), cq(8), AF.Exp, scale=-0.5, wait=(tu2,))
                        t_o2 = K.v("scalar_tensor_tensor", osc4[:, (qb % 4) * 256:(qb % 4 + 1) * 256],
                                   pb[6][:, 0:256], cq(9), subln_bc[:],
                                   ALU.mult, ALU.mult, wait=[tl2, t_sq, osc4_free, t_sbc])
                        if qb % 4 == 3:
                            qb0_ = qb - 3
                            K.wait("tensor", t_o2, prev_o_ev)
                            for half4 in range(2):
                                if half4 == 1:
                                    K.wait("tensor", prev_o_ev)
                                for j4 in range(4):
                                    qq = qb0_ + (half4 * 4 + j4) // 2
                                    cch = (half4 * 4 + j4) % 2
                                    K.t("transpose", pb[7][:, j4 * 128:(j4 + 1) * 128],
                                        osc4[:, (qq % 4) * 256 + cch * 128:(qq % 4) * 256 + (cch + 1) * 128],
                                        ident[:])
                                t_tro = K.tok("T")
                                slo_ = half4
                                t_oev2 = K.v("tensor_copy", oS[slo_][:, 0:512], pb[7][:, 0:512],
                                             wait=[t_tro, (f"osh{u}_{slo_}", K.cnt.get(f"osh{u}_{slo_}", 0))])
                                prev_o_ev = t_oev2
                                r4o = qb0_ // 8
                                for j4 in range(4):
                                    qq = qb0_ + (half4 * 4 + j4) // 2
                                    cch = (half4 * 4 + j4) % 2
                                    K.dma("dA", cino[4 * u + r4o, cch][:, (qq % 8) * 128:(qq % 8 + 1) * 128],
                                          oS[slo_][:, j4 * 128:(j4 + 1) * 128], sem=f"osh{u}_{slo_}",
                                          wait=(t_oev2,))
                            osc4_free = K.tok("T")
                if u == 0:
                    u0_osh = [K.tok("osh0_0"), K.tok("osh0_1")]
                    u_done = [K.tok("V"), K.tok("SC"), K.tok("T")]
                    prev_ptT_cons = [None, None]
            tok_osh1 = [K.tok("osh1_0"), K.tok("osh1_1")]
            cc_o = K.cc("AllToAll", ALU.bypass, RG, [cin_o.opt()], [cout_o.opt()],
                        wait=u0_osh + tok_osh1)
            tok_a2a_o = K.tok("CC")
            sc_p2.close()
            sc_mask.close()
            K.barrier(dma_sems=[sname for sname in K.cnt if sname.startswith(("aq", "av", "osh"))])

        if "d_oT" in dbg_out and stage >= 3:
            coo = cout_o[:].rearrange("r (c p t) -> r c p t", c=2, p=128)
            tmpo = nc.alloc_sbuf_tensor("tmpo", [128, TOK], F32)
            for r in range(8):
                for cch in range(2):
                    tld = K.dma("dS", tmpo[:], coo[r, cch], sem="dbo",
                                wait=[tok_a2a_o, ("dbo2", K.cnt.get("dbo2", 0))])
                    K.dma("dS", dbg_out["d_oT"][(r * 2 + cch) * 128:(r * 2 + cch + 1) * 128, :],
                          tmpo[:], sem="dbo2", wait=(tld,))

        if "d_xnT" in dbg_out:
            for f in range(16):
                K.dma("dS", dbg_out["d_xnT"][f * 128:(f + 1) * 128, :], xnT[f][:],
                      sem="dbg1", wait=(K.tok("V"),))
        if "d_xq" in dbg_out:
            xqf = xnb
            for f in range(16):
                tcp = K.v("tensor_copy", xqf[f % 2][:, 0:TOK], xqT[f][:],
                          wait=(K.tok("V"), (f"dbg2_{f % 2}", 16 * (f // 2)) if f >= 2 else None))
                K.dma("dS", dbg_out["d_xq"][f * 128:(f + 1) * 128, :], xqf[f % 2][:, 0:TOK],
                      sem=f"dbg2_{f % 2}", wait=(tcp,))
        if "d_dq" in dbg_out:
            K.dma("dS", dbg_out["d_dq"][0:1, :], dqrow[0:1, :], sem="dbg3", wait=(tok_dqrow,))

        if "d_q16" in dbg_out and stage >= 2:
            tmpf = nc.alloc_sbuf_tensor("tmpf", [128, NT], F32)
            tmph = nc.alloc_sbuf_tensor("tmph", [128, NT], F16)
            tmpl = nc.alloc_sbuf_tensor("tmpl", [128, NT], F16)
            co16 = cout_q16[:].rearrange("r (s p t) -> r s p t", s=8, p=128)
            co32 = cout_v32[:].rearrange("r (s p t) -> r s p t", s=2, p=128)
            for sec in range(4):
                vv_ = [co16, co16, ck16, ck16][sec]
                shi = [0, 1, 0, 1][sec]
                slo = [2, 3, 2, 3][sec]
                for r in range(8):
                    K.dma("dS", tmph[:, r * 1024:(r + 1) * 1024], vv_[r, shi], sem="dbh",
                          wait=(tok_a2a_qkv, K.tok("V")))
                    K.dma("dS", tmpl[:, r * 1024:(r + 1) * 1024], vv_[r, slo], sem="dbh")
                tld = K.tok("dbh")
                tadd = K.v("tensor_add", tmpf[:], tmph[:], tmpl[:],
                           wait=(tld, ("dbq", K.cnt.get("dbq", 0))))
                K.dma("dS", dbg_out["d_q16"][sec * 128:(sec + 1) * 128, :], tmpf[:],
                      sem="dbq", wait=(tadd,))
            for half in range(2):
                for r in range(8):
                    K.dma("dS", tmpf[:, r * 1024:(r + 1) * 1024], co32[r, half], sem="dbv",
                          wait=(tok_a2a_qkv, ("dbq", K.cnt.get("dbq", 0)), K.tok("V")))
                tld = K.tok("dbv")
                K.dma("dS", dbg_out["d_q16"][(4 + half) * 128:(5 + half) * 128, :], tmpf[:],
                      sem="dbq", wait=(tld,))

        tok_ffn_scr = None
        if stage >= 4:
            if stage < 90 and not (stage >= 3):
                pass
            if not (stage >= 3):
                sc_mask.close()
            sc_p3 = ExitStack()
            W2p = sc_p3.enter_context(nc.sbuf_tensor("W2p", [128, 16384], F32))
            Zb = sc_p3.enter_context(nc.sbuf_tensor("Zb", [128, 16384], F32))
            W2qp = Zb[:].bitcast(BF16)
            wstr = [sc_p3.enter_context(nc.sbuf_tensor(f"wstr{j}", [128, 4096], BF16)) for j in range(4)]
            xqs = sc_p3.enter_context(nc.sbuf_tensor("xqs", [128, 4096], BF16))
            itile = [sc_p3.enter_context(nc.sbuf_tensor(f"it{j}", [128, 256], F32)) for j in range(4)]
            utile = [sc_p3.enter_context(nc.sbuf_tensor(f"ut{j}", [128, 256], F32)) for j in range(2)]
            sgt = sc_p3.enter_context(nc.sbuf_tensor("sgt", [128, 256], F32))
            sut = sc_p3.enter_context(nc.sbuf_tensor("sut", [128, 256], F32))
            sdt = sc_p3.enter_context(nc.sbuf_tensor("sdt", [128, 256], F32))
            scl = sc_p3.enter_context(nc.sbuf_tensor("scl", [128, 128], F32))
            zti = [sc_p3.enter_context(nc.sbuf_tensor(f"zti{j}", [128, 512], F32)) for j in range(2)]
            fout = [sc_p3.enter_context(nc.sbuf_tensor(f"fout{j}", [128, 256], F32)) for j in range(2)]
            cowq_blk = cout_wq[:].rearrange("w (s p f) -> w s p f", s=4, p=128)
            xqv = xq_scr[:]

            gu_cons = [None, None]
            dn_cons = None
            fg_done = None
            pb03_free = None
            it_free = [None] * 4
            ut_free = [None] * 2
            fo_free = [None] * 2
            prev_gbc = None
            st2_zti_cons = [None, None]
            for g in range(NGF):
                K.dma("dS", xqs[:], xq_scr[g], sem="fxq",
                      wait=[tok_xq_scr] + ([fg_done] if fg_done else []))
                tok_xqs = K.tok("fxq")
                K.wait("tensor", tok_dqgu, prev_gbc)
                t_g1 = K.t("matmul", pb[6][:, 0:256], onerow[:], dqg_row[0:1, g * 256:(g + 1) * 256],
                           start=True, stop=True)
                t_g2 = K.t("matmul", pb[7][:, 0:256], onerow[:], dqu_row[0:1, g * 256:(g + 1) * 256],
                           start=True, stop=True)
                t_gb = K.v("tensor_copy", sgt[:], pb[6][:, 0:256], wait=[t_g1, prev_gbc])
                t_ub = K.v("tensor_copy", sut[:], pb[7][:, 0:256], wait=(t_g2,))
                prev_gbc = K.tok("V")
                st1_sl = 0
                st1_ev_last = None
                st1_bank_tok = [None, None]
                for ob in range(32):
                    jj = ob % 2
                    r_ = ob >> 2
                    sub = ob & 3
                    wld = [tok_ag] + ([gu_cons[jj]] if gu_cons[jj] else []) \
                          + ([dn_cons] if (ob < 2 and dn_cons) else [])
                    K.dma("dS", wstr[jj][:], cowq_blk[3 * r_, sub], sem=f"fg{jj}", wait=wld)
                    K.dma("dA", wstr[2 + jj][:], cowq_blk[3 * r_ + 1, sub], sem=f"fu{jj}", wait=wld)
                    tok_wg = K.tok(f"fg{jj}")
                    tok_wu = K.tok(f"fu{jj}")
                    if "d_ops" in dbg_out and g == 0 and ob == 0:
                        tmpo1 = W2p[:, 0:4096]
                        tc1 = K.v("tensor_copy", tmpo1, xqs[:], wait=(tok_xqs,))
                        K.dma("dS", dbg_out["d_ops"][0:128, :], tmpo1, sem="dbo1", wait=(tc1,))
                        tc2 = K.v("tensor_copy", tmpo1, wstr[jj][:], wait=(tok_wg, ("dbo1", 16)))
                        K.dma("dS", dbg_out["d_ops"][128:256, :], tmpo1, sem="dbo1", wait=(tc2,))
                        K.wait("vector", ("dbo1", 32))
                    K.wait("tensor", tok_wg, tok_wu, tok_xqs, pb03_free)
                    for oc2 in range(2):
                        for kc in range(16):
                            K.t("matmul", pb[oc2][:, 0:256],
                                wstr[jj][:, kc * 256 + oc2 * 128: kc * 256 + oc2 * 128 + 128],
                                xqs[:, kc * 256:(kc + 1) * 256],
                                start=(kc == 0), stop=(kc == 15))
                        for kc in range(16):
                            K.t("matmul", pb[2 + oc2][:, 0:256],
                                wstr[2 + jj][:, kc * 256 + oc2 * 128: kc * 256 + oc2 * 128 + 128],
                                xqs[:, kc * 256:(kc + 1) * 256],
                                start=(kc == 0), stop=(kc == 15))
                    gu_cons[jj] = K.tok("T")
                    tmmgu = gu_cons[jj]
                    for oc2 in range(2):
                        a_ = ob * 2 + oc2
                        isl = a_ % 4
                        usl = a_ % 2
                        t_ga = K.v("tensor_mul", itile[isl][:], pb[oc2][:, 0:256], sgt[:],
                                   wait=[tmmgu, t_gb, it_free[isl]])
                        if "d_g0" in dbg_out and g == 0 and ob == 0 and oc2 == 0:
                            K.dma("dS", dbg_out["d_g0"][0:128, :], itile[isl][:], sem="dbg0", wait=(t_ga,))
                            K.wait("scalar", ("dbg0", 16))
                        t_si = K.s("activation", itile[isl][:], itile[isl][:], AF.Silu, wait=(t_ga,))
                        t_iv = K.v("tensor_mul", itile[isl][:], itile[isl][:], pb[2 + oc2][:, 0:256],
                                   wait=(t_si, tmmgu))
                        sb_ = 4 + (a_ % 2)
                        K.wait("tensor", t_iv, st1_bank_tok[a_ % 2])
                        t_s1 = K.t("matmul", pb[sb_][:, 0:256], h128[:], itile[isl][:],
                                   start=True, stop=True)
                        it_free[isl] = t_s1
                        if "d_int" in dbg_out and g == 0:
                            dtok = K.dma("dS", dbg_out["d_int"][a_ * 128:(a_ + 1) * 128, :],
                                         itile[isl][:], sem=f"dbi{isl}", wait=(t_iv,))
                            it_free[isl] = dtok
                        st1_ev_last = K.v("tensor_copy", Zb[:, a_::64],
                                          pb[sb_][:, 0:256], wait=(t_s1,))
                        st1_bank_tok[a_ % 2] = st1_ev_last
                    pb03_free = K.tok("V")
                tok_z = K.tok("V")
                st2_bank = [None, None]
                for p4 in range(32):
                    slz = p4 % 2
                    K.wait("tensor", tok_z, st2_zti_cons[slz])
                    for j4 in range(4):
                        p_ = p4 * 4 + j4
                        K.t("transpose", pb[0 + slz][:, j4 * 128:(j4 + 1) * 128],
                            Zb[:, p_ * 128:(p_ + 1) * 128], ident[:])
                    ttr = K.tok("T")
                    tze = K.v("tensor_copy", zti[slz][:], pb[0 + slz][:, 0:512],
                              wait=[ttr, st2_zti_cons[slz]])
                    K.wait("tensor", tze, st2_bank[slz])
                    for j4 in range(4):
                        K.t("matmul", pb[2 + slz][:, j4 * 128:(j4 + 1) * 128], bd64[:],
                            zti[slz][:, j4 * 128:(j4 + 1) * 128],
                            start=(j4 == 0), stop=(j4 == 3), skip_group_check=True)
                    tm2 = K.tok("T")
                    st2_zti_cons[slz] = tm2
                    w2_ev = K.v("tensor_copy", W2p[:, p4 * 512:(p4 + 1) * 512], pb[2 + slz][:, 0:512],
                                wait=(tm2,))
                    st2_bank[slz] = w2_ev
                tok_w2 = K.tok("V")
                t_cm = K.v("tensor_reduce", sm2[:, 120:121] if False else scl[:, 0:128],
                           W2p[:].rearrange("p (q b) -> p q b", q=128), axis=AX.X, op=ALU.max,
                           apply_absolute_value=True, wait=(tok_w2,))
                K.wait("tensor", t_cm)
                t_trc = K.t("transpose", pb[6][:, 0:128], scl[:, 0:128], ident[:])
                t_cm2 = K.v("tensor_copy", scl[:, 0:128], pb[6][:, 0:128], wait=(t_cm, t_trc))
                t_M = K.v("tensor_reduce", sm2[:, 120:122],
                          scl[:, 0:128].rearrange("p (t a) -> p t a", t=2), axis=AX.X, op=ALU.max,
                          wait=(t_cm2,))
                t_Mc = K.v("tensor_scalar_max", sm2[:, 120:122], sm2[:, 120:122], EPS, wait=(K.tok("V"),))
                t_s = K.v("reciprocal", sm2[:, 122:124], sm2[:, 120:122], wait=(t_Mc,))
                t_s2 = K.v("tensor_scalar_mul", sm2[:, 122:124], sm2[:, 122:124], 128.0, wait=(K.tok("V"),))
                t_dq = K.v("tensor_scalar", sm2[:, 124:126], sm2[:, 120:122], swcol[:, 3:4], None,
                           ALU.mult, wait=(t_Mc, t_cdc))
                K.wait("tensor", t_s2, t_dq)
                t_ts = K.t("transpose", pb[6][0:2, 0:128], sm2[:, 122:124], ident[:])
                t_sr = K.v("tensor_copy", rowbuf[0:2, 0:128], pb[6][0:2, 0:128], wait=(t_ts,))
                t_t1 = K.t("transpose", pb[6][0:1, 128:256], sm2[:, 124:125], ident[:], wait=(t_sr,))
                t_t2 = K.t("transpose", pb[6][0:1, 256:384], sm2[:, 125:126], ident[:])
                t_f1 = K.v("tensor_copy", rowbuf[0:1, 128:384:2], pb[6][0:1, 128:256], wait=(t_t1, t_t2))
                t_f2 = K.v("tensor_copy", rowbuf[0:1, 129:384:2], pb[6][0:1, 256:384], wait=(K.tok("V"),))
                K.wait("tensor", t_sr, t_f1, t_f2)
                t_dm = K.t("matmul", pb[7][:, 0:128], p2c[:], rowbuf[0:2, 0:128][0:2, :],
                           start=True, stop=True)
                t_scl = K.v("tensor_copy", scl[:, 0:128], pb[7][:, 0:128], wait=(t_dm, t_M))
                t_sd = K.t("matmul", pb[6][:, 0:256], onerow[:], rowbuf[0:1, 128:384],
                           start=True, stop=True)
                t_sdt0 = K.v("tensor_copy", sdt[:], pb[6][:, 0:256], wait=(t_sd,))
                t_sdt = K.v("tensor_mul", sdt[:], sdt[:], sut[:], wait=(t_sdt0, t_ub))
                q1 = K.v("tensor_tensor", W2p[:].rearrange("p (q b) -> p q b", q=128),
                         W2p[:].rearrange("p (q b) -> p q b", q=128),
                         scl[:, 0:128].rearrange("p q -> p q 1" if False else "p (q o) -> p q o", o=1).broadcast_to([128, 128, 128]),
                         ALU.mult, wait=(t_scl, tok_w2))
                q2 = K.v("tensor_scalar_add", W2p[:], W2p[:], MAGIC, wait=(q1,))
                q3 = K.v("tensor_scalar", W2p[:], W2p[:], MAGIC, 127.0,
                         ALU.subtract, ALU.min, wait=(q2,))
                tq_bank = [None, None]
                for p4 in range(32):
                    slz = p4 % 2
                    K.wait("tensor", q3, tq_bank[slz])
                    for j4 in range(4):
                        p_ = p4 * 4 + j4
                        K.t("transpose", pb[2 + slz][:, j4 * 128:(j4 + 1) * 128],
                            W2p[:, p_ * 128:(p_ + 1) * 128], ident[:])
                    ttr = K.tok("T")
                    tq_ev = K.v("tensor_copy", W2qp[:, 16384 + p4 * 512:16384 + (p4 + 1) * 512],
                                pb[2 + slz][:, 0:512], wait=[ttr, tq_bank[slz]])
                    tq_bank[slz] = tq_ev
                tok_w2q = K.tok("V")
                dn_bank = [None, None]
                for ocp in range(8):
                    wldd = [tok_ag, gu_cons[0], gu_cons[1]]
                    if ocp > 0:
                        wldd.append(dn_b_cons)
                    elif dn_cons is not None:
                        wldd.append(dn_cons)
                    for qtr in range(4):
                        K.dma("dS" if qtr % 2 == 0 else "dA", wstr[qtr][:],
                              cowq_blk[3 * ocp + 2, qtr], sem="fd", wait=wldd)
                    tok_wd = K.tok("fd")
                    for oc2 in range(2):
                        oc = ocp * 2 + oc2
                        db_ = oc % 2
                        K.wait("tensor", tok_wd, tok_w2q, dn_bank[db_])
                        for a_ in range(64):
                            K.t("matmul", pb[4 + db_][:, 0:256],
                                wstr[a_ >> 4][:, (a_ & 15) * 256 + oc2 * 128:(a_ & 15) * 256 + oc2 * 128 + 128],
                                W2qp[:, 16384 + a_:16384 + 16384:128][:, 0:256] if False else
                                W2qp[:].rearrange("p (h q t a) -> p h q t a", h=2, q=128, t=2)[:, 1, :, :, a_].rearrange("p q t -> p (q t)"),
                                start=(a_ == 0), stop=(a_ == 63))
                        tdn = K.tok("T")
                        fsl = oc % 2
                        t_fo = K.v("tensor_mul", fout[fsl][:], pb[4 + db_][:, 0:256], sdt[:],
                                   wait=[tdn, t_sdt, fo_free[fsl]])
                        dn_bank[db_] = t_fo
                        fo_free[fsl] = K.dma("dA", ffn_scr[oc][:, g * 256:(g + 1) * 256],
                                             fout[fsl][:], sem=f"ffs{fsl}", wait=(t_fo,))
                    dn_b_cons = K.tok("T")
                dn_cons = K.tok("T")
                fg_done = K.tok("T")
            tok_ffn_scr = None
            for _t in [K.tok("ffs0"), K.tok("ffs1")]:
                pass
            tok_ffn_scr = [K.tok("ffs0"), K.tok("ffs1")]
            sc_p3.close()
            K.barrier(dma_sems=["fxq", "fg0", "fg1", "fu0", "fu1", "fd", "ffs0", "ffs1"])

        if "d_wq" in dbg_out and stage >= 4:
            tmpw = nc.alloc_sbuf_tensor("tmpw", [128, 1024], F32)
            tmpwb = nc.alloc_sbuf_tensor("tmpwb", [128, 1024], BF16)
            cowq2 = cout_wq[:].rearrange("w (k p c) -> w k p c", k=16, p=128)
            for kc in range(16):
                tld = K.dma("dS", tmpwb[:], cowq2[0, kc], sem="dbw",
                            wait=[tok_ag, ("dbw2", K.cnt.get("dbw2", 0)), K.tok("V")])
                tcv = K.v("tensor_copy", tmpw[:], tmpwb[:], wait=(tld,))
                K.dma("dS", dbg_out["d_wq"][kc * 128:(kc + 1) * 128, :], tmpw[:], sem="dbw2", wait=(tcv,))
            K.dma("dS", dbg_out["d_wq"][2048:2049, 0:8], mean_sb[0:1, 0:8], sem="dbw3",
                  wait=(K.tok("V"), t_cdc))
        if "d_ffn" in dbg_out and stage >= 4:
            tmpf3 = nc.alloc_sbuf_tensor("tmpf3", [128, TOK], F32)
            for f in range(16):
                tld = K.dma("dS", tmpf3[:], ffn_scr[f], sem="dbf",
                            wait=tok_ffn_scr + [("dbf2", K.cnt.get("dbf2", 0))])
                K.dma("dS", dbg_out["d_ffn"][f * 128:(f + 1) * 128, :], tmpf3[:], sem="dbf2", wait=(tld,))

        if stage >= 5:
            sc_p4 = ExitStack()
            ybuf = [sc_p4.enter_context(nc.sbuf_tensor(f"yb{k}", [128, TOK], F32)) for k in range(16)]
            sc_p4a = ExitStack()
            oTf = [sc_p4a.enter_context(nc.sbuf_tensor(f"oTf{k}", [128, TOK], F32)) for k in range(16)]
            wo = sc_p4a.enter_context(nc.sbuf_tensor("wo", [128, 8192], F32))
            s4a = [sc_p4a.enter_context(nc.sbuf_tensor(f"s4a{j}", [128, 512], F32)) for j in range(2)]
            fnt = [sc_p4a.enter_context(nc.sbuf_tensor(f"fnt{j}", [128, 512], F32)) for j in range(2)]
            coo = cout_o[:].rearrange("r (c p t) -> r c p t", c=2, p=128)
            for k in range(16):
                K.dma("dS", oTf[k][:], coo[k // 2, k % 2], sem="p4o", wait=(tok_a2a_o,))
            tok_oT = K.tok("p4o")
            wo_tok = None
            wo_cons = None
            bank_tok4 = [None] * 4
            s4_free = [None, None]
            fnt_free = [None, None]
            for og in range(4):
                wldw = [wo_cons] if wo_cons else []
                K.dma("dS", wo[:], owT[og], sem="p4w", wait=wldw)
                wo_tok = K.tok("p4w")
                for tg in range(2):
                    for oc4 in range(4):
                        oc = og * 4 + oc4
                        K.wait("tensor", wo_tok, tok_oT, bank_tok4[oc4])
                        for kc in range(16):
                            K.t("matmul", pb[oc4][:, 0:512],
                                wo[:, kc * 512 + oc4 * 128:kc * 512 + (oc4 + 1) * 128],
                                oTf[kc][:, tg * 512:(tg + 1) * 512],
                                start=(kc == 0), stop=(kc == 15))
                        tmm4 = K.tok("T")
                        ssl = (tg * 4 + oc4) % 2
                        t_b4 = K.s("activation", s4a[ssl][:], pb[oc4][:, 0:512], AF.Identity,
                                   bias=outb_sb[:, oc:oc + 1], scale=1.0,
                                   wait=[tmm4, s4_free[ssl]])
                        bank_tok4[oc4] = t_b4
                        fnl = K.dma("dA", fnt[ssl][:], ffn_scr[oc][:, tg * 512:(tg + 1) * 512],
                                    sem=f"p4f{ssl}", wait=(tok_ffn_scr or []) + [fnt_free[ssl]])
                        t_a4 = K.v("tensor_add", ybuf[oc][:, tg * 512:(tg + 1) * 512],
                                   s4a[ssl][:], fnt[ssl][:], wait=[t_b4, fnl])
                        s4_free[ssl] = t_a4
                        fnt_free[ssl] = t_a4
                wo_cons = K.tok("T")
            tok_yb = K.tok("V")
            sc_p4a.close()
            K.barrier(dma_sems=["p4o", "p4w", "p4f0", "p4f1"])
            xrl = [sc_p4.enter_context(nc.sbuf_tensor(f"xrl{j}", [128, DM], F32)) for j in range(2)]
            yst = [sc_p4.enter_context(nc.sbuf_tensor(f"yst{j}", [128, DM], F32)) for j in range(2)]
            xld = [None] * NTILE
            for i in range(2):
                xld[i] = K.dma("dS", xrl[i][:], x_in[i * 128:(i + 1) * 128, :], sem=f"p4x{i % 2}")
            y_done_v = [None, None]
            for i in range(NTILE):
                bs4 = i % 2
                evs4 = []
                for bg in range(4):
                    bk = pb[4 + bg]
                    w = [tok_yb]
                    if i >= 1:
                        w.append(prev_evs4_last)
                    for sl in range(4):
                        f = bg * 4 + sl
                        K.t("transpose", bk[:, sl * 128:(sl + 1) * 128],
                            ybuf[f][:, i * 128:(i + 1) * 128], ident[:],
                            wait=w if sl == 0 else ())
                    tbk = K.tok("T")
                    evs4.append(K.v("tensor_add", yst[bs4][:, bg * 512:(bg + 1) * 512],
                                    bk[:, 0:512], xrl[bs4][:, bg * 512:(bg + 1) * 512],
                                    wait=[tbk, xld[i], y_done_v[bs4]]))
                prev_evs4_last = evs4[-1]
                tok_y_i = K.tok("V")
                ydma = K.dma("dA", y_out[i * 128:(i + 1) * 128, :], yst[bs4][:], sem=f"p4y{bs4}",
                             wait=(tok_y_i,))
                y_done_v[bs4] = ydma
                if i + 2 < NTILE:
                    xld[i + 2] = K.dma("dS", xrl[bs4][:], x_in[(i + 2) * 128:(i + 3) * 128, :],
                                       sem=f"p4x{bs4}", wait=(tok_y_i,))


        if stage < 5:
            zf = nc.alloc_sbuf_tensor("zf", [128, DM], F32)
            tz = K.v("memset", zf[:], 0.0, wait=(K.tok("V"), K.tok("SC"), K.tok("T")))
            for i in range(NTILE):
                K.dma("dS", y_out[i * 128:(i + 1) * 128, :], zf[:], sem="yw", wait=(tz,))
        for sname, cval in list(K.cnt.items()):
            if sname not in ("V", "SC", "T", "CC") and cval > 0:
                K.wait("sync", (sname, cval))
    return nc, dbg_out


def make_consts():
    ident = np.eye(128, dtype=np.float32)
    idx = np.arange(128)
    pc = np.array([bin(i & j).count("1") for i in idx for j in idx]).reshape(128, 128)
    h128 = ((-1.0) ** pc).astype(np.float32)
    i64 = np.arange(64)
    pc64 = np.array([bin(a & b).count("1") for a in i64 for b in i64]).reshape(64, 64)
    h64 = ((-1.0) ** pc64).astype(np.float32)
    bd64 = np.zeros((128, 128), np.float32)
    bd64[:64, :64] = h64
    bd64[64:, 64:] = h64
    p2 = np.zeros((2, 128), np.float32)
    p2[0, :64] = 1.0
    p2[1, 64:] = 1.0
    cmat = np.concatenate([ident, h128, bd64, p2], axis=0)
    masks = np.zeros((4, 128, 512), np.float32)
    for m_ in range(4):
        for p in range(128):
            masks[m_, p, m_ * 128 + p + 1:] = NEG
    return cmat, masks.reshape(512, 512)


def prep_inputs(inputs):
    x = np.ascontiguousarray(np.asarray(inputs["L0_attn_hidden_states"], np.float32).reshape(NT, DM))
    _wq = np.asarray(inputs["L0_attn_Wqkv_w"], np.float32).T
    wqkvT = np.ascontiguousarray(_wq.reshape(16, 128, 8, 512).transpose(2, 1, 0, 3)).reshape(8, 128, 8192)
    qkvb = np.ascontiguousarray(np.asarray(inputs["L0_attn_Wqkv_b"], np.float32).reshape(32, 128).T)
    _ow = np.asarray(inputs["L0_attn_out_w"], np.float32).T
    owT = np.ascontiguousarray(_ow.reshape(16, 128, 4, 512).transpose(2, 1, 0, 3)).reshape(4, 128, 8192)
    outb = np.ascontiguousarray(np.asarray(inputs["L0_attn_out_b"], np.float32).reshape(16, 128).T)
    lamv = np.concatenate([np.asarray(inputs["L0_attn_lambda_q1"], np.float32),
                           np.asarray(inputs["L0_attn_lambda_k1"], np.float32),
                           np.asarray(inputs["L0_attn_lambda_q2"], np.float32),
                           np.asarray(inputs["L0_attn_lambda_k2"], np.float32)]).reshape(1, 512)
    sublnw = np.asarray(inputs["L0_attn_subln_w"], np.float32).reshape(1, 256)
    wgT = np.asarray(inputs["L0_ffn_w_gate"], np.float32).T
    wuT = np.asarray(inputs["L0_ffn_w_up"], np.float32).T
    wdT = np.asarray(inputs["L0_ffn_w_down"], np.float32).T
    cmat, masks = make_consts()
    in_maps = []
    for cid in range(8):
        in_maps.append({
            "x": np.ascontiguousarray(x[cid * TOK:(cid + 1) * TOK]),
            "wqkvT": wqkvT,
            "qkv_b": qkvb,
            "owT": owT,
            "out_b": outb,
            "lamv": lamv,
            "sublnw": sublnw,
            "wgs": np.ascontiguousarray(wgT[:, cid * 1024:(cid + 1) * 1024]),
            "wus": np.ascontiguousarray(wuT[:, cid * 1024:(cid + 1) * 1024]),
            "wds": np.ascontiguousarray(wdT[:, cid * 256:(cid + 1) * 256]),
            "cmat": cmat,
            "masks": masks,
        })
    return in_maps


_NC_CACHE = {}


def _get_nc():
    if "nc" not in _NC_CACHE:
        _NC_CACHE["nc"] = build(stage=5, dbg=())[0]
    return _NC_CACHE["nc"]


def kernel(**inputs):
    in_maps = prep_inputs(inputs)
    nc = _get_nc()
    res = run_bass_kernel_spmd(nc, in_maps, core_ids=list(range(8)))
    y = np.concatenate([np.asarray(r["y"]) for r in res.results], axis=0)
    return np.ascontiguousarray(y.reshape(B, S, DM).astype(np.float32, copy=False))


# version 13 (final)
# speedup vs baseline: 1.0359x; 1.0100x over previous
import math
from contextlib import ExitStack
import numpy as np
import ml_dtypes
import concourse.bass as bass
import concourse.mybir as mybir
from concourse.bass_utils import run_bass_kernel_spmd
from concourse.aot_env import allow_ici_all_to_all

F32 = mybir.dt.float32
F16 = mybir.dt.float16
BF16 = mybir.dt.bfloat16
I32 = mybir.dt.int32
AF = mybir.ActivationFunctionType
ALU = mybir.AluOpType
AX = mybir.AxisListType

B, S, DM = 2, 4096, 2048
NT = B * S
TOK = NT // 8
NTILE = TOK // 128
QKVO = 4096
D = 128
E = 256
I_FF = 8192
LAMBDA_INIT = 0.8 - 0.6 * math.exp(-0.3)
SCALE = D ** -0.5
EPS = 1e-5
NEG = float(np.finfo(np.float32).min)
MAGIC = 12582912.0
TGF = 256
NGF = TOK // TGF


class Build:

    def __init__(self, nc):
        self.nc = nc
        self.sem = {}
        self.cnt = {}
        self.waited = {}
        for s in ["V", "SC", "T", "CC"]:
            self.sem[s] = nc.alloc_semaphore("sem_" + s)
            self.cnt[s] = 0
        for e in ["sync", "vector", "scalar", "tensor", "gpsimd"]:
            self.waited[e] = {}

    def dsem(self, name):
        if name not in self.sem:
            self.sem[name] = self.nc.alloc_semaphore("sem_" + name)
            self.cnt[name] = 0
        return name

    def tok(self, s):
        return (s, self.cnt[s])

    def _eng(self, name):
        return getattr(self.nc, name)

    def wait(self, engname, *tokens):
        w = self.waited[engname]
        for t in tokens:
            if t is None:
                continue
            s, v = t
            if v <= 0 or w.get(s, 0) >= v:
                continue
            self._eng(engname).wait_ge(self.sem[s], v)
            w[s] = v

    def bump(self, s, instr, k=1):
        instr.then_inc(self.sem[s], k)
        self.cnt[s] += k
        return (s, self.cnt[s])

    def dma(self, q, out, in_, sem=None, wait=()):
        eng = {"dS": "sync", "dA": "scalar", "dG": "gpsimd"}[q]
        s = self.dsem(sem if sem is not None else "d_" + q)
        self.wait(eng, *wait)
        return self.bump(s, self._eng(eng).dma_start(out=out, in_=in_), 16)

    def v(self, fn, *a, wait=(), **kw):
        self.wait("vector", *wait)
        return self.bump("V", getattr(self.nc.vector, fn)(*a, **kw))

    def s(self, fn, *a, wait=(), **kw):
        self.wait("scalar", *wait)
        return self.bump("SC", getattr(self.nc.scalar, fn)(*a, **kw))

    def t(self, fn, *a, wait=(), **kw):
        self.wait("tensor", *wait)
        return self.bump("T", getattr(self.nc.tensor, fn)(*a, **kw))

    def barrier(self, dma_sems=()):
        toks = [self.tok("V"), self.tok("SC"), self.tok("T")]
        toks += [(s, self.cnt[s]) for s in dma_sems if s in self.cnt]
        for e in ["sync", "vector", "scalar", "tensor"]:
            self.wait(e, *toks)

    def cc(self, kind, op, rg, ins, outs, wait=()):
        self.wait("gpsimd", *wait)
        return self.bump("CC", self.nc.gpsimd.collective_compute(
            kind, op, replica_groups=rg, ins=ins, outs=outs))


P0CUT = 99


def build(stage=9, dbg=(), dbg_qbmax=0):
    nc = bass.Bass("TRN2", target_bir_lowering=False, debug=False, num_devices=8)
    K = Build(nc)
    RG = [list(range(8))]

    x_in = nc.dram_tensor("x", [TOK, DM], F32, kind="ExternalInput").ap()
    wqkvT = nc.dram_tensor("wqkvT", [8, 128, 8192], F32, kind="ExternalInput").ap()
    qkv_b = nc.dram_tensor("qkv_b", [128, 32], F32, kind="ExternalInput").ap()
    owT = nc.dram_tensor("owT", [4, 128, 8192], F32, kind="ExternalInput").ap()
    out_b = nc.dram_tensor("out_b", [128, 16], F32, kind="ExternalInput").ap()
    lamv = nc.dram_tensor("lamv", [1, 512], F32, kind="ExternalInput").ap()
    sublnw = nc.dram_tensor("sublnw", [1, 256], F32, kind="ExternalInput").ap()
    wgs = nc.dram_tensor("wgs", [DM, 1024], F32, kind="ExternalInput").ap()
    wus = nc.dram_tensor("wus", [DM, 1024], F32, kind="ExternalInput").ap()
    wds = nc.dram_tensor("wds", [I_FF, 256], F32, kind="ExternalInput").ap()
    cmat = nc.dram_tensor("cmat", [386, 128], F32, kind="ExternalInput").ap()
    masks = nc.dram_tensor("masks", [512, 512], F32, kind="ExternalInput").ap()

    y_out = nc.dram_tensor("y", [TOK, DM], F32, kind="ExternalOutput").ap()
    dbg_out = {}
    for name, shape in dbg:
        dbg_out[name] = nc.dram_tensor(name, list(shape), F32, kind="ExternalOutput").ap()

    cin_mean = nc.dram_tensor("cin_mean", [1, 8], F32).ap()
    cout_mean = nc.dram_tensor("cout_mean", [1, 8], F32).ap()
    cin_wq = nc.dram_tensor("cin_wq", [3, DM * 1024], BF16).ap()
    cout_wq = nc.dram_tensor("cout_wq", [24, DM * 1024], BF16).ap()
    SH16 = 2 * 2 * 128 * TOK
    cin_q16 = nc.dram_tensor("cin_q16", [8, SH16], F16).ap()
    cout_q16 = nc.dram_tensor("cout_q16", [8, SH16], F16).ap()
    cin_k16 = nc.dram_tensor("cin_k16", [8, SH16], F16).ap()
    cout_k16 = nc.dram_tensor("cout_k16", [8, SH16], F16).ap()
    SH32 = 2 * 128 * TOK
    cin_v32 = nc.dram_tensor("cin_v32", [8, SH32], F32).ap()
    cout_v32 = nc.dram_tensor("cout_v32", [8, SH32], F32).ap()
    SHO = 2 * 128 * TOK
    cin_o = nc.dram_tensor("cin_o", [8, SHO], F32).ap()
    cout_o = nc.dram_tensor("cout_o", [8, SHO], F32).ap()
    xq_scr = nc.dram_tensor("xq_scr", [4, 128, 4096], BF16).ap()
    ffn_scr = nc.dram_tensor("ffn_scr", [16, 128, TOK], F32).ap()

    ident = nc.alloc_sbuf_tensor("ident", [128, 128], F32)
    identb = nc.alloc_sbuf_tensor("identb", [128, 128], BF16)
    h128 = nc.alloc_sbuf_tensor("h128", [128, 128], F32)
    bd64 = nc.alloc_sbuf_tensor("bd64", [128, 128], F32)
    p2c = nc.alloc_sbuf_tensor("p2c", [2, 128], F32)
    onerow = nc.alloc_sbuf_tensor("onerow", [1, 128], F32)
    qkvb_sb = nc.alloc_sbuf_tensor("qkvb_sb", [128, 32], F32)
    outb_sb = nc.alloc_sbuf_tensor("outb_sb", [128, 16], F32)
    lam_sb = nc.alloc_sbuf_tensor("lam_sb", [1, 512], F32)
    subln_sb = nc.alloc_sbuf_tensor("subln_sb", [1, 256], F32)
    subln_bc = nc.alloc_sbuf_tensor("subln_bc", [128, 256], F32)
    lam_col = nc.alloc_sbuf_tensor("lam_col", [128, 1], F32)
    nlam_col = nc.alloc_sbuf_tensor("nlam_col", [128, 1], F32)
    epsc = nc.alloc_sbuf_tensor("epsc", [128, 2], F32)
    sm = nc.alloc_sbuf_tensor("sm", [128, 192], F32)
    sm2 = nc.alloc_sbuf_tensor("sm2", [128, 160], F32)
    rowbuf = nc.alloc_sbuf_tensor("rowbuf", [2, 512], F32)
    dq_t = nc.alloc_sbuf_tensor("dq_t", [128, 8], F32)
    dqg_row = nc.alloc_sbuf_tensor("dqg_row", [1, TOK], F32)
    dqu_row = nc.alloc_sbuf_tensor("dqu_row", [1, TOK], F32)
    dqd_row = nc.alloc_sbuf_tensor("dqd_row", [1, TOK], F32)
    mean_sb = nc.alloc_sbuf_tensor("mean_sb", [1, 8], F32)
    swcol = nc.alloc_sbuf_tensor("swcol", [128, 4], F32)

    sc_mask = ExitStack()
    mask_sb = sc_mask.enter_context(nc.sbuf_tensor("mask_sb", [128, 2048], F32))
    dqrow = sc_mask.enter_context(nc.sbuf_tensor("dqrow", [1, TOK], F32))
    pb = [nc.alloc_psum_tensor(f"pb{i}", [128, 512], F32) for i in range(8)]

    with nc.Block() as block, allow_ici_all_to_all():
        K.dma("dS", ident[:], cmat[0:128, :], sem="cst")
        K.dma("dS", h128[:], cmat[128:256, :], sem="cst")
        K.dma("dS", bd64[:], cmat[256:384, :], sem="cst")
        K.dma("dS", p2c[:], cmat[384:386, :], sem="cst")
        for _m in range(4):
            K.dma("dS", mask_sb[:, _m * 512:(_m + 1) * 512], masks[_m * 128:(_m + 1) * 128, :], sem="cst")
        K.dma("dS", qkvb_sb[:], qkv_b, sem="cst")
        K.dma("dS", outb_sb[:], out_b, sem="cst")
        K.dma("dS", lam_sb[:], lamv, sem="cst")
        K.dma("dS", subln_sb[:], sublnw, sem="cst")
        tok_const = K.tok("cst")

        t_ones = K.v("memset", onerow[:], 1.0)
        K.v("memset", epsc[:, 0:1], EPS)
        K.v("memset", epsc[:, 1:2], 1.0 / 256.0)
        tok_eps = K.tok("V")
        K.v("tensor_copy", identb[:], ident[:], wait=(tok_const,))
        tok_idc = K.tok("V")
        K.wait("tensor", tok_const, t_ones)
        t_mm = K.t("matmul", pb[6][:, 0:256], onerow[:], subln_sb[:], start=True, stop=True)
        t_sbc = K.v("tensor_scalar_mul", subln_bc[:], pb[6][:, 0:256], 1.0 - LAMBDA_INIT, wait=(t_mm,))

        t1 = K.v("tensor_mul", rowbuf[0:1, 0:128], lam_sb[0:1, 0:128], lam_sb[0:1, 128:256], wait=(tok_const,))
        t2 = K.v("tensor_mul", rowbuf[0:1, 128:256], lam_sb[0:1, 256:384], lam_sb[0:1, 384:512])
        t3 = K.v("reduce_sum", sm2[0:1, 140:141], rowbuf[0:1, 0:128], axis=AX.X, wait=(t1,))
        t4 = K.v("reduce_sum", sm2[0:1, 141:142], rowbuf[0:1, 128:256], axis=AX.X, wait=(t2,))
        t5 = K.s("activation", sm2[0:1, 142:143], sm2[0:1, 140:141], AF.Exp, wait=(t3,))
        t6 = K.s("activation", sm2[0:1, 143:144], sm2[0:1, 141:142], AF.Exp, wait=(t4,))
        t7 = K.v("tensor_scalar", sm2[0:1, 138:139], sm2[0:1, 143:144], -1.0, LAMBDA_INIT,
                 ALU.mult, ALU.add, wait=(t5, t6))
        t8 = K.v("tensor_add", sm2[0:1, 139:140], sm2[0:1, 138:139], sm2[0:1, 142:143], wait=(t7,))
        K.wait("tensor", t8, t_sbc)
        t9 = K.t("matmul", pb[6][:, 256:257], onerow[:], sm2[0:1, 139:140], start=True, stop=True)
        K.v("tensor_copy", lam_col[:], pb[6][:, 256:257], wait=(t9,))
        K.v("tensor_scalar_mul", nlam_col[:], pb[6][:, 256:257], -1.0, wait=(K.tok("V"),))
        tok_lam = K.tok("V")
        tok_pb6_free = K.tok("V")

        sc_xnT = ExitStack()
        sc_p0 = ExitStack()
        xnT = [sc_xnT.enter_context(nc.sbuf_tensor(f"xnT{f}", [128, TOK], F32)) for f in range(16)]
        xqT = [sc_p0.enter_context(nc.sbuf_tensor(f"xqT{f}", [128, TOK], BF16)) for f in range(16)]
        xt = [sc_p0.enter_context(nc.sbuf_tensor(f"xt{j}", [128, DM], F32)) for j in range(2)]
        xc = [sc_p0.enter_context(nc.sbuf_tensor(f"xc{j}", [128, DM], F32)) for j in range(2)]
        xnb = [sc_p0.enter_context(nc.sbuf_tensor(f"xnb{j}", [128, DM], F32)) for j in range(2)]
        sqd = xt
        scrf = xc

        xdma = [None] * NTILE
        tile_vdone = [None] * NTILE
        tile_tdone = [None] * NTILE
        last_xq_ev = [None] * NTILE
        for i in range(2):
            xdma[i] = K.dma("dS", xt[i][:], x_in[i * 128:(i + 1) * 128, :], sem=f"x{i % 2}")
        for i in range(NTILE):
            bs_ = i % 2
            cb = 16 * i
            c = lambda j: sm[:, cb + j:cb + j + 1]
            K.v("reduce_sum", c(0), xt[bs_][:], axis=AX.X, wait=(xdma[i],))
            tm = K.v("tensor_scalar_mul", c(1), c(0), -1.0 / DM, wait=(K.tok("V"),))
            t_xc = K.s("activation", xc[bs_][:], xt[bs_][:], AF.Identity,
                       bias=c(1), scale=1.0, wait=(tm, xdma[i]))
            t_sq = K.s("activation", sqd[bs_][:], xc[bs_][:], AF.Square,
                       accum_out=c(2), wait=(t_xc,))
            tv = K.v("tensor_scalar", c(3), c(2), 1.0 / DM, EPS, ALU.mult, ALU.add, wait=(t_sq,))
            tu = K.s("activation", c(4), c(3), AF.Ln, wait=(tv,))
            tlast = K.s("activation", c(5), c(4), AF.Exp, scale=-0.5, wait=(tu,))
            rcur = 5
            for it in range(1):
                K.v("tensor_mul", c(6), c(rcur), c(rcur), wait=(tlast,))
                K.v("tensor_mul", c(6), c(6), c(3), wait=(K.tok("V"),))
                K.v("tensor_scalar", c(6), c(6), -0.5, 1.5, ALU.mult, ALU.add, wait=(K.tok("V"),))
                tlast = K.v("tensor_mul", c(7), c(rcur), c(6), wait=(K.tok("V"),))
                rcur = 7
            wprev = [tile_tdone[i - 2], tile_vdone[i - 2]] if i >= 2 else []
            t_xn = K.v("tensor_scalar_mul", xnb[bs_][:], xc[bs_][:], c(rcur),
                       wait=[tlast] + wprev)
            if P0CUT < 2:
                tile_vdone[i] = K.tok("V")
                tile_tdone[i] = K.tok("T")
                last_xq_ev[i] = None
                if i + 2 < NTILE:
                    xdma[i + 2] = K.dma("dS", xt[bs_][:], x_in[(i + 2) * 128:(i + 3) * 128, :],
                                        sem=f"x{bs_}", wait=(tile_vdone[i], K.tok("V")))
                continue
            K.v("tensor_reduce", c(8), xnb[bs_][:], axis=AX.X, op=ALU.max,
                apply_absolute_value=True, wait=(t_xn,))
            K.v("tensor_scalar_max", c(9), c(8), EPS, wait=(K.tok("V"),))
            K.v("reciprocal", c(10), c(9), wait=(K.tok("V"),))
            tsc2 = K.v("tensor_scalar_mul", c(11), c(10), 128.0, wait=(K.tok("V"),))
            K.v("tensor_scalar_mul", dq_t[:, i:i + 1], c(9), 1.0 / 128.0)
            tq1 = K.v("tensor_scalar_mul", scrf[bs_][:], xnb[bs_][:], c(11), wait=(tsc2, t_xn))
            tq2 = K.v("tensor_scalar_add", scrf[bs_][:], scrf[bs_][:], MAGIC, wait=(tq1,))
            tq4 = K.v("tensor_scalar", sqd[bs_][:], scrf[bs_][:], MAGIC, 127.0,
                      ALU.subtract, ALU.min, wait=(tq2, t_sq))
            tile_vdone[i] = K.tok("V")
            if P0CUT < 3:
                tile_tdone[i] = K.tok("T")
                last_xq_ev[i] = None
                if i + 2 < NTILE:
                    xdma[i + 2] = K.dma("dS", xt[bs_][:], x_in[(i + 2) * 128:(i + 3) * 128, :],
                                        sem=f"x{bs_}", wait=(tile_vdone[i], K.tok("V")))
                continue
            ev_xn = []
            for bg in range(4):
                bk = pb[4 + bg]
                w = [t_xn, tok_idc]
                if i >= 1:
                    w.append(last_xq_ev[i - 1])
                for sl in range(4):
                    f = bg * 4 + sl
                    K.t("transpose", bk[:, sl * 128:(sl + 1) * 128],
                        xnb[bs_][:, f * 128:(f + 1) * 128], ident[:],
                        wait=w if sl == 0 else ())
                tbk = K.tok("T")
                for sl in range(4):
                    f = bg * 4 + sl
                    ev_xn.append(K.v("tensor_copy", xnT[f][:, i * 128:(i + 1) * 128],
                                     bk[:, sl * 128:(sl + 1) * 128], wait=(tbk,)))
            ev_xq = []
            if P0CUT < 4:
                tile_tdone[i] = K.tok("T")
                last_xq_ev[i] = ev_xn[-1]
                if i + 2 < NTILE:
                    xdma[i + 2] = K.dma("dS", xt[bs_][:], x_in[(i + 2) * 128:(i + 3) * 128, :],
                                        sem=f"x{bs_}", wait=(tile_vdone[i], tile_tdone[i]))
                continue
            for bg in range(4):
                bk = pb[4 + bg]
                for sl in range(4):
                    f = bg * 4 + sl
                    K.t("transpose", bk[:, sl * 128:(sl + 1) * 128],
                        sqd[bs_][:, f * 128:(f + 1) * 128], ident[:],
                        wait=(tq4, ev_xn[bg * 4 + 3]) if sl == 0 else ())
                tbk = K.tok("T")
                for sl in range(4):
                    f = bg * 4 + sl
                    ev_xq.append(K.v("tensor_copy", xqT[f][:, i * 128:(i + 1) * 128],
                                     bk[:, sl * 128:(sl + 1) * 128], wait=(tbk,)))
            last_xq_ev[i] = ev_xq[-1]
            tile_tdone[i] = K.tok("T")
            if i + 2 < NTILE:
                xdma[i + 2] = K.dma("dS", xt[bs_][:], x_in[(i + 2) * 128:(i + 3) * 128, :],
                                    sem=f"x{bs_}", wait=(tile_vdone[i], tile_tdone[i]))
        K.wait("tensor", tok_pb6_free, K.tok("V"))
        for half in range(2 if P0CUT >= 5 else 0):
            for i4 in range(4):
                i = half * 4 + i4
                K.t("transpose", pb[6][0:1, i4 * 128:(i4 + 1) * 128], dq_t[:, i:i + 1], ident[:])
            ttr = K.tok("T")
            K.v("tensor_copy", dqrow[0:1, half * 512:(half + 1) * 512], pb[6][0:1, 0:512], wait=(ttr,))
            K.wait("tensor", K.tok("V"))
        tok_dqrow = K.tok("V")

        for f in range(16 if P0CUT >= 6 else 0):
            for g_ in range(4):
                K.dma("dA", xq_scr[:].rearrange("g p (k c) -> g p k c", k=16)[g_, :, f],
                      xqT[f][:, g_ * 256:(g_ + 1) * 256], sem="xqw", wait=(K.tok("V"),))
        tok_xq_scr = K.tok("xqw") if "xqw" in K.cnt else None

        tok_ag = None
        if stage >= 4:
            sc_pp = ExitStack()
            ptile = [sc_pp.enter_context(nc.sbuf_tensor(f"ptile{j}", [128, 1024], F32)) for j in range(2)]
            qtile = [sc_pp.enter_context(nc.sbuf_tensor(f"qtile{j}", [128, 1024], BF16)) for j in range(2)]
            wds_v = wds[:].rearrange("(a p) c -> a p c", p=128)
            wgs_v = wgs[:].rearrange("(a p) c -> a p c", p=128)
            wus_v = wus[:].rearrange("(a p) c -> a p c", p=128)
            ciwq_blk = cin_wq[:].rearrange("w (s p k c) -> w s p k c", s=4, p=128, k=16)

            pdma = [None, None]
            pcons = [None, None]
            pmm = [None, None]
            onepc = sm2[:, 33:34]
            t_one2 = K.v("memset", onepc, 1.0)
            for wi, (wv, ntl) in enumerate([(wgs_v, 16), (wus_v, 16), (wds_v, 64)]):
                for a in range(ntl):
                    j = (a + wi) % 2
                    w = [pcons[j]] if pcons[j] else []
                    srcv = wv[a] if ntl == 16 else wv[a][:, 0:256]
                    pdma[j] = K.dma("dG", ptile[j][:, 0:(1024 if ntl == 16 else 256)], srcv,
                                    sem=f"pp{j}", wait=w)
                    tr = K.v("tensor_reduce", sm2[:, 34 + j:35 + j], ptile[j][:, 0:(1024 if ntl == 16 else 256)],
                             axis=AX.X, op=ALU.add, apply_absolute_value=True,
                             wait=[pdma[j], pmm[j]])
                    pcons[j] = tr
                    K.wait("tensor", tr, t_one2)
                    pmm[j] = K.t("matmul", pb[0][0:1, wi:wi + 1], sm2[:, 34 + j:35 + j], onepc,
                                 start=(wi == 0 and a == 0), stop=(wi == 2 and a == ntl - 1),
                                 skip_group_check=True)
            t_sums = K.v("tensor_copy", mean_sb[0:1, 0:3], pb[0][0:1, 0:3], wait=(K.tok("T"),))
            tz = K.v("memset", mean_sb[0:1, 3:8], 0.0)
            tmdma = K.dma("dG", cin_mean, mean_sb[0:1, :], sem="ppm", wait=(t_sums, tz))
            cc_m = K.cc("AllReduce", ALU.add, RG, [cin_mean.opt()], [cout_mean.opt()],
                        wait=[tmdma])
            tok_ar = K.tok("CC")
            tmback = K.dma("dG", mean_sb[0:1, :], cout_mean, sem="ppm2", wait=(tok_ar,))
            NELT = float(DM * I_FF)
            t_m1 = K.v("tensor_scalar_mul", mean_sb[0:1, 4:7], mean_sb[0:1, 0:3], 1.0 / NELT, wait=(tmback,))
            t_m2 = K.v("tensor_scalar_max", mean_sb[0:1, 4:7], mean_sb[0:1, 4:7], EPS, wait=(K.tok("V"),))
            t_m3 = K.v("reciprocal", mean_sb[0:1, 0:3], mean_sb[0:1, 4:7], wait=(K.tok("V"),))
            K.wait("tensor", t_m3)
            t_bcm = K.t("matmul", pb[0][:, 0:3], onerow[:], mean_sb[0:1, 0:3], start=True, stop=True)
            t_swc = K.v("tensor_copy", swcol[:, 0:3], pb[0][:, 0:3], wait=(t_bcm,))
            t_cd = K.v("tensor_scalar_mul", mean_sb[0:1, 7:8], mean_sb[0:1, 6:7],
                       1.0 / (128.0 * math.sqrt(float(I_FF))), wait=(t_m2,))
            K.wait("tensor", t_cd, t_swc)
            t_bcd = K.t("matmul", pb[0][:, 4:5], onerow[:], mean_sb[0:1, 7:8], start=True, stop=True)
            t_cdc = K.v("tensor_copy", swcol[:, 3:4], pb[0][:, 4:5], wait=(t_bcd,))
            t_dg = K.v("tensor_scalar", dqg_row[0:1, :], dqrow[0:1, :], mean_sb[0:1, 4:5], None,
                       ALU.mult, wait=(tok_dqrow, t_m2))
            t_du = K.v("tensor_scalar", dqu_row[0:1, :], dqrow[0:1, :], mean_sb[0:1, 5:6], None,
                       ALU.mult, wait=(K.tok("V"),))
            tok_dqgu = K.tok("V")
            pdma = [None, None]
            pcons = [None, None]
            qfree = [None, None]
            for wi, (wv, ntl) in enumerate([(wgs_v, 16), (wus_v, 16), (wds_v, 64)]):
                for a in range(ntl):
                    j = (a + wi) % 2
                    ww = 1024 if ntl == 16 else 256
                    w = [pcons[j]] if pcons[j] else []
                    pdma[j] = K.dma("dG", ptile[j][:, 0:ww], wv[a] if ntl == 16 else wv[a][:, 0:256],
                                    sem=f"pq{j}", wait=w)
                    t1 = K.v("tensor_scalar", ptile[j][:, 0:ww], ptile[j][:, 0:ww],
                             swcol[:, wi:wi + 1], MAGIC, ALU.mult, ALU.add,
                             wait=(pdma[j], t_swc, qfree[j]))
                    t2 = K.v("tensor_scalar", ptile[j][:, 0:ww], ptile[j][:, 0:ww],
                             MAGIC, 1.0, ALU.subtract, ALU.min, wait=(K.tok("V"),))
                    t3 = K.v("tensor_scalar_max", qtile[j][:, 0:ww], ptile[j][:, 0:ww],
                             -1.0, wait=(K.tok("V"),))
                    pcons[j] = t3
                    if ntl == 16:
                        for sub in range(4):
                            K.dma("dA", ciwq_blk[wi, sub, :, a],
                                  qtile[j][:, sub * 256:(sub + 1) * 256], sem=f"pw{j}",
                                  wait=(t3,))
                        qfree[j] = K.tok(f"pw{j}")
                    else:
                        qfree[j] = K.dma("dA", ciwq_blk[2, a >> 4, :, a & 15],
                                         qtile[j][:, 0:256], sem=f"pw{j}", wait=(t3,))
            tok_pw = [K.tok("pw0"), K.tok("pw1")]
            cc_ag = K.cc("AllGather", ALU.bypass, RG, [cin_wq.opt()], [cout_wq.opt()],
                         wait=tok_pw)
            tok_ag = K.tok("CC")
            sc_pp.close()
            K.barrier(dma_sems=["pp0", "pp1", "pq0", "pq1", "pw0", "pw1", "ppm", "ppm2"])

        q16v = cin_q16[:].rearrange("r (s p t) -> r s p t", s=4, p=128)
        k16v = cin_k16[:].rearrange("r (s p t) -> r s p t", s=4, p=128)
        v32v = cin_v32[:].rearrange("r (s p t) -> r s p t", s=2, p=128)
        tok_p1 = None
        if stage >= 2:
            sc_p0.close()
            K.barrier(dma_sems=["xqw", "x0", "x1"])
            sc_p1 = ExitStack()
            wqb = [sc_p1.enter_context(nc.sbuf_tensor(f"wqb{j}", [128, 8192], F32))
                   for j in range(2)]
            stq = [sc_p1.enter_context(nc.sbuf_tensor(f"stq{j}", [128, 512], F32)) for j in range(2)]
            sthi = [sc_p1.enter_context(nc.sbuf_tensor(f"sthi{j}", [128, 512], F16)) for j in range(2)]
            stlo = [sc_p1.enter_context(nc.sbuf_tensor(f"stlo{j}", [128, 512], F16)) for j in range(2)]

            wq_tok = [None, None]
            wq_cons = [None, None]

            def load_wq(og):
                j = og % 2
                w = [wq_cons[j]] if wq_cons[j] else []
                K.dma("dS", wqb[j][:, 0:4096], wqkvT[og][:, 0:4096], sem=f"wq{j}", wait=w)
                K.dma("dA", wqb[j][:, 4096:8192], wqkvT[og][:, 4096:8192], sem=f"wqa{j}", wait=w)
                wq_tok[j] = [K.tok(f"wq{j}"), K.tok(f"wqa{j}")]

            load_wq(4)
            load_wq(5)
            bank_tok = [None] * 4
            slot_q_free = [None, None]
            slot_h_free = [None, None]
            slot_l_free = [None, None]
            m = 0
            tok_a2a_kv_holder = [None]
            og_order = [4, 5, 6, 7, 0, 1, 2, 3]
            for og in og_order:
                j = og % 2
                for tg in range(2):
                    for oc4 in range(4):
                        oc = og * 4 + oc4
                        bk = pb[oc4]
                        K.wait("tensor", *(wq_tok[j] + [bank_tok[oc4]]))
                        for kc in range(16):
                            K.t("matmul", bk[:, 0:512],
                                wqb[j][:, kc * 512 + oc4 * 128:kc * 512 + (oc4 + 1) * 128],
                                xnT[kc][:, tg * 512:(tg + 1) * 512],
                                start=(kc == 0), stop=(kc == 15))
                        tmm = K.tok("T")
                        sq_ = m % 2
                        t_qf = K.s("activation", stq[sq_][:], bk[:, 0:512], AF.Identity,
                                   bias=qkvb_sb[:, oc:oc + 1], scale=1.0,
                                   wait=(tmm, slot_q_free[sq_]))
                        bank_tok[oc4] = t_qf
                        if oc < 24:
                            t_hi = K.s("activation", sthi[sq_][:], stq[sq_][:], AF.Identity,
                                       bias=0.0, scale=1.0, wait=(t_qf, slot_h_free[sq_]))
                            t_lo = K.v("tensor_sub", stlo[sq_][:], stq[sq_][:], sthi[sq_][:],
                                       wait=(t_hi, slot_l_free[sq_]))
                            slot_q_free[sq_] = t_lo
                            if oc < 16:
                                dsts = [oc >> 1]
                                dstv = q16v
                                pre = "q"
                            else:
                                g = (oc - 16) >> 1
                                dsts = [2 * g, 2 * g + 1]
                                dstv = k16v
                                pre = "k"
                            sec = 0 + (oc & 1)
                            secl = 2 + (oc & 1)
                            for dj in dsts:
                                K.dma("dA", dstv[dj, sec, :, tg * 512:(tg + 1) * 512],
                                      sthi[sq_][:], sem=f"{pre}h{sq_}", wait=(t_hi,))
                                K.dma("dA", dstv[dj, secl, :, tg * 512:(tg + 1) * 512],
                                      stlo[sq_][:], sem=f"{pre}l{sq_}", wait=(t_lo,))
                            slot_h_free[sq_] = K.tok(f"{pre}h{sq_}")
                            slot_l_free[sq_] = K.tok(f"{pre}l{sq_}")
                        else:
                            g = (oc - 24) >> 1
                            half = (oc - 24) & 1
                            for dj in (2 * g, 2 * g + 1):
                                K.dma("dA", v32v[dj, half, :, tg * 512:(tg + 1) * 512],
                                      stq[sq_][:], sem=f"qv{sq_}", wait=(t_qf,))
                            slot_q_free[sq_] = K.tok(f"qv{sq_}")
                        m += 1
                oi = og_order.index(og)
                if oi == 3:
                    kv_toks = [K.tok(s) for s in list(K.cnt) if s.startswith(("kh", "kl", "qv"))]
                    cck = K.cc("AllToAll", ALU.bypass, RG, [cin_k16.opt()], [cout_k16.opt()],
                               wait=kv_toks)
                    ccv = K.cc("AllToAll", ALU.bypass, RG, [cin_v32.opt()], [cout_v32.opt()])
                    tok_a2a_kv_holder[0] = K.tok("CC")
                if oi + 2 < 8:
                    wq_cons[og_order[oi + 2] % 2] = K.tok("T")
                    load_wq(og_order[oi + 2])
            stage_toks = [K.tok(s) for s in list(K.cnt)
                          if s.startswith(("qh", "ql"))]
            tok_p1 = stage_toks
            sc_p1.close()
            sc_xnT.close()
            K.barrier(dma_sems=[s for s in K.cnt if s.startswith(("qh", "ql", "kh", "kl", "qv", "wq"))])

            if stage >= 2:
                ccq = K.cc("AllToAll", ALU.bypass, RG, [cin_q16.opt()], [cout_q16.opt()],
                           wait=stage_toks)
                tok_a2a_qkv = K.tok("CC")

        tok_a2a_o = None
        if stage >= 3:
            sc_p2 = ExitStack()
            qh = [sc_p2.enter_context(nc.sbuf_tensor(f"qh{c}", [128, S], F16)) for c in range(2)]
            ql = [sc_p2.enter_context(nc.sbuf_tensor(f"ql{c}", [128, S], F16)) for c in range(2)]
            kh = [sc_p2.enter_context(nc.sbuf_tensor(f"kh{c}", [128, S], F16)) for c in range(2)]
            kl = [sc_p2.enter_context(nc.sbuf_tensor(f"kl{c}", [128, S], F16)) for c in range(2)]
            vT = sc_p2.enter_context(nc.sbuf_tensor("vT", [128, 32 * 256], F32))
            s1b = sc_p2.enter_context(nc.sbuf_tensor("s1b", [128, S], F32))
            s2b = sc_p2.enter_context(nc.sbuf_tensor("s2b", [128, S], F32))
            s1c = sc_p2.enter_context(nc.sbuf_tensor("s1c", [128, S], F32))
            s2c = sc_p2.enter_context(nc.sbuf_tensor("s2c", [128, S], F32))
            vtmp = [s1b[:, 0:1024], s1b[:, 1024:2048]]
            ptT = [sc_p2.enter_context(nc.sbuf_tensor(f"ptT{j}", [128, 512], F32)) for j in range(2)]
            oS = [sc_p2.enter_context(nc.sbuf_tensor(f"oS{j}", [128, 512], F32)) for j in range(2)]
            osc4 = sc_p2.enter_context(nc.sbuf_tensor("osc4", [128, 1024], F32))
            osc = sc_p2.enter_context(nc.sbuf_tensor("osc", [128, 256], F32))
            co16 = cout_q16[:].rearrange("r (s p t) -> r s p t", s=4, p=128)
            ck16 = cout_k16[:].rearrange("r (s p t) -> r s p t", s=4, p=128)
            co32 = cout_v32[:].rearrange("r (s p t) -> r s p t", s=2, p=128)
            cino = cin_o[:].rearrange("r (c p t) -> r c p t", c=2, p=128)

            u_done = []
            u0_osh = None
            vtr_done = [None, None]
            vtr_prev_ev = None
            for u in range(2):
                w_ld = [tok_a2a_qkv] + u_done
                if u == 1:
                    w_ld += u0_osh
                for c in range(2):
                    for r4 in range(4):
                        r = 4 * u + r4
                        K.dma("dS", qh[c][:, r4 * 1024:(r4 + 1) * 1024], co16[r, 0 + c], sem=f"aq{u}", wait=w_ld)
                        K.dma("dS", ql[c][:, r4 * 1024:(r4 + 1) * 1024], co16[r, 2 + c], sem=f"aq{u}", wait=w_ld)
                        K.dma("dS", kh[c][:, r4 * 1024:(r4 + 1) * 1024], ck16[r, 0 + c], sem=f"aq{u}", wait=w_ld)
                        K.dma("dS", kl[c][:, r4 * 1024:(r4 + 1) * 1024], ck16[r, 2 + c], sem=f"aq{u}", wait=w_ld)
                tok_aq = K.tok(f"aq{u}")
                for half in range(2):
                    for r4 in range(4):
                        r = 4 * u + r4
                        jj = (half * 4 + r4) % 2
                        wv = w_ld + ([vtr_done[jj]] if vtr_done[jj] is not None else [])
                        vdma = K.dma("dS", vtmp[jj], co32[r, half], sem=f"av{u}_{half}_{r4}", wait=wv)
                        for grp in range(2):
                            wtr = [vdma, tok_idc] + u_done
                            if vtr_prev_ev is not None:
                                wtr.append(vtr_prev_ev)
                            for sl in range(4):
                                t8 = grp * 4 + sl
                                K.t("transpose", pb[7][:, sl * 128:(sl + 1) * 128],
                                    s1b[:, jj * 1024 + t8 * 128: jj * 1024 + (t8 + 1) * 128],
                                    ident[:], wait=wtr if sl == 0 else ())
                            tbk = K.tok("T")
                            kb0 = r4 * 8 + grp * 4
                            dst = vT[:].rearrange("p (k e) -> p k e", k=32)[
                                :, kb0:kb0 + 4, half * 128:half * 128 + 128]
                            vtr_prev_ev = K.v("tensor_copy", dst, pb[7][:, 0:512].rearrange(
                                "p (k e) -> p k e", k=4), wait=(tbk,))
                        vtr_done[jj] = K.tok("T")
                tok_vT = K.tok("V")

                tok_bounds = K.tok("V")

                prev_ptT_cons = [None, None]
                prev_ptT_ev = [None, None]
                prev_s_cons = [None, None]
                prev_sq = None
                prev_o_ev = None
                osc4_free = None
                SB1 = [s1b, s1c]
                SB2 = [s2b, s2c]
                state = [None, None]
                for qbi in range(33):
                    if qbi >= 1:
                        qb = qbi - 1
                        par = qb % 2
                        st = state[par]
                        cq = lambda j: sm2[:, 96 + 32 * par + j:96 + 32 * par + j + 1]
                        nkb = qb // 4 + 1
                        ncols = nkb * 512
                        sA, sB = SB1[par], SB2[par]
                        t_e1 = K.s("activation", sA[:, 0:ncols], sA[:, 0:ncols], AF.Exp,
                                   bias=cq(14), scale=1.0, accum_out=cq(1),
                                   wait=[st["ev1"], st["tmx"]])
                        t_e2 = K.s("activation", sB[:, 0:ncols], sB[:, 0:ncols], AF.Exp,
                                   bias=cq(15), scale=1.0, accum_out=cq(2), wait=(t_e1,))
                        t_a = K.v("reciprocal", cq(3), cq(1), wait=(t_e1,))
                        t_c0 = K.v("tensor_scalar_mul", sA[:, 0:ncols], sA[:, 0:ncols],
                                   cq(3), wait=(t_a,))
                        t_b0 = K.v("reciprocal", cq(4), cq(2), wait=(t_e2,))
                        t_b = K.v("tensor_mul", cq(5), cq(4), nlam_col[:], wait=(t_b0, tok_lam))
                        t_c2 = K.v("scalar_tensor_tensor", sA[:, 0:ncols], sB[:, 0:ncols],
                                   cq(5), sA[:, 0:ncols], ALU.mult, ALU.add,
                                   wait=(t_b, t_c0))
                        state[par]["c2"] = t_c2
                    if qbi < 32:
                        qb = qbi
                        par = qb % 2
                        cq = lambda j: sm2[:, 96 + 32 * par + j:96 + 32 * par + j + 1]
                        cb_ = 96 + 32 * par
                        nkb = qb // 4 + 1
                        diag = qb // 4
                        sA, sB = SB1[par], SB2[par]
                        ev1 = [None] * nkb
                        ev2 = [None] * nkb
                        tmx = None
                        for kb in range(nkb):
                            b1 = pb[kb % 2]
                            b2 = pb[2 + kb % 2]
                            wmm = [tok_aq]
                            if prev_s_cons[par] is not None:
                                wmm.append(prev_s_cons[par])
                            if kb >= 2:
                                wmm += [ev1[kb - 2], ev2[kb - 2]]
                            elif qbi >= 1:
                                wmm += [prev_ev1_last, prev_ev2_last]
                            K.wait("tensor", *wmm)
                            for (lh, rh, st_, sp_) in ((qh, kh, True, False), (qh, kl, False, False), (ql, kh, False, True)):
                                K.t("matmul", b1[:, 0:512], lh[0][:, qb * 128:(qb + 1) * 128],
                                    rh[0][:, kb * 512:(kb + 1) * 512], start=st_, stop=sp_)
                            for (lh, rh, st_, sp_) in ((qh, kh, True, False), (qh, kl, False, False), (ql, kh, False, True)):
                                K.t("matmul", b2[:, 0:512], lh[1][:, qb * 128:(qb + 1) * 128],
                                    rh[1][:, kb * 512:(kb + 1) * 512], start=st_, stop=sp_)
                            tmm = K.tok("T")
                            if kb == diag:
                                msl = mask_sb[:, (qb % 4) * 512:(qb % 4 + 1) * 512]
                                ev1[kb] = K.v("scalar_tensor_tensor", sA[:, kb * 512:(kb + 1) * 512],
                                              b1[:, 0:512], SCALE, msl, ALU.mult, ALU.add,
                                              wait=(tmm,))
                                ev2[kb] = K.v("scalar_tensor_tensor", sB[:, kb * 512:(kb + 1) * 512],
                                              b2[:, 0:512], SCALE, msl, ALU.mult, ALU.add,
                                              wait=(tmm,))
                            else:
                                ev1[kb] = K.s("activation", sA[:, kb * 512:(kb + 1) * 512], b1[:, 0:512],
                                              AF.Identity, bias=0.0, scale=SCALE, wait=(tmm,))
                                ev2[kb] = K.v("tensor_scalar_mul", sB[:, kb * 512:(kb + 1) * 512],
                                              b2[:, 0:512], SCALE, wait=(tmm,))
                            K.v("reduce_max", cq(16 + kb), sA[:, kb * 512:(kb + 1) * 512],
                                axis=AX.X, negate=True, wait=(ev1[kb],))
                            K.v("reduce_max", cq(24 + kb), sB[:, kb * 512:(kb + 1) * 512],
                                axis=AX.X, negate=True, wait=(ev2[kb],))
                        K.v("tensor_reduce", cq(14), sm2[:, cb_ + 16:cb_ + 16 + nkb],
                            axis=AX.X, op=ALU.min, wait=(K.tok("V"), K.tok("SC")))
                        tmx = K.v("tensor_reduce", cq(15), sm2[:, cb_ + 24:cb_ + 24 + nkb],
                                  axis=AX.X, op=ALU.min, wait=(K.tok("V"),))
                        prev_ev1_last, prev_ev2_last = ev1[-1], ev2[-1]
                        state[par] = {"ev1": ev1[-1], "ev2": ev2[-1], "tmx": tmx}
                    if qbi >= 1:
                        qb = qbi - 1
                        par = qb % 2
                        cq = lambda j: sm2[:, 96 + 32 * par + j:96 + 32 * par + j + 1]
                        sA = SB1[par]
                        t_c2 = state[par]["c2"]
                        n128 = qb + 1
                        for g4 in range(0, n128, 4):
                            cnt4 = min(4, n128 - g4)
                            sl_ = (g4 // 4) % 2
                            bkt = pb[4 + sl_]
                            wtr = [t_c2, prev_ptT_ev[sl_]]
                            for j in range(cnt4):
                                kb1 = g4 + j
                                K.t("transpose", bkt[:, j * 128:(j + 1) * 128],
                                    sA[:, kb1 * 128:(kb1 + 1) * 128], ident[:],
                                    wait=wtr if j == 0 else ())
                            trs_t = K.tok("T")
                            t_ev = K.v("tensor_copy", ptT[sl_][:, 0:cnt4 * 128], bkt[:, 0:cnt4 * 128],
                                       wait=[trs_t, prev_ptT_cons[sl_]])
                            prev_ptT_ev[sl_] = t_ev
                            K.wait("tensor", t_ev, tok_vT, prev_sq)
                            for j in range(cnt4):
                                kb1 = g4 + j
                                K.t("matmul", pb[6][:, 0:256], ptT[sl_][:, j * 128:(j + 1) * 128],
                                    vT[:, kb1 * 256:(kb1 + 1) * 256],
                                    start=(kb1 == 0), stop=(kb1 == n128 - 1))
                            prev_ptT_cons[sl_] = K.tok("T")
                        t_pv = K.tok("T")
                        prev_s_cons[par] = t_pv
                        t_sq = K.s("activation", osc[:], pb[6][:, 0:256], AF.Square,
                                   accum_out=cq(6), wait=[t_pv, prev_o_ev])
                        prev_sq = t_sq
                        tu2 = K.s("activation", cq(8), cq(6), AF.Ln, bias=epsc[:, 0:1],
                                  scale=epsc[:, 1:2], wait=(t_sq, tok_eps))
                        tl2 = K.s("activation", cq(9), cq(8), AF.Exp, scale=-0.5, wait=(tu2,))
                        t_o2 = K.v("scalar_tensor_tensor", osc4[:, (qb % 4) * 256:(qb % 4 + 1) * 256],
                                   pb[6][:, 0:256], cq(9), subln_bc[:],
                                   ALU.mult, ALU.mult, wait=[tl2, t_sq, osc4_free, t_sbc])
                        if qb % 4 == 3:
                            qb0_ = qb - 3
                            K.wait("tensor", t_o2, prev_o_ev)
                            for half4 in range(2):
                                if half4 == 1:
                                    K.wait("tensor", prev_o_ev)
                                for j4 in range(4):
                                    qq = qb0_ + (half4 * 4 + j4) // 2
                                    cch = (half4 * 4 + j4) % 2
                                    K.t("transpose", pb[7][:, j4 * 128:(j4 + 1) * 128],
                                        osc4[:, (qq % 4) * 256 + cch * 128:(qq % 4) * 256 + (cch + 1) * 128],
                                        ident[:])
                                t_tro = K.tok("T")
                                slo_ = half4
                                t_oev2 = K.v("tensor_copy", oS[slo_][:, 0:512], pb[7][:, 0:512],
                                             wait=[t_tro, (f"osh{u}_{slo_}", K.cnt.get(f"osh{u}_{slo_}", 0))])
                                prev_o_ev = t_oev2
                                r4o = qb0_ // 8
                                for j4 in range(4):
                                    qq = qb0_ + (half4 * 4 + j4) // 2
                                    cch = (half4 * 4 + j4) % 2
                                    K.dma("dA", cino[4 * u + r4o, cch][:, (qq % 8) * 128:(qq % 8 + 1) * 128],
                                          oS[slo_][:, j4 * 128:(j4 + 1) * 128], sem=f"osh{u}_{slo_}",
                                          wait=(t_oev2,))
                            osc4_free = K.tok("T")
                if u == 0:
                    u0_osh = [K.tok("osh0_0"), K.tok("osh0_1")]
                    u_done = [K.tok("V"), K.tok("SC"), K.tok("T")]
                    prev_ptT_cons = [None, None]
            tok_osh1 = [K.tok("osh1_0"), K.tok("osh1_1")]
            cc_o = K.cc("AllToAll", ALU.bypass, RG, [cin_o.opt()], [cout_o.opt()],
                        wait=u0_osh + tok_osh1)
            tok_a2a_o = K.tok("CC")
            sc_p2.close()
            sc_mask.close()
            K.barrier(dma_sems=[sname for sname in K.cnt if sname.startswith(("aq", "av", "osh"))])

        if "d_oT" in dbg_out and stage >= 3:
            coo = cout_o[:].rearrange("r (c p t) -> r c p t", c=2, p=128)
            tmpo = nc.alloc_sbuf_tensor("tmpo", [128, TOK], F32)
            for r in range(8):
                for cch in range(2):
                    tld = K.dma("dS", tmpo[:], coo[r, cch], sem="dbo",
                                wait=[tok_a2a_o, ("dbo2", K.cnt.get("dbo2", 0))])
                    K.dma("dS", dbg_out["d_oT"][(r * 2 + cch) * 128:(r * 2 + cch + 1) * 128, :],
                          tmpo[:], sem="dbo2", wait=(tld,))

        if "d_xnT" in dbg_out:
            for f in range(16):
                K.dma("dS", dbg_out["d_xnT"][f * 128:(f + 1) * 128, :], xnT[f][:],
                      sem="dbg1", wait=(K.tok("V"),))
        if "d_xq" in dbg_out:
            xqf = xnb
            for f in range(16):
                tcp = K.v("tensor_copy", xqf[f % 2][:, 0:TOK], xqT[f][:],
                          wait=(K.tok("V"), (f"dbg2_{f % 2}", 16 * (f // 2)) if f >= 2 else None))
                K.dma("dS", dbg_out["d_xq"][f * 128:(f + 1) * 128, :], xqf[f % 2][:, 0:TOK],
                      sem=f"dbg2_{f % 2}", wait=(tcp,))
        if "d_dq" in dbg_out:
            K.dma("dS", dbg_out["d_dq"][0:1, :], dqrow[0:1, :], sem="dbg3", wait=(tok_dqrow,))

        if "d_q16" in dbg_out and stage >= 2:
            tmpf = nc.alloc_sbuf_tensor("tmpf", [128, NT], F32)
            tmph = nc.alloc_sbuf_tensor("tmph", [128, NT], F16)
            tmpl = nc.alloc_sbuf_tensor("tmpl", [128, NT], F16)
            co16 = cout_q16[:].rearrange("r (s p t) -> r s p t", s=8, p=128)
            co32 = cout_v32[:].rearrange("r (s p t) -> r s p t", s=2, p=128)
            for sec in range(4):
                vv_ = [co16, co16, ck16, ck16][sec]
                shi = [0, 1, 0, 1][sec]
                slo = [2, 3, 2, 3][sec]
                for r in range(8):
                    K.dma("dS", tmph[:, r * 1024:(r + 1) * 1024], vv_[r, shi], sem="dbh",
                          wait=(tok_a2a_qkv, K.tok("V")))
                    K.dma("dS", tmpl[:, r * 1024:(r + 1) * 1024], vv_[r, slo], sem="dbh")
                tld = K.tok("dbh")
                tadd = K.v("tensor_add", tmpf[:], tmph[:], tmpl[:],
                           wait=(tld, ("dbq", K.cnt.get("dbq", 0))))
                K.dma("dS", dbg_out["d_q16"][sec * 128:(sec + 1) * 128, :], tmpf[:],
                      sem="dbq", wait=(tadd,))
            for half in range(2):
                for r in range(8):
                    K.dma("dS", tmpf[:, r * 1024:(r + 1) * 1024], co32[r, half], sem="dbv",
                          wait=(tok_a2a_qkv, ("dbq", K.cnt.get("dbq", 0)), K.tok("V")))
                tld = K.tok("dbv")
                K.dma("dS", dbg_out["d_q16"][(4 + half) * 128:(5 + half) * 128, :], tmpf[:],
                      sem="dbq", wait=(tld,))

        tok_ffn_scr = None
        if stage >= 4:
            if stage < 90 and not (stage >= 3):
                pass
            if not (stage >= 3):
                sc_mask.close()
            sc_p3 = ExitStack()
            W2p = sc_p3.enter_context(nc.sbuf_tensor("W2p", [128, 16384], F32))
            Zb = sc_p3.enter_context(nc.sbuf_tensor("Zb", [128, 16384], F32))
            W2qp = Zb[:].bitcast(BF16)
            wstr = [sc_p3.enter_context(nc.sbuf_tensor(f"wstr{j}", [128, 4096], BF16)) for j in range(4)]
            xqs = sc_p3.enter_context(nc.sbuf_tensor("xqs", [128, 4096], BF16))
            itile = [sc_p3.enter_context(nc.sbuf_tensor(f"it{j}", [128, 256], F32)) for j in range(4)]
            utile = [sc_p3.enter_context(nc.sbuf_tensor(f"ut{j}", [128, 256], F32)) for j in range(2)]
            sgt = sc_p3.enter_context(nc.sbuf_tensor("sgt", [128, 256], F32))
            sut = sc_p3.enter_context(nc.sbuf_tensor("sut", [128, 256], F32))
            sdt = sc_p3.enter_context(nc.sbuf_tensor("sdt", [128, 256], F32))
            scl = sc_p3.enter_context(nc.sbuf_tensor("scl", [128, 128], F32))
            zti = [sc_p3.enter_context(nc.sbuf_tensor(f"zti{j}", [128, 512], F32)) for j in range(2)]
            fout = [sc_p3.enter_context(nc.sbuf_tensor(f"fout{j}", [128, 256], F32)) for j in range(2)]
            cowq_blk = cout_wq[:].rearrange("w (s p f) -> w s p f", s=4, p=128)
            xqv = xq_scr[:]

            gu_cons = [None, None]
            dn_cons = None
            fg_done = None
            pb03_free = None
            it_free = [None] * 4
            ut_free = [None] * 2
            fo_free = [None] * 2
            prev_gbc = None
            st2_zti_cons = [None, None]
            for g in range(NGF):
                K.dma("dS", xqs[:], xq_scr[g], sem="fxq",
                      wait=[tok_xq_scr] + ([fg_done] if fg_done else []))
                tok_xqs = K.tok("fxq")
                K.wait("tensor", tok_dqgu, prev_gbc)
                t_g1 = K.t("matmul", pb[6][:, 0:256], onerow[:], dqg_row[0:1, g * 256:(g + 1) * 256],
                           start=True, stop=True)
                t_g2 = K.t("matmul", pb[7][:, 0:256], onerow[:], dqu_row[0:1, g * 256:(g + 1) * 256],
                           start=True, stop=True)
                t_gb = K.v("tensor_copy", sgt[:], pb[6][:, 0:256], wait=[t_g1, prev_gbc])
                t_ub = K.v("tensor_copy", sut[:], pb[7][:, 0:256], wait=(t_g2,))
                prev_gbc = K.tok("V")
                st1_sl = 0
                st1_ev_last = None
                st1_bank_tok = [None, None]
                for ob in range(32):
                    jj = ob % 2
                    r_ = ob >> 2
                    sub = ob & 3
                    wld = [tok_ag] + ([gu_cons[jj]] if gu_cons[jj] else []) \
                          + ([dn_cons] if (ob < 2 and dn_cons) else [])
                    K.dma("dS", wstr[jj][:], cowq_blk[3 * r_, sub], sem=f"fg{jj}", wait=wld)
                    K.dma("dA", wstr[2 + jj][:], cowq_blk[3 * r_ + 1, sub], sem=f"fu{jj}", wait=wld)
                    tok_wg = K.tok(f"fg{jj}")
                    tok_wu = K.tok(f"fu{jj}")
                    if "d_ops" in dbg_out and g == 0 and ob == 0:
                        tmpo1 = W2p[:, 0:4096]
                        tc1 = K.v("tensor_copy", tmpo1, xqs[:], wait=(tok_xqs,))
                        K.dma("dS", dbg_out["d_ops"][0:128, :], tmpo1, sem="dbo1", wait=(tc1,))
                        tc2 = K.v("tensor_copy", tmpo1, wstr[jj][:], wait=(tok_wg, ("dbo1", 16)))
                        K.dma("dS", dbg_out["d_ops"][128:256, :], tmpo1, sem="dbo1", wait=(tc2,))
                        K.wait("vector", ("dbo1", 32))
                    K.wait("tensor", tok_wg, tok_wu, tok_xqs, pb03_free)
                    for oc2 in range(2):
                        for kc in range(16):
                            K.t("matmul", pb[oc2][:, 0:256],
                                wstr[jj][:, kc * 256 + oc2 * 128: kc * 256 + oc2 * 128 + 128],
                                xqs[:, kc * 256:(kc + 1) * 256],
                                start=(kc == 0), stop=(kc == 15))
                        for kc in range(16):
                            K.t("matmul", pb[2 + oc2][:, 0:256],
                                wstr[2 + jj][:, kc * 256 + oc2 * 128: kc * 256 + oc2 * 128 + 128],
                                xqs[:, kc * 256:(kc + 1) * 256],
                                start=(kc == 0), stop=(kc == 15))
                    gu_cons[jj] = K.tok("T")
                    tmmgu = gu_cons[jj]
                    for oc2 in range(2):
                        a_ = ob * 2 + oc2
                        isl = a_ % 4
                        usl = a_ % 2
                        t_ga = K.v("tensor_mul", itile[isl][:], pb[oc2][:, 0:256], sgt[:],
                                   wait=[tmmgu, t_gb, it_free[isl]])
                        if "d_g0" in dbg_out and g == 0 and ob == 0 and oc2 == 0:
                            K.dma("dS", dbg_out["d_g0"][0:128, :], itile[isl][:], sem="dbg0", wait=(t_ga,))
                            K.wait("scalar", ("dbg0", 16))
                        t_si = K.s("activation", itile[isl][:], itile[isl][:], AF.Silu, wait=(t_ga,))
                        t_iv = K.v("tensor_mul", itile[isl][:], itile[isl][:], pb[2 + oc2][:, 0:256],
                                   wait=(t_si, tmmgu))
                        sb_ = 4 + (a_ % 2)
                        K.wait("tensor", t_iv, st1_bank_tok[a_ % 2])
                        t_s1 = K.t("matmul", pb[sb_][:, 0:256], h128[:], itile[isl][:],
                                   start=True, stop=True)
                        it_free[isl] = t_s1
                        if "d_int" in dbg_out and g == 0:
                            dtok = K.dma("dS", dbg_out["d_int"][a_ * 128:(a_ + 1) * 128, :],
                                         itile[isl][:], sem=f"dbi{isl}", wait=(t_iv,))
                            it_free[isl] = dtok
                        st1_ev_last = K.v("tensor_copy", Zb[:, a_::64],
                                          pb[sb_][:, 0:256], wait=(t_s1,))
                        st1_bank_tok[a_ % 2] = st1_ev_last
                    pb03_free = K.tok("V")
                tok_z = K.tok("V")
                st2_bank = [None, None]
                for p4 in range(32):
                    slz = p4 % 2
                    K.wait("tensor", tok_z, st2_zti_cons[slz])
                    for j4 in range(4):
                        p_ = p4 * 4 + j4
                        K.t("transpose", pb[0 + slz][:, j4 * 128:(j4 + 1) * 128],
                            Zb[:, p_ * 128:(p_ + 1) * 128], ident[:])
                    ttr = K.tok("T")
                    tze = K.v("tensor_copy", zti[slz][:], pb[0 + slz][:, 0:512],
                              wait=[ttr, st2_zti_cons[slz]])
                    K.wait("tensor", tze, st2_bank[slz])
                    for j4 in range(4):
                        K.t("matmul", pb[2 + slz][:, j4 * 128:(j4 + 1) * 128], bd64[:],
                            zti[slz][:, j4 * 128:(j4 + 1) * 128],
                            start=(j4 == 0), stop=(j4 == 3), skip_group_check=True)
                    tm2 = K.tok("T")
                    st2_zti_cons[slz] = tm2
                    w2_ev = K.v("tensor_copy", W2p[:, p4 * 512:(p4 + 1) * 512], pb[2 + slz][:, 0:512],
                                wait=(tm2,))
                    st2_bank[slz] = w2_ev
                tok_w2 = K.tok("V")
                t_cm = K.v("tensor_reduce", sm2[:, 120:121] if False else scl[:, 0:128],
                           W2p[:].rearrange("p (q b) -> p q b", q=128), axis=AX.X, op=ALU.max,
                           apply_absolute_value=True, wait=(tok_w2,))
                K.wait("tensor", t_cm)
                t_trc = K.t("transpose", pb[6][:, 0:128], scl[:, 0:128], ident[:])
                t_cm2 = K.v("tensor_copy", scl[:, 0:128], pb[6][:, 0:128], wait=(t_cm, t_trc))
                t_M = K.v("tensor_reduce", sm2[:, 120:122],
                          scl[:, 0:128].rearrange("p (t a) -> p t a", t=2), axis=AX.X, op=ALU.max,
                          wait=(t_cm2,))
                t_Mc = K.v("tensor_scalar_max", sm2[:, 120:122], sm2[:, 120:122], EPS, wait=(K.tok("V"),))
                t_s = K.v("reciprocal", sm2[:, 122:124], sm2[:, 120:122], wait=(t_Mc,))
                t_s2 = K.v("tensor_scalar_mul", sm2[:, 122:124], sm2[:, 122:124], 128.0, wait=(K.tok("V"),))
                t_dq = K.v("tensor_scalar", sm2[:, 124:126], sm2[:, 120:122], swcol[:, 3:4], None,
                           ALU.mult, wait=(t_Mc, t_cdc))
                K.wait("tensor", t_s2, t_dq)
                t_ts = K.t("transpose", pb[6][0:2, 0:128], sm2[:, 122:124], ident[:])
                t_sr = K.v("tensor_copy", rowbuf[0:2, 0:128], pb[6][0:2, 0:128], wait=(t_ts,))
                t_t1 = K.t("transpose", pb[6][0:1, 128:256], sm2[:, 124:125], ident[:], wait=(t_sr,))
                t_t2 = K.t("transpose", pb[6][0:1, 256:384], sm2[:, 125:126], ident[:])
                t_f1 = K.v("tensor_copy", rowbuf[0:1, 128:384:2], pb[6][0:1, 128:256], wait=(t_t1, t_t2))
                t_f2 = K.v("tensor_copy", rowbuf[0:1, 129:384:2], pb[6][0:1, 256:384], wait=(K.tok("V"),))
                K.wait("tensor", t_sr, t_f1, t_f2)
                t_dm = K.t("matmul", pb[7][:, 0:128], p2c[:], rowbuf[0:2, 0:128][0:2, :],
                           start=True, stop=True)
                t_scl = K.v("tensor_copy", scl[:, 0:128], pb[7][:, 0:128], wait=(t_dm, t_M))
                t_sd = K.t("matmul", pb[6][:, 0:256], onerow[:], rowbuf[0:1, 128:384],
                           start=True, stop=True)
                t_sdt0 = K.v("tensor_copy", sdt[:], pb[6][:, 0:256], wait=(t_sd,))
                t_sdt = K.v("tensor_mul", sdt[:], sdt[:], sut[:], wait=(t_sdt0, t_ub))
                q1 = K.v("tensor_tensor", W2p[:].rearrange("p (q b) -> p q b", q=128),
                         W2p[:].rearrange("p (q b) -> p q b", q=128),
                         scl[:, 0:128].rearrange("p q -> p q 1" if False else "p (q o) -> p q o", o=1).broadcast_to([128, 128, 128]),
                         ALU.mult, wait=(t_scl, tok_w2))
                q2 = K.v("tensor_scalar_add", W2p[:], W2p[:], MAGIC, wait=(q1,))
                q3 = K.v("tensor_scalar", W2p[:], W2p[:], MAGIC, 127.0,
                         ALU.subtract, ALU.min, wait=(q2,))
                tq_bank = [None, None]
                for p4 in range(32):
                    slz = p4 % 2
                    K.wait("tensor", q3, tq_bank[slz])
                    for j4 in range(4):
                        p_ = p4 * 4 + j4
                        K.t("transpose", pb[2 + slz][:, j4 * 128:(j4 + 1) * 128],
                            W2p[:, p_ * 128:(p_ + 1) * 128], ident[:])
                    ttr = K.tok("T")
                    tq_ev = K.v("tensor_copy", W2qp[:, 16384 + p4 * 512:16384 + (p4 + 1) * 512],
                                pb[2 + slz][:, 0:512], wait=[ttr, tq_bank[slz]])
                    tq_bank[slz] = tq_ev
                tok_w2q = K.tok("V")
                dn_bank = [None, None]
                for ocp in range(8):
                    wldd = [tok_ag, gu_cons[0], gu_cons[1]]
                    if ocp > 0:
                        wldd.append(dn_b_cons)
                    elif dn_cons is not None:
                        wldd.append(dn_cons)
                    for qtr in range(4):
                        K.dma("dS" if qtr % 2 == 0 else "dA", wstr[qtr][:],
                              cowq_blk[3 * ocp + 2, qtr], sem="fd", wait=wldd)
                    tok_wd = K.tok("fd")
                    for oc2 in range(2):
                        oc = ocp * 2 + oc2
                        db_ = oc % 2
                        K.wait("tensor", tok_wd, tok_w2q, dn_bank[db_])
                        for a_ in range(64):
                            K.t("matmul", pb[4 + db_][:, 0:256],
                                wstr[a_ >> 4][:, (a_ & 15) * 256 + oc2 * 128:(a_ & 15) * 256 + oc2 * 128 + 128],
                                W2qp[:, 16384 + a_:16384 + 16384:128][:, 0:256] if False else
                                W2qp[:].rearrange("p (h q t a) -> p h q t a", h=2, q=128, t=2)[:, 1, :, :, a_].rearrange("p q t -> p (q t)"),
                                start=(a_ == 0), stop=(a_ == 63))
                        tdn = K.tok("T")
                        fsl = oc % 2
                        t_fo = K.v("tensor_mul", fout[fsl][:], pb[4 + db_][:, 0:256], sdt[:],
                                   wait=[tdn, t_sdt, fo_free[fsl]])
                        dn_bank[db_] = t_fo
                        fo_free[fsl] = K.dma("dA", ffn_scr[oc][:, g * 256:(g + 1) * 256],
                                             fout[fsl][:], sem=f"ffs{fsl}", wait=(t_fo,))
                    dn_b_cons = K.tok("T")
                dn_cons = K.tok("T")
                fg_done = K.tok("T")
            tok_ffn_scr = None
            for _t in [K.tok("ffs0"), K.tok("ffs1")]:
                pass
            tok_ffn_scr = [K.tok("ffs0"), K.tok("ffs1")]
            sc_p3.close()
            K.barrier(dma_sems=["fxq", "fg0", "fg1", "fu0", "fu1", "fd", "ffs0", "ffs1"])

        if "d_wq" in dbg_out and stage >= 4:
            tmpw = nc.alloc_sbuf_tensor("tmpw", [128, 1024], F32)
            tmpwb = nc.alloc_sbuf_tensor("tmpwb", [128, 1024], BF16)
            cowq2 = cout_wq[:].rearrange("w (k p c) -> w k p c", k=16, p=128)
            for kc in range(16):
                tld = K.dma("dS", tmpwb[:], cowq2[0, kc], sem="dbw",
                            wait=[tok_ag, ("dbw2", K.cnt.get("dbw2", 0)), K.tok("V")])
                tcv = K.v("tensor_copy", tmpw[:], tmpwb[:], wait=(tld,))
                K.dma("dS", dbg_out["d_wq"][kc * 128:(kc + 1) * 128, :], tmpw[:], sem="dbw2", wait=(tcv,))
            K.dma("dS", dbg_out["d_wq"][2048:2049, 0:8], mean_sb[0:1, 0:8], sem="dbw3",
                  wait=(K.tok("V"), t_cdc))
        if "d_ffn" in dbg_out and stage >= 4:
            tmpf3 = nc.alloc_sbuf_tensor("tmpf3", [128, TOK], F32)
            for f in range(16):
                tld = K.dma("dS", tmpf3[:], ffn_scr[f], sem="dbf",
                            wait=tok_ffn_scr + [("dbf2", K.cnt.get("dbf2", 0))])
                K.dma("dS", dbg_out["d_ffn"][f * 128:(f + 1) * 128, :], tmpf3[:], sem="dbf2", wait=(tld,))

        if stage >= 5:
            sc_p4 = ExitStack()
            ybuf = [sc_p4.enter_context(nc.sbuf_tensor(f"yb{k}", [128, TOK], F32)) for k in range(16)]
            sc_p4a = ExitStack()
            oTf = [sc_p4a.enter_context(nc.sbuf_tensor(f"oTf{k}", [128, TOK], F32)) for k in range(16)]
            wo = sc_p4a.enter_context(nc.sbuf_tensor("wo", [128, 8192], F32))
            s4a = [sc_p4a.enter_context(nc.sbuf_tensor(f"s4a{j}", [128, 512], F32)) for j in range(2)]
            fnt = [sc_p4a.enter_context(nc.sbuf_tensor(f"fnt{j}", [128, 512], F32)) for j in range(2)]
            coo = cout_o[:].rearrange("r (c p t) -> r c p t", c=2, p=128)
            for k in range(16):
                K.dma("dS", oTf[k][:], coo[k // 2, k % 2], sem="p4o", wait=(tok_a2a_o,))
            tok_oT = K.tok("p4o")
            wo_tok = None
            wo_cons = None
            bank_tok4 = [None] * 4
            s4_free = [None, None]
            fnt_free = [None, None]
            for og in range(4):
                wldw = [wo_cons] if wo_cons else []
                K.dma("dS", wo[:], owT[og], sem="p4w", wait=wldw)
                wo_tok = K.tok("p4w")
                for tg in range(2):
                    for oc4 in range(4):
                        oc = og * 4 + oc4
                        K.wait("tensor", wo_tok, tok_oT, bank_tok4[oc4])
                        for kc in range(16):
                            K.t("matmul", pb[oc4][:, 0:512],
                                wo[:, kc * 512 + oc4 * 128:kc * 512 + (oc4 + 1) * 128],
                                oTf[kc][:, tg * 512:(tg + 1) * 512],
                                start=(kc == 0), stop=(kc == 15))
                        tmm4 = K.tok("T")
                        ssl = (tg * 4 + oc4) % 2
                        t_b4 = K.s("activation", s4a[ssl][:], pb[oc4][:, 0:512], AF.Identity,
                                   bias=outb_sb[:, oc:oc + 1], scale=1.0,
                                   wait=[tmm4, s4_free[ssl]])
                        bank_tok4[oc4] = t_b4
                        fnl = K.dma("dA", fnt[ssl][:], ffn_scr[oc][:, tg * 512:(tg + 1) * 512],
                                    sem=f"p4f{ssl}", wait=(tok_ffn_scr or []) + [fnt_free[ssl]])
                        t_a4 = K.v("tensor_add", ybuf[oc][:, tg * 512:(tg + 1) * 512],
                                   s4a[ssl][:], fnt[ssl][:], wait=[t_b4, fnl])
                        s4_free[ssl] = t_a4
                        fnt_free[ssl] = t_a4
                wo_cons = K.tok("T")
            tok_yb = K.tok("V")
            sc_p4a.close()
            K.barrier(dma_sems=["p4o", "p4w", "p4f0", "p4f1"])
            xrl = [sc_p4.enter_context(nc.sbuf_tensor(f"xrl{j}", [128, DM], F32)) for j in range(2)]
            yst = [sc_p4.enter_context(nc.sbuf_tensor(f"yst{j}", [128, DM], F32)) for j in range(2)]
            xld = [None] * NTILE
            for i in range(2):
                xld[i] = K.dma("dS", xrl[i][:], x_in[i * 128:(i + 1) * 128, :], sem=f"p4x{i % 2}")
            y_done_v = [None, None]
            for i in range(NTILE):
                bs4 = i % 2
                evs4 = []
                for bg in range(4):
                    bk = pb[4 + bg]
                    w = [tok_yb]
                    if i >= 1:
                        w.append(prev_evs4_last)
                    for sl in range(4):
                        f = bg * 4 + sl
                        K.t("transpose", bk[:, sl * 128:(sl + 1) * 128],
                            ybuf[f][:, i * 128:(i + 1) * 128], ident[:],
                            wait=w if sl == 0 else ())
                    tbk = K.tok("T")
                    evs4.append(K.v("tensor_add", yst[bs4][:, bg * 512:(bg + 1) * 512],
                                    bk[:, 0:512], xrl[bs4][:, bg * 512:(bg + 1) * 512],
                                    wait=[tbk, xld[i], y_done_v[bs4]]))
                prev_evs4_last = evs4[-1]
                tok_y_i = K.tok("V")
                ydma = K.dma("dA", y_out[i * 128:(i + 1) * 128, :], yst[bs4][:], sem=f"p4y{bs4}",
                             wait=(tok_y_i,))
                y_done_v[bs4] = ydma
                if i + 2 < NTILE:
                    xld[i + 2] = K.dma("dS", xrl[bs4][:], x_in[(i + 2) * 128:(i + 3) * 128, :],
                                       sem=f"p4x{bs4}", wait=(tok_y_i,))


        if stage < 5:
            zf = nc.alloc_sbuf_tensor("zf", [128, DM], F32)
            tz = K.v("memset", zf[:], 0.0, wait=(K.tok("V"), K.tok("SC"), K.tok("T")))
            for i in range(NTILE):
                K.dma("dS", y_out[i * 128:(i + 1) * 128, :], zf[:], sem="yw", wait=(tz,))
        for sname, cval in list(K.cnt.items()):
            if sname not in ("V", "SC", "T", "CC") and cval > 0:
                K.wait("sync", (sname, cval))
    return nc, dbg_out


def make_consts():
    ident = np.eye(128, dtype=np.float32)
    idx = np.arange(128)
    pc = np.array([bin(i & j).count("1") for i in idx for j in idx]).reshape(128, 128)
    h128 = ((-1.0) ** pc).astype(np.float32)
    i64 = np.arange(64)
    pc64 = np.array([bin(a & b).count("1") for a in i64 for b in i64]).reshape(64, 64)
    h64 = ((-1.0) ** pc64).astype(np.float32)
    bd64 = np.zeros((128, 128), np.float32)
    bd64[:64, :64] = h64
    bd64[64:, 64:] = h64
    p2 = np.zeros((2, 128), np.float32)
    p2[0, :64] = 1.0
    p2[1, 64:] = 1.0
    cmat = np.concatenate([ident, h128, bd64, p2], axis=0)
    masks = np.zeros((4, 128, 512), np.float32)
    for m_ in range(4):
        for p in range(128):
            masks[m_, p, m_ * 128 + p + 1:] = NEG
    return cmat, masks.reshape(512, 512)


def prep_inputs(inputs):
    x = np.ascontiguousarray(np.asarray(inputs["L0_attn_hidden_states"], np.float32).reshape(NT, DM))
    _wq = np.asarray(inputs["L0_attn_Wqkv_w"], np.float32).T
    wqkvT = np.ascontiguousarray(_wq.reshape(16, 128, 8, 512).transpose(2, 1, 0, 3)).reshape(8, 128, 8192)
    qkvb = np.ascontiguousarray(np.asarray(inputs["L0_attn_Wqkv_b"], np.float32).reshape(32, 128).T)
    _ow = np.asarray(inputs["L0_attn_out_w"], np.float32).T
    owT = np.ascontiguousarray(_ow.reshape(16, 128, 4, 512).transpose(2, 1, 0, 3)).reshape(4, 128, 8192)
    outb = np.ascontiguousarray(np.asarray(inputs["L0_attn_out_b"], np.float32).reshape(16, 128).T)
    lamv = np.concatenate([np.asarray(inputs["L0_attn_lambda_q1"], np.float32),
                           np.asarray(inputs["L0_attn_lambda_k1"], np.float32),
                           np.asarray(inputs["L0_attn_lambda_q2"], np.float32),
                           np.asarray(inputs["L0_attn_lambda_k2"], np.float32)]).reshape(1, 512)
    sublnw = np.asarray(inputs["L0_attn_subln_w"], np.float32).reshape(1, 256)
    wgT = np.asarray(inputs["L0_ffn_w_gate"], np.float32).T
    wuT = np.asarray(inputs["L0_ffn_w_up"], np.float32).T
    wdT = np.asarray(inputs["L0_ffn_w_down"], np.float32).T
    cmat, masks = make_consts()
    in_maps = []
    for cid in range(8):
        in_maps.append({
            "x": np.ascontiguousarray(x[cid * TOK:(cid + 1) * TOK]),
            "wqkvT": wqkvT,
            "qkv_b": qkvb,
            "owT": owT,
            "out_b": outb,
            "lamv": lamv,
            "sublnw": sublnw,
            "wgs": np.ascontiguousarray(wgT[:, cid * 1024:(cid + 1) * 1024]),
            "wus": np.ascontiguousarray(wuT[:, cid * 1024:(cid + 1) * 1024]),
            "wds": np.ascontiguousarray(wdT[:, cid * 256:(cid + 1) * 256]),
            "cmat": cmat,
            "masks": masks,
        })
    return in_maps


_NC_CACHE = {}


def _get_nc():
    if "nc" not in _NC_CACHE:
        _NC_CACHE["nc"] = build(stage=5, dbg=())[0]
    return _NC_CACHE["nc"]


def kernel(**inputs):
    in_maps = prep_inputs(inputs)
    nc = _get_nc()
    res = run_bass_kernel_spmd(nc, in_maps, core_ids=list(range(8)))
    y = np.concatenate([np.asarray(r["y"]) for r in res.results], axis=0)
    return np.ascontiguousarray(y.reshape(B, S, DM).astype(np.float32, copy=False))
